# version 13
# baseline (speedup 1.0000x reference)
import contextlib
import numpy as np
import ml_dtypes

import concourse.bass as bass
import concourse.mybir as mybir
from concourse.bass_utils import run_bass_kernel_spmd

F32 = mybir.dt.float32
BF16 = mybir.dt.bfloat16
AF = mybir.ActivationFunctionType
ALU = mybir.AluOpType
AX = mybir.AxisListType

ENGS = ("sync", "vector", "scalar", "tensor", "gpsimd")

B_, S_, D_ = 4, 4096, 1024
NTOK, NKEY = 2048, 4096
NH, HD = 16, 64
DQ = DKV = 512
DFF = 4096
SCALE = float(np.sqrt(np.float32(NH + DKV + DQ)))
INV_SCALE = float(np.float32(1.0) / np.float32(SCALE))
EPS = 1e-5
CMAGIC = 12582912.0
QMAX_A16 = 32767.0
NMACRO, QMAC = 4, 512
NKC = NKEY // 128
NPAIR = NH // 2
NTT = NTOK // 128
NKT = NKEY // 512
NQT = NTOK // 512


class KB:

    def __init__(self, nc):
        self.nc = nc
        self.ops = {e: [] for e in ENGS}
        self.sems = {}
        self.cnt = {}
        self.waited = {e: {} for e in ENGS}
        self.dma_sems = set()
        self.stack = contextlib.ExitStack()

    def sem(self, name):
        if name not in self.sems:
            self.sems[name] = self.stack.enter_context(self.nc.semaphore(name))
            self.cnt[name] = 0
        return self.sems[name]

    def op(self, eng, fn, sem=None, n=1):
        if sem is None:
            self.ops[eng].append(fn)
            return None
        if n == 16:
            self.dma_sems.add(sem)
        h = self.sem(sem)
        self.cnt[sem] += n
        val = self.cnt[sem]

        def wrapped(e, fn=fn, h=h, n=n):
            fn(e).then_inc(h, n)
        self.ops[eng].append(wrapped)
        return (sem, val)

    def wait(self, eng, tok):
        if tok is None:
            return
        name, val = tok
        if self.waited[eng].get(name, 0) >= val:
            return
        self.waited[eng][name] = val
        h = self.sem(name)
        self.ops[eng].append(lambda e, h=h, val=val: e.wait_ge(h, val))

    def waits(self, eng, toks):
        best = {}
        for t in toks:
            if t is None:
                continue
            name, val = t
            if best.get(name, -1) < val:
                best[name] = val
        for name, val in best.items():
            self.wait(eng, (name, val))

    def join(self):
        names = [s for s in self.cnt if not s.startswith("jn_")]
        for e in ENGS:
            for sem in sorted(names):
                self.wait(e, (sem, self.cnt[sem]))
        toks = [self.op(e, lambda en: en.nop(), sem="jn_" + e) for e in ENGS]
        for e in ENGS:
            for t in toks:
                self.wait(e, t)

    def runner(self, e):
        ops = self.ops[e]

        def run(eng):
            for f in ops:
                f(eng)
        return run


def build_core_program(dbg=False):
    nc = bass.Bass("TRN2", target_bir_lowering=False, debug=False)
    kb = KB(nc)
    o, w, ws = kb.op, kb.wait, kb.waits

    def din(name, shape, dt=F32):
        return nc.dram_tensor(name, shape, dt, kind="ExternalInput").ap()

    def dint(name, shape, dt=F32):
        return nc.dram_tensor(name, shape, dt).ap()

    x_in = din("x", [NKEY, D_])
    wd_q = din("wd_q", [D_, DQ]);   wd_kv = din("wd_kv", [D_, DKV])
    wu_q = din("wu_q", [DQ, D_]);   wq_r = din("wq_r", [DQ, D_])
    wk_r = din("wk_r", [D_, D_]);   wu_k = din("wu_k", [DKV, D_])
    wu_v = din("wu_v", [DKV, D_]);  wo_w = din("wo", [D_, D_])
    w_in = din("w_in", [D_, DFF]);  w_out = din("w_out", [DFF, D_])
    bd_q = din("bd_q", [128, 4]);   bd_kv = din("bd_kv", [128, 4])
    bu_q = din("bu_q", [128, 8]);   bq_r = din("bq_r", [128, 8])
    bk_r = din("bk_r", [128, 8]);   bu_k = din("bu_k", [128, 8])
    b_in = din("b_in", [128, 32])
    bu_v_row = din("bu_v_row", [1, D_])
    bo_row = din("bo_row", [1, D_])
    bout_row = din("bout_row", [1, D_])
    cos_d = din("cos_d", [D_, NKEY], BF16)
    sin_d = din("sin_d", [D_, NKEY], BF16)
    pswap_in = din("pswap", [128, 128], BF16)
    idn_in = din("idn", [128, 128], BF16)
    idn32_in = din("idn32", [128, 128])
    y_out = nc.dram_tensor("y", [NTOK, D_], F32, kind="ExternalOutput").ap()

    ln1x_d = dint("ln1x_d", [NKEY, D_], BF16)
    x1_d = dint("x1_d", [NTOK, D_])
    xsim_d = dint("xsim_d", [NTOK, D_], BF16)
    hid_d = dint("hid_d", [32, 128, NTOK], BF16)
    wint_d = dint("wint_d", [32, 128, 8, 128], BF16)
    wout_d = dint("wout_d", [32, 128, D_], BF16)
    q_d = dint("q_d", [NPAIR, 128, NTOK], BF16)
    wdq_d = dint("wdq_d", [8, 128, DQ], BF16)
    wdkv_d = dint("wdkv_d", [8, 128, DKV], BF16)
    wuq_d = dint("wuq_d", [4, 128, D_], BF16)
    wqr_d = dint("wqr_d", [4, 128, D_], BF16)
    wkr_d = dint("wkr_d", [8, 128, D_], BF16)
    wuk_d = dint("wuk_d", [4, 128, D_], BF16)
    wuv_d = dint("wuv_d", [4, 128, D_], BF16)
    wo_d = dint("wo_d", [8, 128, D_], BF16)
    flip_d = dint("flip_d", [1, 4096])
    cc1_in_h = nc.dram_tensor("cc1_in", [1, 128], F32)
    cc1_out_h = nc.dram_tensor("cc1_out", [1, 128], F32)
    cc2_in_h = nc.dram_tensor("cc2_in", [1, 128], F32)
    cc2_out_h = nc.dram_tensor("cc2_out", [1, 128], F32)

    def mksb(stk):
        def f(name, shape, dt=F32):
            return stk.enter_context(nc.sbuf_tensor(name, shape, dt))
        return f

    sbP = mksb(kb.stack)

    def pst(name, shape):
        return kb.stack.enter_context(nc.psum_tensor(name, shape, F32))

    psA = pst("psA", [128, 1024])
    psB = pst("psB", [128, 1024])
    ps1 = pst("ps1", [128, 512])
    ps2 = pst("ps2", [128, 512])
    ps3 = pst("ps3", [128, 512])
    ps4 = pst("ps4", [128, 512])

    ones_row = sbP("ones_row", [1, 128])
    ones2 = sbP("ones2", [128, 2])
    ones2b = sbP("ones2b", [128, 2], BF16)
    ones21 = sbP("ones21", [2, 1])
    pswap_s = sbP("pswap_s", [128, 128], BF16)
    idn_s = sbP("idn_s", [128, 128], BF16)
    idn32_s = sbP("idn32_s", [128, 128])
    bias_c = sbP("bias_c", [128, 32])
    qn2 = sbP("qn2", [2, 32])
    kn2 = sbP("kn2", [2, NPAIR])
    kn4 = sbP("kn4", [2, 32])
    knall = sbP("knall", [2, 64])
    cc_sc = sbP("cc_sc", [2, 32])
    negc = sbP("negc", [1, 32])
    rstd3a = sbP("rstd3a", [128, NTT])
    nmr3a = sbP("nmr3a", [128, NTT])
    amax_r = sbP("amax_r", [128, 1])
    hmax_r = sbP("hmax_r", [128, 1])
    hmt_s = sbP("hmt_s", [128, 1])
    sjr_p = sbP("sjr_p", [128, 32])
    grow = sbP("grow", [1, 128])
    ln_sc = sbP("ln_sc", [128, 2, D_])
    s_sum = sbP("s_sum", [128, 2]); s_sq = sbP("s_sq", [128, 2])
    s_mean = sbP("s_mean", [128, 2]); s_var = sbP("s_var", [128, 2])
    s_rstd = sbP("s_rstd", [128, 2]); s_nmr = sbP("s_nmr", [128, 2])
    s_t1 = sbP("s_t1", [128, 2])
    sc_a = sbP("sc_a", [1, 1]); sc_b = sbP("sc_b", [1, 1]); sc_c = sbP("sc_c", [1, 1])
    sxb = sbP("sxb", [128, 1]); invsxb = sbP("invsxb", [128, 1])
    shb = sbP("shb", [128, 1]); invshb = sbP("invshb", [128, 1])
    q2sc = sbP("q2sc", [128, 4]); q2bi = sbP("q2bi", [128, 4])
    eps_b = sbP("eps_b", [128, 1]); cmag_b = sbP("cmag_b", [128, 1])
    tiny_b = sbP("tiny_b", [128, 1])
    bdq_s = sbP("bdq_s", [128, 4]);   bdkv_s = sbP("bdkv_s", [128, 4])
    buq_s = sbP("buq_s", [128, 8]);   bqr_s = sbP("bqr_s", [128, 8])
    bkr_s = sbP("bkr_s", [128, 8]);   buk_s = sbP("buk_s", [128, 8])
    bin_s = sbP("bin_s", [128, 32])

    psum_last = {}

    t_smld = None
    for dst, src in [(bdq_s, bd_q), (bdkv_s, bd_kv), (buq_s, bu_q),
                     (bqr_s, bq_r), (bkr_s, bk_r), (buk_s, bu_k),
                     (bin_s, b_in), (pswap_s, pswap_in)]:
        t_smld = o("sync", lambda e, dst=dst, src=src: e.dma_start(
            out=dst[:, :], in_=src), "smld", 16)

    t_ones = o("vector", lambda e: e.memset(ones_row[:, :], 1.0), "sv")
    t_o2a = o("vector", lambda e: e.memset(ones2[:, :], 0.0), "sv")
    w("vector", t_o2a)
    o("vector", lambda e: e.memset(ones2[0:64, 0:1], 1.0), "sv")
    o("vector", lambda e: e.memset(ones2[64:128, 1:2], 1.0), "sv")
    t_o2b = o("vector", lambda e: e.memset(ones2b[:, :], 0.0), "sv")
    w("vector", t_o2b)
    o("vector", lambda e: e.memset(ones2b[0:64, 0:1], 1.0), "sv")
    t_ones2 = o("vector", lambda e: e.memset(ones2b[64:128, 1:2], 1.0), "sv")
    t_cs1 = o("vector", lambda e: e.memset(eps_b[:, :], EPS), "sv")
    t_cs2 = o("vector", lambda e: e.memset(cmag_b[:, :], CMAGIC), "sv")
    t_cs3 = o("vector", lambda e: e.memset(tiny_b[:, :], 1e-30), "sv")
    t_consts = [t_cs1, t_cs2, t_cs3]
    t_amz = o("vector", lambda e: e.memset(amax_r[:, :], 0.0), "sv")
    t_hmz = o("vector", lambda e: e.memset(hmax_r[:, :], 0.0), "sv")
    o("sync", lambda e: e.dma_start(out=idn_s[:, :], in_=idn_in), "smld", 16)
    t_idn = o("sync", lambda e: e.dma_start(out=idn32_s[:, :], in_=idn32_in), "smld", 16)
    t_small_loads = t_idn
    t_o21 = o("vector", lambda e: e.memset(ones21[:, :], 1.0), "sv")

    _ln_state = [{"t_n": None, "t_r2": None, "readers": []},
                 {"t_n": None, "t_r2": None, "readers": []}]
    _ln_ctr = [0]

    def ln_stats_from(src_ap, readers):
        g = _ln_ctr[0] % 2
        _ln_ctr[0] += 1
        stt = _ln_state[g]
        sl = lambda t: t[:, g:g + 1]
        lsc = ln_sc[:, g, :]
        prevs = [t for t in (stt["readers"] + list(readers)
                             + [stt["t_n"], stt["t_r2"]]) if t]
        ws("vector", prevs)
        t_s1 = o("vector", lambda e: e.tensor_scalar(
            lsc, src_ap, 0.0, 0.0, ALU.add, ALU.add, accum_out=sl(s_sum)), "sv")
        w("vector", t_s1)
        t_s2 = o("vector", lambda e: e.scalar_tensor_tensor(
            lsc, src_ap, 1.0, src_ap, ALU.bypass, ALU.mult,
            accum_out=sl(s_sq)), "sv")
        w("vector", t_s2)
        t_m = o("vector", lambda e: e.tensor_scalar(
            sl(s_mean), sl(s_sum), 1.0 / D_, None, ALU.mult), "sv")
        w("vector", t_m)
        t_v1 = o("vector", lambda e: e.scalar_tensor_tensor(
            sl(s_var), sl(s_mean), -1.0, sl(s_mean), ALU.mult, ALU.mult), "sv")
        w("vector", t_v1)
        t_v2 = o("vector", lambda e: e.tensor_scalar(
            sl(s_t1), sl(s_sq), 1.0 / D_, sl(s_var), ALU.mult, ALU.add), "sv")
        ws("scalar", prevs)
        w("scalar", t_v2)
        ws("scalar", t_consts)
        t_r1 = o("scalar", lambda e: e.activation(
            sl(s_var), sl(s_t1), AF.Ln, bias=eps_b[:, :], scale=1.0), "ss")
        w("scalar", t_r1)
        t_r2 = o("scalar", lambda e: e.activation(
            sl(s_rstd), sl(s_var), AF.Exp, bias=0.0, scale=-0.5), "ss")
        w("vector", t_r2)
        t_n = o("vector", lambda e: e.scalar_tensor_tensor(
            sl(s_nmr), sl(s_mean), -1.0, sl(s_rstd), ALU.mult, ALU.mult), "sv")
        stt["t_n"] = t_n
        stt["t_r2"] = t_r2
        stt["readers"] = []
        return (t_n, sl(s_rstd), sl(s_nmr), g)

    def ln_mark_readers(g, toks):
        _ln_state[g]["readers"] = [t for t in toks if t]

    stA = contextlib.ExitStack()
    sbA = mksb(stA)
    xa_t = sbA("xa_t", [128, 2, D_])
    ln_bf = sbA("ln_bf", [128, 2, D_], BF16)
    wst_f = sbA("wst_f", [128, 2, D_])
    wst_b = sbA("wst_b", [128, 2, D_], BF16)

    wq_h = sbA("wq_h", [128, 8, DFF // 2])
    M_h = sbA("M_h", [128, DFF // 2])
    A_t = sbA("A_t", [128, 32])
    at_sb = sbA("at_sb", [16, 128])
    srow_h = sbA("srow_h", [1, DFF // 2])
    Sbc_h = sbA("Sbc_h", [128, DFF // 2])
    wint_h = sbA("wint_h", [128, DFF // 2], BF16)
    wqo_f = sbA("wqo_f", [128, 2, D_])
    so_t = sbA("so_t", [128, 32])
    sor_t = sbA("sor_t", [128, 32])
    nCr_t = sbA("nCr_t", [128, 32])
    wob16 = sbA("wob16", [128, D_], BF16)
    w127 = sbA("w127", [128, 32])

    R = {}

    def gen_ln1():
        xa_reuse = [None, None]
        lnbf_reuse = [None, None]
        t_aff_prev = None
        t_st_ln = None
        for i in range(NKEY // 128):
            bi = i % 2
            w("sync", xa_reuse[bi])
            t_x = o("sync", lambda e, i=i, bi=bi: e.dma_start(
                out=xa_t[:, bi, :], in_=x_in[i * 128:(i + 1) * 128, :]), "xal%d" % bi, 16)
            w("vector", t_x)
            t_n, rstd_ap, nmr_ap, g = ln_stats_from(xa_t[:, bi, :], [t_aff_prev])
            ws("scalar", [t_n, lnbf_reuse[bi]])
            t_ap = o("scalar", lambda e, bi=bi, rstd_ap=rstd_ap, nmr_ap=nmr_ap: e.activation(
                ln_bf[:, bi, :], xa_t[:, bi, :], AF.Identity,
                bias=nmr_ap, scale=rstd_ap), "ss")
            ln_mark_readers(g, [t_ap])
            xa_reuse[bi] = t_ap
            t_aff_prev = t_ap
            w("gpsimd", t_ap)
            t_st_ln = o("gpsimd", lambda e, i=i, bi=bi: e.dma_start(
                out=ln1x_d[i * 128:(i + 1) * 128, :], in_=ln_bf[:, bi, :]), "lns%d" % bi, 16)
            lnbf_reuse[bi] = t_st_ln
            yield
        R["t_ln1_both"] = [lnbf_reuse[0], lnbf_reuse[1]]

    def gen_precast():
        st_tok = [None, None]
        cp_tok = [None, None]
        ci = 0
        tok = None
        for srcw, dst_d, nrc, ncols in [
                (wd_q, wdq_d, 8, 512), (wd_kv, wdkv_d, 8, 512),
                (wu_q, wuq_d, 4, 1024), (wq_r, wqr_d, 4, 1024),
                (wk_r, wkr_d, 8, 1024), (wu_k, wuk_d, 4, 1024),
                (wu_v, wuv_d, 4, 1024), (wo_w, wo_d, 8, 1024)]:
            for rc in range(nrc):
                bi = ci % 2
                ci += 1
                w("sync", st_tok[bi])
                t_l = o("sync", lambda e, rc=rc, srcw=srcw, ncols=ncols, bi=bi: e.dma_start(
                    out=wst_f[:, bi, 0:ncols], in_=srcw[rc * 128:(rc + 1) * 128, :]),
                    "pcl%d" % bi, 16)
                ws("scalar", [t_l, st_tok[bi]])
                t_c = o("scalar", lambda e, ncols=ncols, bi=bi: e.activation(
                    wst_b[:, bi, 0:ncols], wst_f[:, bi, 0:ncols], AF.Identity,
                    bias=0.0, scale=1.0), "ss")
                cp_tok[bi] = t_c
                w("gpsimd", t_c)
                tok = o("gpsimd", lambda e, rc=rc, dst_d=dst_d, ncols=ncols, bi=bi: e.dma_start(
                    out=dst_d[rc, :, :], in_=wst_b[:, bi, 0:ncols]), "pcs%d" % bi, 16)
                st_tok[bi] = tok
                yield
        R["t_precast"] = tok

    def gen_wout():
        wqo_reuse = [None, None]
        t_sc_pr = None
        t_p2_prev = None
        for jc in range(32):
            bi = jc % 2
            w("sync", wqo_reuse[bi])
            t_l = o("sync", lambda e, jc=jc, bi=bi: e.dma_start(
                out=wqo_f[:, bi, :], in_=w_out[jc * 128:(jc + 1) * 128, :]), "wqo%d" % bi, 16)
            ws("vector", [t_l, t_sc_pr])
            t_am = o("vector", lambda e, jc=jc, bi=bi: e.tensor_reduce(
                so_t[:, jc:jc + 1], wqo_f[:, bi, :], AX.X, ALU.max,
                apply_absolute_value=True), "sv")
            w("vector", t_am)
            t_oc = o("vector", lambda e, jc=jc: e.tensor_scalar(
                so_t[:, jc:jc + 1], so_t[:, jc:jc + 1], 1e-8, None, ALU.max), "sv")
            w("vector", t_oc)
            t_or = o("vector", lambda e, jc=jc: e.tensor_scalar(
                sor_t[:, jc:jc + 1], so_t[:, jc:jc + 1], 1.0 / 127.0, None, ALU.mult), "sv")
            t_o127 = o("vector", lambda e, jc=jc: e.reciprocal(
                w127[:, jc:jc + 1], so_t[:, jc:jc + 1]), "sv")
            ws("vector", [t_or, t_o127])
            t_ncr = o("vector", lambda e, jc=jc: e.tensor_scalar(
                nCr_t[:, jc:jc + 1], sor_t[:, jc:jc + 1], -CMAGIC, None, ALU.mult), "sv")
            t_osd = o("vector", lambda e, jc=jc: e.tensor_scalar(
                so_t[:, jc:jc + 1], w127[:, jc:jc + 1], 127.0, None, ALU.mult), "sv")
            t_sc_pr = t_osd
            ws("scalar", [t_osd, t_ncr, t_consts[1]])
            t_p1 = o("scalar", lambda e, jc=jc, bi=bi: e.activation(
                wqo_f[:, bi, :], wqo_f[:, bi, :], AF.Identity,
                bias=cmag_b[:, :], scale=so_t[:, jc:jc + 1]), "ss")
            ws("scalar", [t_p1, t_p2_prev])
            t_p2 = o("scalar", lambda e, jc=jc, bi=bi: e.activation(
                wob16[:, :], wqo_f[:, bi, :], AF.Identity,
                bias=nCr_t[:, jc:jc + 1], scale=sor_t[:, jc:jc + 1]), "ss")
            wqo_reuse[bi] = t_p2
            w("gpsimd", t_p2)
            t_p2_prev = o("gpsimd", lambda e, jc=jc: e.dma_start(
                out=wout_d[jc, :, :], in_=wob16[:, :]), "wts", 16)
            yield

    def gen_win():
        wq_reuse = [None] * 8
        wi_prev = [None, None]
        t_sbc_read = [None]
        t_fold_done = [None]
        t_ac = None
        for half in range(2):
            c0 = half * 2048
            mprev = None
            for fc in range(8):
                w("sync", wq_reuse[fc])
                t_l = o("sync", lambda e, fc=fc, c0=c0: e.dma_start(
                    out=wq_h[:, fc, :], in_=w_in[fc * 128:(fc + 1) * 128, c0:c0 + 2048]),
                    "wql%d" % fc, 16)
                if fc == 0:
                    ws("scalar", [t_l, mprev, t_fold_done[0]])
                    mprev = o("scalar", lambda e, fc=fc: e.activation(
                        M_h[:, :], wq_h[:, fc, :], AF.Abs, bias=0.0, scale=1.0), "ss")
                else:
                    ws("scalar", [t_l, t_sbc_read[0]])
                    t_abs = o("scalar", lambda e, fc=fc: e.activation(
                        Sbc_h[:, :], wq_h[:, fc, :], AF.Abs, bias=0.0, scale=1.0), "ss")
                    ws("vector", [t_abs, mprev])
                    mprev = o("vector", lambda e: e.tensor_tensor(
                        M_h[:, :], M_h[:, :], Sbc_h[:, :], ALU.max), "sv")
                    t_sbc_read[0] = mprev
                yield
            slots = [ps3[:, 0:128], ps3[:, 128:256], ps4[:, 0:128], ps4[:, 128:256]]
            slot_tok = [psum_last.get("ps3"), psum_last.get("ps3"),
                        psum_last.get("ps4"), psum_last.get("ps4")]
            t_red = None
            for c in range(16):
                sl_ = c % 4
                ws("tensor", [mprev, t_idn, slot_tok[sl_]])
                t_tr = o("tensor", lambda e, c=c, sl_=sl_: e.transpose(
                    slots[sl_], M_h[:, c * 128:(c + 1) * 128], idn32_s[:, :]), "st")
                w("vector", t_tr)
                t_red = o("vector", lambda e, c=c, half=half, sl_=sl_: e.tensor_reduce(
                    A_t[:, half * 16 + c: half * 16 + c + 1], slots[sl_], AX.X, ALU.max), "sv")
                slot_tok[sl_] = t_red
                if c % 4 == 3:
                    yield
            t_fold_done[0] = t_red
            psum_last["ps3"] = t_red
            psum_last["ps4"] = t_red
            w("vector", t_red)
            t_ac = o("vector", lambda e, half=half: e.tensor_scalar(
                A_t[:, half * 16:(half + 1) * 16], A_t[:, half * 16:(half + 1) * 16],
                1e-8, None, ALU.max), "sv")
            ws("tensor", [t_ac, psum_last.get("ps3")])
            t_t2 = o("tensor", lambda e, half=half: e.transpose(
                ps3[0:16, 0:128], A_t[:, half * 16:(half + 1) * 16], idn32_s[:, :]), "st")
            w("vector", t_t2)
            t_cpt = o("vector", lambda e: e.tensor_copy(at_sb[:, :], ps3[0:16, 0:128]), "sv")
            psum_last["ps3"] = t_cpt
            w("gpsimd", t_cpt)
            t_fl1 = o("gpsimd", lambda e: e.dma_start(
                out=flip_d[0:1, 0:2048].rearrange("o (a b) -> (o a) b", b=128),
                in_=at_sb[:, :]), "flst", 16)
            w("sync", t_fl1)
            t_fl2 = o("sync", lambda e: e.dma_start(
                out=srow_h[:, :], in_=flip_d[0:1, 0:2048]), "flrd", 16)
            w("vector", t_fl2)
            t_sr1 = o("vector", lambda e: e.reciprocal(srow_h[:, :], srow_h[:, :]), "sv")
            w("vector", t_sr1)
            t_sr2 = o("vector", lambda e: e.tensor_scalar(
                srow_h[:, :], srow_h[:, :], 127.0, None, ALU.mult), "sv")
            yield
            t_sbc = None
            for q in range(4):
                ws("tensor", [t_sr2, t_ones, psum_last.get("ps4")])
                t_mmq = o("tensor", lambda e, q=q: e.matmul(
                    ps4[:, 0:512], ones_row[:, :], srow_h[:, q * 512:(q + 1) * 512],
                    start=True, stop=True), "st")
                w("vector", t_mmq)
                t_sbc = o("vector", lambda e, q=q: e.tensor_copy(
                    Sbc_h[:, q * 512:(q + 1) * 512], ps4[:, 0:512]), "sv")
                psum_last["ps4"] = t_sbc
            yield
            t_wi_last = None
            for fc in range(8):
                ws("vector", [t_sbc, wi_prev[fc % 2]])
                t_m1 = o("vector", lambda e, fc=fc: e.tensor_tensor(
                    wq_h[:, fc, :], wq_h[:, fc, :], Sbc_h[:, :], ALU.mult), "sv")
                t_sbc_read[0] = t_m1
                ws("vector", [t_m1, wi_prev[1 - fc % 2]])
                t_m2 = o("vector", lambda e, fc=fc: e.tensor_scalar(
                    wint_h[:, :], wq_h[:, fc, :], CMAGIC, CMAGIC, ALU.add, ALU.subtract), "sv")
                wq_reuse[fc] = t_m2
                w("gpsimd", t_m2)
                t_wi_last = o("gpsimd", lambda e, fc=fc, half=half: e.dma_start(
                    out=wint_d[half * 16:(half + 1) * 16, :, fc, :].rearrange("j p c -> p j c"),
                    in_=wint_h[:, :].rearrange("p (j c) -> p j c", c=128)), "wis", 16)
                wi_prev[fc % 2] = t_wi_last
                yield
        w("vector", t_ac)
        t_sjr = o("vector", lambda e: e.tensor_scalar(
            sjr_p[:, :], A_t[:, :], 1.0 / 127.0, None, ALU.mult), "sv")
        R["t_sjr_done"] = t_sjr

    def drive(gens):
        gens = list(gens)
        while gens:
            nxt = []
            for gobj in gens:
                try:
                    next(gobj)
                    nxt.append(gobj)
                except StopIteration:
                    pass
            gens = nxt

    drive([gen_ln1(), gen_precast(), gen_wout(), gen_win()])
    t_ln1_both = R["t_ln1_both"]
    t_precast = R["t_precast"]
    t_sjr_done = R["t_sjr_done"]

    kb.join()
    stA.close()

    stKV = contextlib.ExitStack()
    sbKV = mksb(stKV)
    K_fm = sbKV("K_fm", [128, NPAIR, NKEY], BF16)
    V_tm = sbKV("V_tm", [128, NKC, NH * 65], BF16)
    t_vones = o("vector", lambda e: e.memset(
        V_tm[:, :, :].rearrange("p k (h v) -> p k h v", v=65)[:, :, :, 64:65], 1.0), "sv")

    stB = contextlib.ExitStack()
    sbB = mksb(stB)
    lx_t = sbB("lx_t", [128, 2, 8, 512], BF16)
    cs_t = sbB("cs_t", [128, 2, 2, 512], BF16)
    rp_t0 = sbB("rp_t0", [128, 2, 512], BF16)
    rp_t1 = sbB("rp_t1", [128, 2, 512], BF16)
    rp_t2 = sbB("rp_t2", [128, 2, 512], BF16)
    kc_bf = sbB("kc_bf", [128, 2, 512], BF16)
    sq_t = sbB("sq_t", [128, 2, 512], BF16)
    stB1 = contextlib.ExitStack()
    sbB1 = mksb(stB1)
    ckv_t = sbB1("ckv_t", [128, 4, 512], BF16)
    wdkv_bf = sbB1("wdkv_bf", [128, 8, 512], BF16)
    wuv_bf = sbB1("wuv_bf", [128, 4, 1024], BF16)
    wuk_bf = sbB1("wuk_bf", [128, 4, 1024], BF16)
    wkrp = sbB1("wkrp", [128, 2, 8, 128], BF16)
    vB_bc = sbB1("vB_bc", [128, D_], BF16)
    row_b = sbB1("row_b", [1, 512])

    vb_toks = []
    t_rb = None
    for h in range(2):
        w("sync", t_rb)
        t_rb = o("sync", lambda e, h=h: e.dma_start(
            out=row_b[:, :], in_=bu_v_row[0:1, h * 512:(h + 1) * 512]), "rowb", 16)
        ws("tensor", [t_rb, t_ones, psum_last.get("ps4")])
        vb_toks.append(o("tensor", lambda e: e.matmul(
            ps4[:, 0:512], ones_row[:, :], row_b[:, :],
            start=True, stop=True), "st"))
        w("vector", vb_toks[-1])
        t_vB = o("vector", lambda e, h=h: e.tensor_copy(
            vB_bc[:, h * 512:(h + 1) * 512], ps4[:, 0:512]), "sv")
        psum_last["ps4"] = t_vB
        t_rb_tok = vb_toks[-1]
        t_rb = t_rb if False else vb_toks[-1]

    def loadw(dst, src_d, nrc, ncols, prev, sem="wlb1"):
        tok = prev
        for rc in range(nrc):
            tok = o("sync", lambda e, rc=rc: e.dma_start(
                out=dst[:, rc, :], in_=src_d[rc, :, 0:ncols]), sem, 16)
        return tok

    ws("sync", [t_precast])
    t_wb = loadw(wdkv_bf, wdkv_d, 8, 512, None)
    t_wb = loadw(wuv_bf, wuv_d, 4, 1024, t_wb)
    t_wb = loadw(wuk_bf, wuk_d, 4, 1024, t_wb)
    t_wB1 = t_wb

    lx_reuse = None
    cs_reuse = [None, None]
    wkr_reuse = [None, None]
    rope_gate = [None, None]
    ckv_reuse = None
    t_last_rope = None
    t_last_vev = None
    lx_read = [None, None]
    u4_tok = {}
    kn_i = [0]
    kn_sqd = []
    kn_mm_prev = [None]
    kn_rd_toks = []

    def pump_kn():
        if kn_sqd:
            p0, s80, t_sq0 = kn_sqd.pop(0)
            ws("tensor", [t_sq0, psum_last.get(("kr", 0)), t_ones2])
            t_mmn = o("tensor", lambda e: e.matmul(
                ps4[0:2, 0:512], ones2b[:, :], sq_t[:, 1, :], start=True, stop=True), "st")
            kn_mm_prev[0] = t_mmn
            psum_last[("kr", 0)] = t_mmn
            w("vector", t_mmn)
            t_rdn = o("vector", lambda e, p0=p0, s80=s80: e.tensor_reduce(
                knall[:, p0 * 8 + s80:p0 * 8 + s80 + 1], ps4[0:2, 0:512], AX.X, ALU.max), "sv")
            psum_last[("kr", 0)] = t_rdn
            kn_rd_toks.append(t_rdn)
        j = kn_i[0]
        if j < 64:
            s8j, pj = j // 8, j % 8
            if (pj, s8j) in u4_tok:
                ws("vector", [u4_tok[(pj, s8j)], kn_mm_prev[0]])
                t_sqn = o("vector", lambda e, pj=pj, s8j=s8j: e.scalar_tensor_tensor(
                    sq_t[:, 1, :], K_fm[:, pj, s8j * 512:(s8j + 1) * 512], 1.0,
                    K_fm[:, pj, s8j * 512:(s8j + 1) * 512], ALU.bypass, ALU.mult), "sv")
                kn_sqd.append((pj, s8j, t_sqn))
                kn_i[0] += 1

    for kt in range(NKT):
        lbi = kt % 2
        ws("sync", t_ln1_both + [lx_read[lbi]])
        ttr = None
        for fc in range(8):
            ttr = o("sync", lambda e, kt=kt, fc=fc, lbi=lbi: e.dma_start_transpose(
                lx_t[:, lbi, fc, :],
                ln1x_d[kt * 512:(kt + 1) * 512, fc * 128:(fc + 1) * 128]), "lxt%d" % lbi, 16)
        t_lx = ttr
        ckv_toks = []
        for cc in range(4):
            psl = [psA[:, 0:512], psA[:, 512:1024], psB[:, 0:512], psB[:, 512:1024]][cc]
            deps = [t_lx, t_wB1, psum_last.get(("pAB", cc)), ckv_reuse]
            if cc >= 2:
                deps += [psum_last.get(("kc", 1)), psum_last.get(("kr", 1)),
                         psum_last.get(("sw", 1))]
            ws("tensor", deps)
            tokm = None
            for fc in range(8):
                def mmf(e, cc=cc, fc=fc, psl=psl, lbi=lbi):
                    return e.matmul(psl, wdkv_bf[:, fc, cc * 128:(cc + 1) * 128],
                                    lx_t[:, lbi, fc, :], start=(fc == 0), stop=(fc == 7))
                tokm = o("tensor", mmf, "st") if fc == 7 else o("tensor", mmf)
            w("scalar", tokm)
            t_ev = o("scalar", lambda e, cc=cc, psl=psl: e.activation(
                ckv_t[:, cc, :], psl, AF.Identity, bias=bdkv_s[:, cc:cc + 1], scale=1.0), "ss")
            psum_last[("pAB", cc)] = t_ev
            ckv_toks.append(t_ev)
        for tc in range(4):
            kcix = kt * 4 + tc
            for h2 in range(2):
                psl = ps1[:, 0:512] if h2 == 0 else ps2[:, 0:512]
                ws("tensor", ckv_toks + [psum_last.get(("pV", h2)),
                                         psum_last.get(("sw", h2))])
                tokm = None
                for cc in range(4):
                    def mmv(e, cc=cc, tc=tc, h2=h2, psl=psl):
                        return e.matmul(psl, ckv_t[:, cc, tc * 128:(tc + 1) * 128],
                                        wuv_bf[:, cc, h2 * 512:(h2 + 1) * 512],
                                        start=(cc == 0), stop=(cc == 3))
                    tokm = o("tensor", mmv, "st") if cc == 3 else o("tensor", mmv)
                ws("vector", [tokm, t_vB, t_vones])
                vv = V_tm[:, kcix, :].rearrange("p (h v) -> p h v", v=65)
                t_vev = o("vector", lambda e, h2=h2, psl=psl, vv=vv: e.tensor_tensor(
                    vv[:, h2 * 8:(h2 + 1) * 8, 0:64],
                    psl.rearrange("p (h v) -> p h v", v=64),
                    vB_bc[:, h2 * 512:(h2 + 1) * 512].rearrange("p (h v) -> p h v", v=64),
                    ALU.add), "sv")
                psum_last[("pV", h2)] = t_vev
                t_last_vev = t_vev
        for p in range(NPAIR):
            cbi = p % 2
            if kt >= 1:
                pump_kn()
            KCP = ps3[:, 0:512] if cbi == 0 else psB[:, 0:512]
            KRP = ps4[:, 0:512] if cbi == 0 else psB[:, 512:1024]
            SWP = ps1[:, 0:512] if cbi == 0 else ps2[:, 0:512]
            w("sync", wkr_reuse[cbi])
            t_wk = o("sync", lambda e, p=p, cbi=cbi: e.dma_start(
                out=wkrp[:, cbi, :, :],
                in_=wkr_d[:, :, p * 128:(p + 1) * 128]), "wkl%d" % cbi, 16)
            w("sync", cs_reuse[cbi])
            t_c1 = o("sync", lambda e, p=p, kt=kt, cbi=cbi: e.dma_start(
                out=cs_t[:, cbi, 0, :],
                in_=cos_d[p * 128:(p + 1) * 128, kt * 512:(kt + 1) * 512]), "csl%d" % cbi, 16)
            t_c2 = o("sync", lambda e, p=p, kt=kt, cbi=cbi: e.dma_start(
                out=cs_t[:, cbi, 1, :],
                in_=sin_d[p * 128:(p + 1) * 128, kt * 512:(kt + 1) * 512]), "csl%d" % cbi, 16)
            ws("tensor", ckv_toks + [psum_last.get(("kc", cbi))])
            tok_kc = None
            for cc in range(4):
                def mmk(e, cc=cc, p=p, KCP=KCP):
                    return e.matmul(KCP, wuk_bf[:, cc, p * 128:(p + 1) * 128],
                                    ckv_t[:, cc, :], start=(cc == 0), stop=(cc == 3))
                tok_kc = o("tensor", mmk, "st") if cc == 3 else o("tensor", mmk)
            ws("scalar", [tok_kc, rope_gate[cbi]])
            t_kc = o("scalar", lambda e, p=p, cbi=cbi, KCP=KCP: e.activation(
                kc_bf[:, cbi, :], KCP, AF.Identity, bias=buk_s[:, p:p + 1], scale=1.0), "ss")
            psum_last[("kc", cbi)] = t_kc
            ws("tensor", [t_lx, t_wk, psum_last.get(("kr", cbi))])
            tok_kr = None
            for fc in range(8):
                def mmr(e, fc=fc, cbi=cbi, KRP=KRP, lbi=lbi):
                    return e.matmul(KRP, wkrp[:, cbi, fc, :],
                                    lx_t[:, lbi, fc, :], start=(fc == 0), stop=(fc == 7))
                tok_kr = o("tensor", mmr, "st") if fc == 7 else o("tensor", mmr)
            wkr_reuse[cbi] = tok_kr
            ws("scalar", [tok_kr, rope_gate[cbi]])
            t_kr = o("scalar", lambda e, p=p, cbi=cbi, KRP=KRP: e.activation(
                rp_t0[:, cbi, :], KRP, AF.Identity, bias=bkr_s[:, p:p + 1], scale=1.0), "ss")
            psum_last[("kr", cbi)] = t_kr
            ws("tensor", [t_kr, psum_last.get(("sw", cbi)),
                          psum_last.get(("pV", cbi)), t_small_loads])
            tok_sw = o("tensor", lambda e, cbi=cbi, SWP=SWP: e.matmul(
                SWP, pswap_s[:, :], rp_t0[:, cbi, :], start=True, stop=True), "st")
            ws("vector", [t_kr, tok_sw, t_c1, t_c2, rope_gate[cbi]])
            t_u1 = o("vector", lambda e, cbi=cbi: e.tensor_tensor(
                rp_t1[:, cbi, :], rp_t0[:, cbi, :], cs_t[:, cbi, 0, :], ALU.mult), "sv")
            t_u2 = o("vector", lambda e, cbi=cbi, SWP=SWP: e.tensor_tensor(
                rp_t2[:, cbi, :], SWP, cs_t[:, cbi, 1, :], ALU.mult), "sv")
            psum_last[("sw", cbi)] = t_u2
            ws("vector", [t_u1, t_u2])
            t_u3 = o("vector", lambda e, cbi=cbi: e.tensor_tensor(
                rp_t1[:, cbi, :], rp_t1[:, cbi, :], rp_t2[:, cbi, :], ALU.add), "sv")
            ws("vector", [t_u3, t_kc])
            t_u4 = o("vector", lambda e, p=p, kt=kt, cbi=cbi: e.tensor_tensor(
                K_fm[:, p, kt * 512:(kt + 1) * 512], rp_t1[:, cbi, :], kc_bf[:, cbi, :], ALU.add), "sv")
            cs_reuse[cbi] = t_u2
            rope_gate[cbi] = t_u4
            t_last_rope = t_u4
            u4_tok[(p, kt)] = t_u4
            if p == NPAIR - 1:
                lx_read[lbi] = tok_kr
        ckv_reuse = t_u4
    t_K_done = t_u4
    t_V_done = t_last_vev


    kb.join()
    stB1.close()
    ctq_t = sbB("ctq_t", [128, 4, 2, 512], BF16)
    wdq_bf = sbB("wdq_bf", [128, 8, 512], BF16)
    wuq_bf = sbB("wuq_bf", [128, 4, 1024], BF16)
    wqr_bf = sbB("wqr_bf", [128, 4, 1024], BF16)
    q_bf = sbB("q_bf", [128, 2, 512], BF16)

    t_wb2 = loadw(wdq_bf, wdq_d, 8, 512, t_wB1, "wlb2")
    t_wb2 = loadw(wuq_bf, wuq_d, 4, 1024, t_wb2, "wlb2")
    t_wb2 = loadw(wqr_bf, wqr_d, 4, 1024, t_wb2, "wlb2")

    t_qspill = None
    qsp_gate = [None, None]
    qsp_last = [None, None]
    pend_qn = [None]
    nsq_gate = [None, None]
    ct_read = [None, None]

    def emit_qnorm(p, qt, cbi, t_u4tok):
        ws("vector", [t_u4tok, t_qn_pr[0]])
        t_sq2 = o("vector", lambda e, cbi=cbi: e.scalar_tensor_tensor(
            sq_t[:, 0, :], q_bf[:, cbi, :], 1.0, q_bf[:, cbi, :], ALU.bypass, ALU.mult), "sv")
        nsq_gate[cbi] = t_sq2
        ws("tensor", [t_sq2, psum_last.get(("kc", 0))])
        t_mm = o("tensor", lambda e: e.matmul(
            ps3[0:2, 0:512], ones2b[:, :], sq_t[:, 0, :], start=True, stop=True), "st")
        psum_last[("kc", 0)] = t_mm
        w("vector", t_mm)
        t_rd = o("vector", lambda e, p=p, qt=qt: e.tensor_reduce(
            qn2[:, p * 4 + qt:p * 4 + qt + 1], ps3[0:2, 0:512], AX.X, ALU.max), "sv")
        psum_last[("kc", 0)] = t_rd
        t_qn_pr[0] = t_rd

    t_qn_pr = [None]
    for qt in range(NQT):
        lbi = (NKT + qt) % 2
        ws("sync", t_ln1_both + [lx_read[lbi]])
        ttr = None
        for fc in range(8):
            ttr = o("sync", lambda e, qt=qt, fc=fc, lbi=lbi: e.dma_start_transpose(
                lx_t[:, lbi, fc, :],
                ln1x_d[qt * 512:(qt + 1) * 512, fc * 128:(fc + 1) * 128]), "lxt%d" % lbi, 16)
        t_lx = ttr
        ctq_toks = []
        for cc in range(4):
            psl = [psA[:, 0:512], psA[:, 512:1024], psB[:, 0:512], psB[:, 512:1024]][cc]
            deps = [t_lx, t_wb2, psum_last.get(("pAB", cc))]
            if cc >= 2:
                deps += [psum_last.get(("kc", 1)), psum_last.get(("kr", 1)),
                         psum_last.get(("sw", 1))]
            ws("tensor", deps)
            tokm = None
            for fc in range(8):
                def mmf(e, cc=cc, fc=fc, psl=psl, lbi=lbi):
                    return e.matmul(psl, wdq_bf[:, fc, cc * 128:(cc + 1) * 128],
                                    lx_t[:, lbi, fc, :], start=(fc == 0), stop=(fc == 7))
                tokm = o("tensor", mmf, "st") if fc == 7 else o("tensor", mmf)
            ws("scalar", [tokm, ct_read[qt % 2]])
            t_ev = o("scalar", lambda e, cc=cc, qt=qt, psl=psl: e.activation(
                ctq_t[:, cc, qt % 2, :], psl, AF.Identity,
                bias=bdq_s[:, cc:cc + 1], scale=1.0), "ss")
            psum_last[("pAB", cc)] = t_ev
            ctq_toks.append(t_ev)
        lq_reuse = t_ev
        for p in range(NPAIR):
            cbi = p % 2
            QCP = ps3[:, 0:512] if cbi == 0 else psB[:, 0:512]
            QRP = ps4[:, 0:512] if cbi == 0 else psB[:, 512:1024]
            SWP = ps1[:, 0:512] if cbi == 0 else ps2[:, 0:512]
            if qt < 2:
                pump_kn()
            if pend_qn[0] is not None:
                emit_qnorm(*pend_qn[0])
                pend_qn[0] = None
            w("sync", cs_reuse[cbi])
            t_c1 = o("sync", lambda e, p=p, qt=qt, cbi=cbi: e.dma_start(
                out=cs_t[:, cbi, 0, :],
                in_=cos_d[p * 128:(p + 1) * 128, qt * 512:(qt + 1) * 512]), "csl%d" % cbi, 16)
            t_c2 = o("sync", lambda e, p=p, qt=qt, cbi=cbi: e.dma_start(
                out=cs_t[:, cbi, 1, :],
                in_=sin_d[p * 128:(p + 1) * 128, qt * 512:(qt + 1) * 512]), "csl%d" % cbi, 16)
            ws("tensor", ctq_toks + [psum_last.get(("kc", cbi))])
            tok_qc = None
            for cc in range(4):
                def mmk(e, cc=cc, p=p, qt=qt, QCP=QCP):
                    return e.matmul(QCP, wuq_bf[:, cc, p * 128:(p + 1) * 128],
                                    ctq_t[:, cc, qt % 2, :],
                                    start=(cc == 0), stop=(cc == 3))
                tok_qc = o("tensor", mmk, "st") if cc == 3 else o("tensor", mmk)
            ws("scalar", [tok_qc, rope_gate[cbi]])
            t_qc = o("scalar", lambda e, p=p, cbi=cbi, QCP=QCP: e.activation(
                kc_bf[:, cbi, :], QCP, AF.Identity, bias=buq_s[:, p:p + 1], scale=1.0), "ss")
            psum_last[("kc", cbi)] = t_qc
            ws("tensor", ctq_toks + [psum_last.get(("kr", cbi))])
            tok_qr = None
            for cc in range(4):
                def mmr(e, cc=cc, p=p, qt=qt, QRP=QRP):
                    return e.matmul(QRP, wqr_bf[:, cc, p * 128:(p + 1) * 128],
                                    ctq_t[:, cc, qt % 2, :],
                                    start=(cc == 0), stop=(cc == 3))
                tok_qr = o("tensor", mmr, "st") if cc == 3 else o("tensor", mmr)
            ws("scalar", [tok_qr, rope_gate[cbi]])
            t_qr = o("scalar", lambda e, p=p, cbi=cbi, QRP=QRP: e.activation(
                rp_t0[:, cbi, :], QRP, AF.Identity, bias=bqr_s[:, p:p + 1], scale=1.0), "ss")
            psum_last[("kr", cbi)] = t_qr
            ws("tensor", [t_qr, t_qc, psum_last.get(("sw", cbi))])
            tok_sw = o("tensor", lambda e, cbi=cbi, SWP=SWP: e.matmul(
                SWP, pswap_s[:, :], rp_t0[:, cbi, :], start=True, stop=True), "st")
            ws("vector", [t_qr, tok_sw, t_c1, t_c2, rope_gate[cbi]])
            t_u1 = o("vector", lambda e, cbi=cbi: e.tensor_tensor(
                rp_t1[:, cbi, :], rp_t0[:, cbi, :], cs_t[:, cbi, 0, :], ALU.mult), "sv")
            t_u2 = o("vector", lambda e, cbi=cbi, SWP=SWP: e.tensor_tensor(
                rp_t2[:, cbi, :], SWP, cs_t[:, cbi, 1, :], ALU.mult), "sv")
            psum_last[("sw", cbi)] = t_u2
            ws("vector", [t_u1, t_u2])
            t_u3 = o("vector", lambda e, cbi=cbi: e.tensor_tensor(
                rp_t1[:, cbi, :], rp_t1[:, cbi, :], rp_t2[:, cbi, :], ALU.add), "sv")
            ws("vector", [t_u3, t_qc, qsp_gate[cbi], nsq_gate[cbi]])
            t_u4 = o("vector", lambda e, cbi=cbi: e.tensor_tensor(
                q_bf[:, cbi, :], rp_t1[:, cbi, :], kc_bf[:, cbi, :], ALU.add), "sv")
            cs_reuse[cbi] = t_u2
            rope_gate[cbi] = t_u4
            if p == NPAIR - 1:
                lx_read[lbi] = tok_qr
                ct_read[qt % 2] = tok_qr
            w("gpsimd", t_u4)
            t_qspill = o("gpsimd", lambda e, p=p, qt=qt, cbi=cbi: e.dma_start(
                out=q_d[p, :, qt * 512:(qt + 1) * 512], in_=q_bf[:, cbi, :]), "qsp%d" % cbi, 16)
            qsp_gate[cbi] = t_qspill
            qsp_last[cbi] = t_qspill
            pend_qn[0] = (p, qt, cbi, t_u4)
    emit_qnorm(*pend_qn[0])
    pend_qn[0] = None
    while kn_sqd or kn_i[0] < 64:
        pump_kn()
    t_kn = None
    ws("vector", kn_rd_toks)
    for p in range(NPAIR):
        t_kn = o("vector", lambda e, p=p: e.tensor_reduce(
            kn2[:, p:p + 1], knall[:, p * 8:(p + 1) * 8], AX.X, ALU.max), "sv")
    t_knorm = t_kn
    t_rd = t_qn_pr[0]
    t_Q_done = [qsp_last[0], qsp_last[1]]
    t_qnorm = t_rd

    ws("scalar", [t_qnorm, t_knorm])
    ws("scalar", t_consts)
    t_b1 = o("scalar", lambda e: e.activation(qn2[:, :], qn2[:, :], AF.Ln, bias=tiny_b[0:2, :], scale=1.0), "ss")
    w("scalar", t_b1)
    t_b2 = o("scalar", lambda e: e.activation(qn2[:, :], qn2[:, :], AF.Exp, bias=0.0, scale=0.5), "ss")
    w("scalar", t_b2)
    t_b3 = o("scalar", lambda e: e.activation(kn2[:, :], kn2[:, :], AF.Ln, bias=tiny_b[0:2, :], scale=1.0), "ss")
    w("scalar", t_b3)
    t_b4 = o("scalar", lambda e: e.activation(kn2[:, :], kn2[:, :], AF.Exp, bias=0.0, scale=0.5), "ss")
    kn4v = kn4[:, :].rearrange("p (q m) -> p q m", m=4)
    w("vector", t_b4)
    t_b5 = None
    for mq in range(4):
        w("vector", t_b5)
        t_b5 = o("vector", lambda e, mq=mq, kn4v=kn4v: e.tensor_copy(
            kn4v[:, :, mq], kn2[:, :]), "sv")
    ws("vector", [t_b5, t_b2])
    t_b6 = o("vector", lambda e: e.tensor_tensor(cc_sc[:, :], qn2[:, :], kn4[:, :], ALU.mult), "sv")
    ws("tensor", [t_b6, t_o21, psum_last.get("ps4")])
    t_b7 = o("tensor", lambda e: e.matmul(
        ps4[0:1, 0:32], ones21[:, :], cc_sc[:, :], start=True, stop=True), "st")
    w("vector", t_b7)
    t_b8 = o("vector", lambda e: e.tensor_scalar(
        negc[:, :], ps4[0:1, 0:32], -INV_SCALE * 1.0001, None, ALU.mult), "sv")
    psum_last["ps4"] = t_b8
    ws("tensor", [t_b8, psum_last.get("ps3")])
    t_b9 = o("tensor", lambda e: e.matmul(
        ps3[:, 0:32], ones_row[:, :], negc[:, :], start=True, stop=True), "st")
    w("vector", t_b9)
    t_bias_c = o("vector", lambda e: e.tensor_copy(bias_c[:, :], ps3[:, 0:32]), "sv")
    psum_last["ps3"] = t_bias_c

    kb.join()
    stB.close()

    stC = contextlib.ExitStack()
    sbC = mksb(stC)
    O_fm = sbC("O_fm", [128, 2, 8, QMAC], BF16)
    P_pr = sbC("P_pr", [128, 2, 1024], BF16)
    Qm = sbC("Qm", [128, NPAIR, QMAC], BF16)
    Oraw = sbC("Oraw", [128, 2, 512])
    wo_bf = sbC("wo_bf", [128, 8, D_], BF16)
    woB_bc = sbC("woB_bc", [128, D_])
    w_sb = sbC("w_sb", [128, D_])
    xr_t = sbC("xr_t", [128, D_])
    taff = sbC("taff", [128, D_])
    r_sb = sbC("r_sb", [128, 512])

    t_wol = loadw(wo_bf, wo_d, 8, 1024, None, "wlc")
    t_rb2 = o("sync", lambda e: e.dma_start(out=xr_t[0:1, 0:1024], in_=bo_row), "rowb", 16)
    t_woB = None
    for h in range(2):
        ws("tensor", [t_rb2, t_ones, psum_last.get("ps4")])
        t_mmb = o("tensor", lambda e, h=h: e.matmul(
            ps4[:, 0:512], ones_row[:, :], xr_t[0:1, h * 512:(h + 1) * 512],
            start=True, stop=True), "st")
        w("vector", t_mmb)
        t_woB = o("vector", lambda e, h=h: e.tensor_copy(
            woB_bc[:, h * 512:(h + 1) * 512], ps4[:, 0:512]), "sv")
        psum_last["ps4"] = t_woB

    if dbg:
        dbgw_d = dint("dbgw_d", [NTT, 128, D_])
        dbgo_d = dint("dbgo_d", [NMACRO, 128, 8, QMAC], BF16)
        dbgp_d = dint("dbgp_d", [128, 1024], BF16)
        dbgs_d = dint("dbgs_d", [NMACRO, NPAIR, 2, 512])
    s_toks = {0: None, 1: None}
    p_toks = {0: None, 1: None}
    av_read = {0: None, 1: None}
    oraw_gate = {0: None, 1: None}
    qm_pair = [None] * NPAIR
    CS = {"t_r_ps": None, "t_wo_ps": t_woB, "o_read": [None, None],
          "t_af": None, "xr_gate": [t_woB], "t_x1s": None, "t_am2": None}
    evac_toks = {m: [] for m in range(NMACRO)}

    def load_qm_pair(mm, p, gate):
        ws("sync", t_Q_done + [gate])
        qm_pair[p] = o("sync", lambda e, p=p, mm=mm: e.dma_start(
            out=Qm[:, p, :], in_=q_d[p, :, mm * 512:(mm + 1) * 512]), "qml%d" % p, 16)

    def emit_norm(m, p, t_cps):
        om = m % 2
        for j in range(2):
            t_cp = t_cps[j]
            if dbg:
                w("vector", t_cp)
                t_dsum = o("vector", lambda e, j=j: e.tensor_copy(
                    r_sb[0:1, 0:512], Oraw[64:65, j, :]), "sv")
                w("gpsimd", t_dsum)
                o("gpsimd", lambda e, m=m, p=p, j=j: e.dma_start(
                    out=dbgs_d[m, p, j, :].rearrange("(o f) -> o f", o=1),
                    in_=r_sb[0:1, 0:512]), "dbgs", 16)
                ws("vector", [t_dsum])
            ws("vector", [t_cp, CS["t_r_ps"], CS.get("t_oe_prev")])
            t_rc = o("vector", lambda e, j=j: e.reciprocal(
                r_sb[0:1, :], Oraw[64:65, j, :]), "sv")
            ws("tensor", [t_rc, psum_last.get("ps3")])
            t_ro = o("tensor", lambda e: e.matmul(
                ps3[:, 0:512], ones_row[:, :], r_sb[0:1, :],
                start=True, stop=True), "st")
            w("vector", t_ro)
            t_rcp = o("vector", lambda e: e.tensor_copy(r_sb[:, :], ps3[:, 0:512]), "sv")
            psum_last["ps3"] = t_rcp
            CS["t_r_ps"] = t_rcp
            ws("vector", [t_rcp, CS["o_read"][om]])
            t_oe = o("vector", lambda e, p=p, j=j, om=om: e.tensor_tensor(
                O_fm[j * 64:(j + 1) * 64, om, p, :], Oraw[0:64, j, :],
                r_sb[0:64, 0:512], ALU.mult), "sv")
            oraw_gate[j] = t_oe
            CS["t_oe_prev"] = t_oe
            evac_toks[m].append(t_oe)

    def emit_epi(m, tc4):
        om = m % 2
        tcg = m * 4 + tc4
        wo_half_toks = []
        tokm = None
        for h2 in range(2):
            ws("tensor", evac_toks[m] + [t_wol, CS["t_wo_ps"], psum_last.get("ps4")])
            for dc in range(8):
                def mmw(e, dc=dc, tc4=tc4, h2=h2, om=om):
                    return e.matmul(ps4[:, 0:512], O_fm[:, om, dc, tc4 * 128:(tc4 + 1) * 128],
                                    wo_bf[:, dc, h2 * 512:(h2 + 1) * 512],
                                    start=(dc == 0), stop=(dc == 7))
                tokm = o("tensor", mmw, "st") if dc == 7 else o("tensor", mmw)
            ws("vector", [tokm, t_woB, CS["t_af"]])
            t_wv = o("vector", lambda e, h2=h2: e.tensor_tensor(
                w_sb[:, h2 * 512:(h2 + 1) * 512], ps4[:, 0:512],
                woB_bc[:, h2 * 512:(h2 + 1) * 512], ALU.add), "sv")
            CS["t_wo_ps"] = t_wv
            psum_last["ps4"] = t_wv
            wo_half_toks.append(t_wv)
        CS["o_read"][om] = tokm
        if dbg:
            ws("gpsimd", wo_half_toks)
            t_dw = o("gpsimd", lambda e, tcg=tcg: e.dma_start(
                out=dbgw_d[tcg], in_=w_sb[:, :]), "dbgs", 16)
            ws("vector", [t_dw])
            ws("scalar", [t_dw])
        ws("vector", wo_half_toks)
        t_ln2, rstd2_ap, nmr2_ap, g2 = ln_stats_from(w_sb[:, :], [CS["t_af"], CS["t_am2"]])
        ws("sync", CS["xr_gate"])
        t_xl = o("sync", lambda e, tcg=tcg: e.dma_start(
            out=xr_t[:, :], in_=x_in[tcg * 128:(tcg + 1) * 128, :]), "xrl", 16)
        ws("scalar", [t_ln2, CS["t_am2"]])
        t_af = o("scalar", lambda e, rstd2_ap=rstd2_ap, nmr2_ap=nmr2_ap: e.activation(
            taff[:, :], w_sb[:, :], AF.Identity, bias=nmr2_ap, scale=rstd2_ap), "ss")
        CS["t_af"] = t_af
        ws("vector", [t_af, t_xl, CS["t_x1s"]])
        t_x1 = o("vector", lambda e: e.tensor_tensor(
            xr_t[:, :], xr_t[:, :], taff[:, :], ALU.add), "sv")
        w("gpsimd", t_x1)
        CS["t_x1s"] = o("gpsimd", lambda e, tcg=tcg: e.dma_start(
            out=x1_d[tcg * 128:(tcg + 1) * 128, :], in_=xr_t[:, :]), "x1s", 16)
        w("vector", t_x1)
        ln_mark_readers(g2, [t_af, t_x1])
        t_ln3, rstd3_ap, nmr3_ap, g3 = ln_stats_from(xr_t[:, :], [t_af, t_x1])
        w("vector", t_ln3)
        t_sv3 = o("vector", lambda e, tcg=tcg, rstd3_ap=rstd3_ap: e.tensor_copy(
            rstd3a[:, tcg:tcg + 1], rstd3_ap), "sv")
        t_sn3 = o("vector", lambda e, tcg=tcg, nmr3_ap=nmr3_ap: e.tensor_copy(
            nmr3a[:, tcg:tcg + 1], nmr3_ap), "sv")
        ws("vector", [t_sv3, t_sn3, t_af, t_x1])
        t_x3 = o("vector", lambda e, rstd3_ap=rstd3_ap, nmr3_ap=nmr3_ap: e.tensor_scalar(
            taff[:, :], xr_t[:, :], rstd3_ap, nmr3_ap, ALU.mult, ALU.add), "sv")
        CS["xr_gate"] = [CS["t_x1s"], t_ln3, t_x3]
        w("vector", t_x3)
        t_am = o("vector", lambda e, g3=g3: e.tensor_reduce(
            s_t1[:, g3:g3 + 1], taff[:, :], AX.X, ALU.max,
            apply_absolute_value=True), "sv")
        ws("vector", [t_am, t_amz])
        CS["t_am2"] = o("vector", lambda e, g3=g3: e.tensor_tensor(
            amax_r[:, :], amax_r[:, :], s_t1[:, g3:g3 + 1], ALU.max), "sv")
        ln_mark_readers(g3, [t_sv3, t_sn3, t_x3, t_am])

    for p in range(NPAIR):
        load_qm_pair(0, p, None)
    pend_norm = None
    pend_epi = []
    for m in range(NMACRO):
        for p in range(NPAIR):
            hA, hB = 2 * p, 2 * p + 1
            t_avA = t_avB = None
            pend = [None]

            def emit_av(kc, t_ex):
                nonlocal t_avA, t_avB
                sbix = kc % 2
                av_deps = [t_ex]
                if kc == 0:
                    av_deps += [av_read[0], av_read[1]]
                ws("tensor", av_deps)
                t_avA = o("tensor", lambda e, kc=kc, hA=hA, sbix=sbix: e.matmul(
                    ps1[0:65, 0:512], V_tm[:, kc, hA * 65:hA * 65 + 65],
                    P_pr[:, sbix, 0:512],
                    start=(kc == 0), stop=(kc == NKC - 1)), "st")
                t_avB = o("tensor", lambda e, kc=kc, hB=hB, sbix=sbix: e.matmul(
                    ps2[0:65, 0:512], V_tm[:, kc, hB * 65:hB * 65 + 65],
                    P_pr[:, sbix, 512:1024],
                    start=(kc == 0), stop=(kc == NKC - 1)), "st")
                p_toks[sbix] = t_avB

            for kc in range(NKC):
                sbix = kc % 2
                S = psA if sbix == 0 else psB
                ws("tensor", [s_toks[sbix], qm_pair[p], t_K_done, t_bias_c])
                o("tensor", lambda e, p=p, kc=kc, S=S: e.matmul(
                    S[:, 0:512], K_fm[0:64, p, kc * 128:(kc + 1) * 128],
                    Qm[0:64, p, :], start=True, stop=True, tile_position=(0, 0)))
                t_qk = o("tensor", lambda e, p=p, kc=kc, S=S: e.matmul(
                    S[:, 512:1024], K_fm[64:128, p, kc * 128:(kc + 1) * 128],
                    Qm[64:128, p, :], start=True, stop=True, tile_position=(64, 0)), "st")
                if kc == NKC - 1 and m + 1 < NMACRO:
                    load_qm_pair(m + 1, p, t_qk)
                ws("scalar", [t_qk, p_toks[sbix]])
                t_ex = o("scalar", lambda e, p=p, m=m, S=S, sbix=sbix: e.activation(
                    P_pr[:, sbix, :], S[:, :], AF.Exp,
                    bias=bias_c[:, p * 4 + m:p * 4 + m + 1], scale=INV_SCALE), "ss")
                s_toks[sbix] = t_ex
                if dbg and m == 0 and p == 0 and kc == 0:
                    w("gpsimd", t_ex)
                    t_dp = o("gpsimd", lambda e, sbix=sbix: e.dma_start(
                        out=dbgp_d[:, :], in_=P_pr[:, sbix, :]), "dbgs", 16)
                    ws("scalar", [t_dp])
                if pend[0] is not None:
                    emit_av(*pend[0])
                pend[0] = (kc, t_ex)
                if kc == 6 and pend_norm is not None:
                    emit_norm(*pend_norm)
                    pend_norm = None
                if kc == 16 and pend_epi:
                    emit_epi(*pend_epi.pop(0))
            emit_av(*pend[0])
            t_cps = []
            for j, (avps, t_av) in enumerate([(ps1, t_avA), (ps2, t_avB)]):
                ws("vector", [t_av, oraw_gate[j]])
                t_cp = o("vector", lambda e, j=j, avps=avps: e.tensor_copy(
                    Oraw[0:65, j, :], avps[0:65, 0:512]), "sv")
                av_read[j] = t_cp
                t_cps.append(t_cp)
            pend_norm = (m, p, t_cps)
        pend_epi += [(m, tc4) for tc4 in range(4)]
    emit_norm(*pend_norm)
    pend_norm = None
    for item in pend_epi:
        emit_epi(*item)
    pend_epi = []
    t_x1s = CS["t_x1s"]
    t_am2 = CS["t_am2"]
    t_x1_last = t_x1s
    t_amax_done = t_am2

    kb.join()
    stC.close()
    stKV.close()

    stD = contextlib.ExitStack()
    sbD = mksb(stD)
    xsimT = sbD("xsimT", [128, 8, NTOK], BF16)
    wjt = sbD("wjt", [128, 2, 8, 128], BF16)
    wos_s = sbD("wos_s", [128, 32, D_], BF16)
    hg_t = sbD("hg_t", [128, 2, 512], BF16)
    hq_f = sbD("hq_f", [128, 4, 512], BF16)
    hq_w = sbD("hq_w", [128, 2, 512])
    hq_bf = sbD("hq_bf", [128, 4, 512], BF16)
    xq_t = sbD("xq_t", [128, 4, D_])
    xq_bf = sbD("xq_bf", [128, 4, D_], BF16)
    y_sb = sbD("y_sb", [128, D_])
    boB_bc = sbD("boB_bc", [128, D_])
    w_sb2 = sbD("w_sb2", [128, D_])
    xr_t2 = sbD("xr_t2", [128, D_])
    taff2 = sbD("taff2", [128, D_])

    def amax_collective(src_col, cc_in_h, cc_out_h, dst_s, dst_inv, prev_toks):
        ws("gpsimd", prev_toks)
        t_f1 = o("gpsimd", lambda e: e.dma_start(
            out=flip_d[0:1, 0:128], in_=src_col[:, :]), "flst", 16)
        w("sync", t_f1)
        t_f2 = o("sync", lambda e: e.dma_start(out=grow[:, :], in_=flip_d[0:1, 0:128]), "flrd", 16)
        w("gpsimd", t_f2)
        t_c1 = o("gpsimd", lambda e: e.dma_start(out=cc_in_h.ap(), in_=grow[:, :]), "ccb", 16)
        w("gpsimd", t_c1)
        t_cc = o("gpsimd", lambda e: e.collective_compute(
            "AllReduce", ALU.max, replica_groups=[list(range(8))],
            ins=[cc_in_h.ap().opt()], outs=[cc_out_h.ap().opt()]), "sg")
        w("gpsimd", t_cc)
        t_c2 = o("gpsimd", lambda e: e.dma_start(out=grow[:, :], in_=cc_out_h.ap()), "ccb", 16)
        w("vector", t_c2)
        t_g1 = o("vector", lambda e: e.tensor_reduce(sc_a[:, :], grow[:, :], AX.X, ALU.max), "sv")
        w("vector", t_g1)
        t_g2 = o("vector", lambda e: e.tensor_scalar(sc_a[:, :], sc_a[:, :], 1e-8, None, ALU.max), "sv")
        w("vector", t_g2)
        t_g3 = o("vector", lambda e: e.reciprocal(sc_c[:, :], sc_a[:, :]), "sv")
        w("vector", t_g3)
        t_g4 = o("vector", lambda e: e.tensor_scalar(
            sc_b[:, :], sc_c[:, :], QMAX_A16, None, ALU.mult), "sv")
        w("vector", t_g4)
        t_g6 = o("vector", lambda e: e.tensor_scalar(
            sc_c[:, :], sc_a[:, :], 1.0 / QMAX_A16, None, ALU.mult), "sv")
        ws("tensor", [t_g6, psum_last.get("ps3")])
        o("tensor", lambda e: e.matmul(ps3[:, 0:1], ones_row[:, :], sc_b[:, :], start=True, stop=True))
        t_g8 = o("tensor", lambda e: e.matmul(ps3[:, 1:2], ones_row[:, :], sc_c[:, :], start=True, stop=True), "st")
        w("vector", t_g8)
        t_g9 = o("vector", lambda e: e.tensor_copy(dst_s[:, :], ps3[:, 0:1]), "sv")
        w("vector", t_g9)
        t_g10 = o("vector", lambda e: e.tensor_copy(dst_inv[:, :], ps3[:, 1:2]), "sv")
        psum_last["ps3"] = t_g10
        return t_g10

    t_sx = amax_collective(amax_r, cc1_in_h, cc1_out_h, sxb, invsxb, [t_amax_done])

    xq_g3 = [None] * 4
    xq_g4 = [None] * 4
    grp_done = [None] * 4
    for tcg in range(NTT):
        bq = tcg % 4
        ws("sync", [xq_g4[bq]])
        t_l = o("sync", lambda e, tcg=tcg, bq=bq: e.dma_start(
            out=xq_t[:, bq, :], in_=x1_d[tcg * 128:(tcg + 1) * 128, :]), "xql%d" % bq, 16)
        ws("vector", [t_l, t_sx, xq_g3[bq]])
        t_q1 = o("vector", lambda e, tcg=tcg, bq=bq: e.tensor_tensor(
            q2sc[:, bq:bq + 1], rstd3a[:, tcg:tcg + 1], sxb[:, :], ALU.mult), "sv")
        t_q2 = o("vector", lambda e, tcg=tcg, bq=bq: e.tensor_tensor(
            q2bi[:, bq:bq + 1], nmr3a[:, tcg:tcg + 1], sxb[:, :], ALU.mult), "sv")
        w("vector", t_q2)
        t_q2b = o("vector", lambda e, bq=bq: e.tensor_scalar(
            q2bi[:, bq:bq + 1], q2bi[:, bq:bq + 1], CMAGIC, None, ALU.add), "sv")
        ws("scalar", [t_l, t_q1, t_q2b, xq_g4[bq]])
        t_q3 = o("scalar", lambda e, bq=bq: e.activation(
            xq_t[:, bq, :], xq_t[:, bq, :], AF.Identity,
            bias=q2bi[:, bq:bq + 1], scale=q2sc[:, bq:bq + 1]), "ss")
        xq_g3[bq] = t_q3
        ws("vector", [t_q3, grp_done[tcg // 4 - 1] if tcg >= 4 else None])
        t_q4 = o("vector", lambda e, bq=bq: e.tensor_scalar(
            xq_bf[:, bq, :], xq_t[:, bq, :], CMAGIC, invsxb[:, :], ALU.subtract, ALU.mult), "sv")
        xq_g4[bq] = t_q4
        w("gpsimd", t_q4)
        t_xq_tok = o("gpsimd", lambda e, tcg=tcg, bq=bq: e.dma_start(
            out=xsim_d[tcg * 128:(tcg + 1) * 128, :], in_=xq_bf[:, bq, :]), "xst4%d" % (tcg // 4), 16)
        if tcg % 4 == 3:
            grp_done[tcg // 4] = t_xq_tok
    xt_tok = [None] * 4
    for t4 in range(4):
        ws("scalar", [grp_done[t4]])
        t_xt = None
        for fc in range(8):
            t_xt = o("scalar", lambda e, fc=fc, t4=t4: e.dma_start_transpose(
                xsimT[:, fc, t4 * 512:(t4 + 1) * 512],
                xsim_d[t4 * 512:(t4 + 1) * 512, fc * 128:(fc + 1) * 128]), "dsct%d" % t4, 16)
        xt_tok[t4] = t_xt
    t_xsimT = t_xt

    t_wosl = None
    for jc in range(32):
        t_wosl = o("sync", lambda e, jc=jc: e.dma_start(
            out=wos_s[:, jc, :], in_=wout_d[jc, :, :]), "wosl", 16)
    t_wos = t_wosl

    hb_reuse = [None, None]
    hm_reuse = [None, None]
    wj_reuse = [None, None]
    mm1_ps = {}
    t_hm2 = None
    t_hs = None
    for jc in range(32):
        bj = jc % 2
        w("sync", wj_reuse[bj])
        t_wl = o("sync", lambda e, jc=jc, bj=bj: e.dma_start(
            out=wjt[:, bj, :, :], in_=wint_d[jc, :, :, :]), "wjl%d" % bj, 16)
        for t4 in range(4):
            psl = [psA[:, 0:512], psA[:, 512:1024], psB[:, 0:512], psB[:, 512:1024]][t4]
            ws("tensor", [t_wl, xt_tok[t4], mm1_ps.get(t4)])
            tokm = None
            for fc in range(8):
                def mm1(e, fc=fc, t4=t4, bj=bj, psl=psl):
                    return e.matmul(psl, wjt[:, bj, fc, :],
                                    xsimT[:, fc, t4 * 512:(t4 + 1) * 512],
                                    start=(fc == 0), stop=(fc == 7))
                tokm = o("tensor", mm1, "st") if fc == 7 else o("tensor", mm1)
            if t4 == 3:
                wj_reuse[bj] = tokm
            bh = t4 % 2
            ws("scalar", [tokm, hb_reuse[bh], t_sjr_done, hm_reuse[bh]])
            t_ge = o("scalar", lambda e, jc=jc, bh=bh, psl=psl: e.activation(
                hg_t[:, bh, :], psl, AF.Gelu, bias=bin_s[:, jc:jc + 1],
                scale=sjr_p[:, jc:jc + 1]), "ss")
            mm1_ps[t4] = t_ge
            ws("vector", [t_ge, t_hm2])
            t_hm = o("vector", lambda e, bh=bh: e.tensor_reduce(
                hmt_s[:, :], hg_t[:, bh, :], AX.X, ALU.max, apply_absolute_value=True), "sv")
            hm_reuse[bh] = t_hm
            ws("vector", [t_hm, t_hmz])
            t_hm2 = o("vector", lambda e: e.tensor_tensor(
                hmax_r[:, :], hmax_r[:, :], hmt_s[:, :], ALU.max), "sv")
            w("gpsimd", t_ge)
            t_hs = o("gpsimd", lambda e, jc=jc, t4=t4, bh=bh: e.dma_start(
                out=hid_d[jc, :, t4 * 512:(t4 + 1) * 512], in_=hg_t[:, bh, :]), "his%d" % bh, 16)
            hb_reuse[bh] = t_hs
    t_hid_done = t_hs
    t_hmax_done = t_hm2

    kb.join()
    t_sh = amax_collective(hmax_r, cc2_in_h, cc2_out_h, shb, invshb, [t_hmax_done])

    w("sync", t_sh)
    t_bor = o("sync", lambda e: e.dma_start(out=xq_t[0:1, 0, :], in_=bout_row), "rowb", 16)
    bo_toks = []
    for h2 in range(2):
        ws("tensor", [t_bor, psum_last.get("ps3"), psum_last.get("ps4"), t_sh])
        bo_toks.append(o("tensor", lambda e, h2=h2: e.matmul(
            ps3[:, 0:512] if h2 == 0 else ps4[:, 0:512], ones_row[:, :],
            xq_t[0:1, 0, h2 * 512:(h2 + 1) * 512], start=True, stop=True), "st"))
    ws("vector", bo_toks)
    t_boB = o("vector", lambda e: e.tensor_copy(boB_bc[:, 0:512], ps3[:, 0:512]), "sv")
    t_boB2 = o("vector", lambda e: e.tensor_copy(boB_bc[:, 512:1024], ps4[:, 0:512]), "sv")
    psum_last["ps3"] = t_boB
    psum_last["ps4"] = t_boB2

    ps_pairs = [(psA[:, 0:512], psA[:, 512:1024]),
                (psB[:, 0:512], psB[:, 512:1024]),
                (ps1[:, 0:512], ps2[:, 0:512]),
                (ps3[:, 0:512], ps4[:, 0:512])]
    hqf_reuse = [None, None, None, None]
    hqw_gate = [None, None]
    mm2_last = {}
    y_store = []
    t_ln4_prev = None
    t_y_prev = None
    t_ys = None
    for tb in range(4):
        mm2_toks = {}
        for jc in range(32):
            bh = jc % 4
            w("sync", hqf_reuse[bh])
            t_hl = o("sync", lambda e, jc=jc, tb=tb, bh=bh: e.dma_start(
                out=hq_f[:, bh, :], in_=hid_d[jc, :, tb * 512:(tb + 1) * 512]), "hql%d" % bh, 16)
            bw = bh % 2
            ws("scalar", [t_hl, t_sh, hqw_gate[bw]])
            t_tq = o("scalar", lambda e, bh=bh, bw=bw: e.activation(
                hq_w[:, bw, :], hq_f[:, bh, :], AF.Identity, bias=cmag_b[:, :], scale=shb[:, :]), "ss")
            ws("vector", [t_tq, hqf_reuse[bh]])
            t_hq = o("vector", lambda e, bh=bh, bw=bw: e.tensor_scalar(
                hq_bf[:, bh, :], hq_w[:, bw, :], CMAGIC, invshb[:, :],
                ALU.subtract, ALU.mult), "sv")
            hqw_gate[bw] = t_hq
            tk = None
            for tc4 in range(4):
                pa, pb = ps_pairs[tc4]
                deps = [t_hq, t_wos]
                if jc == 0:
                    deps.append(mm2_last.get(tc4))
                ws("tensor", deps)
                def mma(e, jc=jc, tc4=tc4, bh=bh, pa=pa):
                    return e.matmul(pa, hq_bf[:, bh, tc4 * 128:(tc4 + 1) * 128],
                                    wos_s[:, jc, 0:512], start=(jc == 0), stop=(jc == 31))
                def mmb(e, jc=jc, tc4=tc4, bh=bh, pb=pb):
                    return e.matmul(pb, hq_bf[:, bh, tc4 * 128:(tc4 + 1) * 128],
                                    wos_s[:, jc, 512:1024], start=(jc == 0), stop=(jc == 31))
                o("tensor", mma)
                tk = o("tensor", mmb, "st")
                if jc == 31:
                    mm2_toks[tc4] = tk
            hqf_reuse[bh] = tk
        for tc4 in range(4):
            tcg = tb * 4 + tc4
            pa, pb = ps_pairs[tc4]
            ws("vector", [mm2_toks[tc4], t_boB, t_boB2, t_ys, t_ln4_prev])
            t_w1 = o("vector", lambda e, pa=pa: e.tensor_tensor(
                w_sb2[:, 0:512], pa, boB_bc[:, 0:512], ALU.add), "sv")
            w("vector", t_w1)
            t_w2 = o("vector", lambda e, pb=pb: e.tensor_tensor(
                w_sb2[:, 512:1024], pb, boB_bc[:, 512:1024], ALU.add), "sv")
            mm2_last[tc4] = t_w2
            w("vector", t_w2)
            t_ln4, rstd4_ap, nmr4_ap, g4 = ln_stats_from(w_sb2[:, :], [t_ln4_prev, t_y_prev])
            w("sync", t_y_prev)
            t_xl = o("sync", lambda e, tcg=tcg: e.dma_start(
                out=xr_t2[:, :], in_=x1_d[tcg * 128:(tcg + 1) * 128, :]), "xr2", 16)
            ws("scalar", [t_ln4, t_y_prev])
            t_af = o("scalar", lambda e, rstd4_ap=rstd4_ap, nmr4_ap=nmr4_ap: e.activation(
                taff2[:, :], w_sb2[:, :], AF.Identity, bias=nmr4_ap, scale=rstd4_ap), "ss")
            t_ln4_prev = t_af
            ws("vector", [t_af, t_xl])
            t_y = o("vector", lambda e: e.tensor_tensor(
                y_sb[:, :], xr_t2[:, :], taff2[:, :], ALU.add), "sv")
            t_y_prev = t_y
            ln_mark_readers(g4, [t_af, t_y])
            w("gpsimd", t_y)
            t_ys = o("gpsimd", lambda e, tcg=tcg: e.dma_start(
                out=y_out[tcg * 128:(tcg + 1) * 128, :], in_=y_sb[:, :]), "yst", 16)
            y_store.append(t_ys)

    if dbg:
        dbg_specs = [
            ("dbg_w", dbgw_d, [NTT, 128, D_], F32),
            ("dbg_o", dbgo_d, [NMACRO, 128, 8, QMAC], BF16),
            ("dbg_p00", dbgp_d, [128, 1024], BF16),
            ("dbg_sums", dbgs_d, [NMACRO, NPAIR, 2, 512], F32),
            ("dbg_ln1x", ln1x_d, [NKEY, D_], BF16),
            ("dbg_x1", x1_d, [NTOK, D_], F32),
            ("dbg_xsim", xsim_d, [NTOK, D_], BF16),
            ("dbg_q", q_d, [NPAIR, 128, NTOK], BF16),
            ("dbg_hid0", hid_d[0], [128, NTOK], BF16),
            ("dbg_hid7", hid_d[7], [128, NTOK], BF16),
            ("dbg_wint0", wint_d[0], [8, 128, 128], BF16),
            ("dbg_wout0", wout_d[0], [128, D_], BF16),
        ]
        ws("gpsimd", [y_store[-1], t_x1_last, t_hid_done, t_xq_tok])
        for nm, srcap, shp, dt in dbg_specs:
            tdg = o("gpsimd", lambda e, nm=nm, srcap=srcap, shp=shp, dt=dt: e.dma_start(
                out=nc.dram_tensor(nm, shp, dt, kind="ExternalOutput").ap(), in_=srcap), "dbgs", 16)
        for nm, srcap, shp in [("dbg_biasc", bias_c[:, :], [128, 32]),
                               ("dbg_qn2", qn2[:, :], [2, 32]),
                               ("dbg_kn2", kn2[:, :], [2, NPAIR]),
                               ("dbg_sjr", sjr_p[:, :], [128, 32]),
                               ("dbg_sxb", sxb[:, :], [128, 1]),
                               ("dbg_shb", shb[:, :], [128, 1]),
                               ("dbg_rstd3", rstd3a[:, :], [128, NTT]),
                               ("dbg_nmr3", nmr3a[:, :], [128, NTT])]:
            tdg = o("gpsimd", lambda e, nm=nm, srcap=srcap, shp=shp: e.dma_start(
                out=nc.dram_tensor(nm, shp, F32 if "wint" not in nm else BF16,
                                   kind="ExternalOutput").ap(), in_=srcap), "dbgs", 16)
        w("gpsimd", tdg)

    ws("gpsimd", [y_store[-1], t_x1_last, t_hid_done])

    with nc.Block() as block:
        block.sync(kb.runner("sync"))
        block.vector(kb.runner("vector"))
        block.scalar(kb.runner("scalar"))
        block.tensor(kb.runner("tensor"))
        block.gpsimd(kb.runner("gpsimd"))
    stD.close()
    kb.stack.close()
    return nc


def _headperm():
    pi = np.empty(D_, np.int64)
    for h in range(NH):
        for j in range(32):
            pi[64 * h + j] = 64 * h + 2 * j
            pi[64 * h + 32 + j] = 64 * h + 2 * j + 1
    return pi


def _rope_tables(positions):
    pos = np.asarray(positions, np.float32)
    ids = np.arange(512, dtype=np.float32)
    theta = np.float32(10000.0) ** (np.float32(-2.0) * ids) / np.float32(512.0)
    f = np.arange(D_)
    c = f // 128
    r = f % 128
    head = 2 * c + r // 64
    j = r % 64
    pg = 32 * head + (j % 32)
    ang = pos[None, :] * theta[pg][:, None]
    cos = np.cos(ang).astype(np.float32)
    sin = np.sin(ang).astype(np.float32)
    sin[j < 32, :] = -sin[j < 32, :]
    return (cos.astype(ml_dtypes.bfloat16), sin.astype(ml_dtypes.bfloat16))


def _pswap():
    m = np.zeros((128, 128), np.float32)
    for r in range(128):
        jj = r % 64
        partner = r + 32 if jj < 32 else r - 32
        m[partner, r] = 1.0
    return m.astype(ml_dtypes.bfloat16)


def _bias_pm(b, chunks):
    return np.ascontiguousarray(np.asarray(b, np.float32).reshape(chunks, 128).T)


_NC_CACHE = [None]


def kernel(L0_attn_x, L0_attn_Wd_q_w, L0_attn_Wd_q_b, L0_attn_Wu_q_w, L0_attn_Wu_q_b,
           L0_attn_Wq_r_w, L0_attn_Wq_r_b, L0_attn_Wk_r_w, L0_attn_Wk_r_b,
           L0_attn_Wd_kv_w, L0_attn_Wd_kv_b, L0_attn_Wu_k_w, L0_attn_Wu_k_b,
           L0_attn_Wu_v_w, L0_attn_Wu_v_b, L0_attn_Wo_w, L0_attn_Wo_b,
           L0_ffn_W_in, L0_ffn_b_in, L0_ffn_W_out, L0_ffn_b_out):
    f32 = lambda a: np.ascontiguousarray(np.asarray(a), dtype=np.float32)
    x = f32(L0_attn_x)
    pi = _headperm()
    shared = {
        "wd_q": f32(L0_attn_Wd_q_w), "wd_kv": f32(L0_attn_Wd_kv_w),
        "wu_q": np.ascontiguousarray(f32(L0_attn_Wu_q_w)[:, pi]),
        "wq_r": np.ascontiguousarray(f32(L0_attn_Wq_r_w)[:, pi]),
        "wk_r": np.ascontiguousarray(f32(L0_attn_Wk_r_w)[:, pi]),
        "wu_k": np.ascontiguousarray(f32(L0_attn_Wu_k_w)[:, pi]),
        "wu_v": f32(L0_attn_Wu_v_w), "wo": f32(L0_attn_Wo_w),
        "w_in": f32(L0_ffn_W_in), "w_out": f32(L0_ffn_W_out),
        "bd_q": _bias_pm(L0_attn_Wd_q_b, 4), "bd_kv": _bias_pm(L0_attn_Wd_kv_b, 4),
        "bu_q": _bias_pm(np.asarray(L0_attn_Wu_q_b, np.float32)[pi], 8),
        "bq_r": _bias_pm(np.asarray(L0_attn_Wq_r_b, np.float32)[pi], 8),
        "bk_r": _bias_pm(np.asarray(L0_attn_Wk_r_b, np.float32)[pi], 8),
        "bu_k": _bias_pm(np.asarray(L0_attn_Wu_k_b, np.float32)[pi], 8),
        "b_in": _bias_pm(L0_ffn_b_in, 32),
        "bu_v_row": f32(L0_attn_Wu_v_b).reshape(1, D_),
        "bo_row": f32(L0_attn_Wo_b).reshape(1, D_),
        "bout_row": f32(L0_ffn_b_out).reshape(1, D_),
        "pswap": _pswap(),
        "idn": np.eye(128, dtype=np.float32).astype(ml_dtypes.bfloat16),
        "idn32": np.eye(128, dtype=np.float32),
    }
    in_maps = []
    for core in range(8):
        bc, hc = core // 2, core % 2
        order = np.concatenate([np.arange(hc * NTOK, (hc + 1) * NTOK),
                                np.arange((1 - hc) * NTOK, (2 - hc) * NTOK)])
        cosb, sinb = _rope_tables(order)
        m = dict(shared)
        m["x"] = np.ascontiguousarray(x[bc][order])
        m["cos_d"] = np.ascontiguousarray(cosb)
        m["sin_d"] = np.ascontiguousarray(sinb)
        in_maps.append(m)

    if _NC_CACHE[0] is None:
        _NC_CACHE[0] = build_core_program()
    nc = _NC_CACHE[0]
    res = run_bass_kernel_spmd(nc, in_maps, core_ids=list(range(8)))
    out = np.empty((B_, S_, D_), np.float32)
    for core in range(8):
        bc, hc = core // 2, core % 2
        out[bc, hc * NTOK:(hc + 1) * NTOK, :] = res.results[core]["y"]
    return out


# version 14
# speedup vs baseline: 1.0082x; 1.0082x over previous
import contextlib
import numpy as np
import ml_dtypes

import concourse.bass as bass
import concourse.mybir as mybir
from concourse.bass_utils import run_bass_kernel_spmd

F32 = mybir.dt.float32
BF16 = mybir.dt.bfloat16
AF = mybir.ActivationFunctionType
ALU = mybir.AluOpType
AX = mybir.AxisListType

ENGS = ("sync", "vector", "scalar", "tensor", "gpsimd")

B_, S_, D_ = 4, 4096, 1024
NTOK, NKEY = 2048, 4096
NH, HD = 16, 64
DQ = DKV = 512
DFF = 4096
SCALE = float(np.sqrt(np.float32(NH + DKV + DQ)))
INV_SCALE = float(np.float32(1.0) / np.float32(SCALE))
EPS = 1e-5
CMAGIC = 12582912.0
QMAX_A16 = 32767.0
NMACRO, QMAC = 4, 512
NKC = NKEY // 128
NPAIR = NH // 2
NTT = NTOK // 128
NKT = NKEY // 512
NQT = NTOK // 512


class KB:

    def __init__(self, nc):
        self.nc = nc
        self.ops = {e: [] for e in ENGS}
        self.sems = {}
        self.cnt = {}
        self.waited = {e: {} for e in ENGS}
        self.dma_sems = set()
        self.stack = contextlib.ExitStack()

    def sem(self, name):
        if name not in self.sems:
            self.sems[name] = self.stack.enter_context(self.nc.semaphore(name))
            self.cnt[name] = 0
        return self.sems[name]

    def op(self, eng, fn, sem=None, n=1):
        if sem is None:
            self.ops[eng].append(fn)
            return None
        if n == 16:
            self.dma_sems.add(sem)
        h = self.sem(sem)
        self.cnt[sem] += n
        val = self.cnt[sem]

        def wrapped(e, fn=fn, h=h, n=n):
            fn(e).then_inc(h, n)
        self.ops[eng].append(wrapped)
        return (sem, val)

    def wait(self, eng, tok):
        if tok is None:
            return
        name, val = tok
        if self.waited[eng].get(name, 0) >= val:
            return
        self.waited[eng][name] = val
        h = self.sem(name)
        self.ops[eng].append(lambda e, h=h, val=val: e.wait_ge(h, val))

    def waits(self, eng, toks):
        best = {}
        for t in toks:
            if t is None:
                continue
            name, val = t
            if best.get(name, -1) < val:
                best[name] = val
        for name, val in best.items():
            self.wait(eng, (name, val))

    def join(self):
        names = [s for s in self.cnt if not s.startswith("jn_")]
        for e in ENGS:
            for sem in sorted(names):
                self.wait(e, (sem, self.cnt[sem]))
        toks = [self.op(e, lambda en: en.nop(), sem="jn_" + e) for e in ENGS]
        for e in ENGS:
            for t in toks:
                self.wait(e, t)

    def runner(self, e):
        ops = self.ops[e]

        def run(eng):
            for f in ops:
                f(eng)
        return run


def build_core_program(dbg=False):
    nc = bass.Bass("TRN2", target_bir_lowering=False, debug=False)
    kb = KB(nc)
    o, w, ws = kb.op, kb.wait, kb.waits

    def din(name, shape, dt=F32):
        return nc.dram_tensor(name, shape, dt, kind="ExternalInput").ap()

    def dint(name, shape, dt=F32):
        return nc.dram_tensor(name, shape, dt).ap()

    x_in = din("x", [NKEY, D_])
    wd_q = din("wd_q", [D_, DQ]);   wd_kv = din("wd_kv", [D_, DKV])
    wu_q = din("wu_q", [DQ, D_]);   wq_r = din("wq_r", [DQ, D_])
    wk_r = din("wk_r", [D_, D_]);   wu_k = din("wu_k", [DKV, D_])
    wu_v = din("wu_v", [DKV, D_]);  wo_w = din("wo", [D_, D_])
    w_in = din("w_in", [D_, DFF]);  w_out = din("w_out", [DFF, D_])
    bd_q = din("bd_q", [128, 4]);   bd_kv = din("bd_kv", [128, 4])
    bu_q = din("bu_q", [128, 8]);   bq_r = din("bq_r", [128, 8])
    bk_r = din("bk_r", [128, 8]);   bu_k = din("bu_k", [128, 8])
    b_in = din("b_in", [128, 32])
    bu_v_row = din("bu_v_row", [1, D_])
    bo_row = din("bo_row", [1, D_])
    bout_row = din("bout_row", [1, D_])
    cos_d = din("cos_d", [D_, NKEY], BF16)
    sin_d = din("sin_d", [D_, NKEY], BF16)
    pswap_in = din("pswap", [128, 128], BF16)
    idn_in = din("idn", [128, 128], BF16)
    idn32_in = din("idn32", [128, 128])
    y_out = nc.dram_tensor("y", [NTOK, D_], F32, kind="ExternalOutput").ap()

    ln1x_d = dint("ln1x_d", [NKEY, D_], BF16)
    x1_d = dint("x1_d", [NTOK, D_])
    xsim_d = dint("xsim_d", [NTOK, D_], BF16)
    hid_d = dint("hid_d", [32, 128, NTOK], BF16)
    wint_d = dint("wint_d", [32, 128, 8, 128], BF16)
    wout_d = dint("wout_d", [32, 128, D_], BF16)
    q_d = dint("q_d", [NPAIR, 128, NTOK], BF16)
    wdq_d = dint("wdq_d", [8, 128, DQ], BF16)
    wdkv_d = dint("wdkv_d", [8, 128, DKV], BF16)
    wuq_d = dint("wuq_d", [4, 128, D_], BF16)
    wqr_d = dint("wqr_d", [4, 128, D_], BF16)
    wkr_d = dint("wkr_d", [8, 128, D_], BF16)
    wuk_d = dint("wuk_d", [4, 128, D_], BF16)
    wuv_d = dint("wuv_d", [4, 128, D_], BF16)
    wo_d = dint("wo_d", [8, 128, D_], BF16)
    flip_d = dint("flip_d", [1, 4096])
    cc1_in_h = nc.dram_tensor("cc1_in", [1, 128], F32)
    cc1_out_h = nc.dram_tensor("cc1_out", [1, 128], F32)
    cc2_in_h = nc.dram_tensor("cc2_in", [1, 128], F32)
    cc2_out_h = nc.dram_tensor("cc2_out", [1, 128], F32)

    def mksb(stk):
        def f(name, shape, dt=F32):
            return stk.enter_context(nc.sbuf_tensor(name, shape, dt))
        return f

    sbP = mksb(kb.stack)

    def pst(name, shape):
        return kb.stack.enter_context(nc.psum_tensor(name, shape, F32))

    psA = pst("psA", [128, 1024])
    psB = pst("psB", [128, 1024])
    ps1 = pst("ps1", [128, 512])
    ps2 = pst("ps2", [128, 512])
    ps3 = pst("ps3", [128, 512])
    ps4 = pst("ps4", [128, 512])

    ones_row = sbP("ones_row", [1, 128])
    ones2 = sbP("ones2", [128, 2])
    ones2b = sbP("ones2b", [128, 2], BF16)
    ones21 = sbP("ones21", [2, 1])
    pswap_s = sbP("pswap_s", [128, 128], BF16)
    idn_s = sbP("idn_s", [128, 128], BF16)
    idn32_s = sbP("idn32_s", [128, 128])
    bias_c = sbP("bias_c", [128, 32])
    qn2 = sbP("qn2", [2, 32])
    kn2 = sbP("kn2", [2, NPAIR])
    kn4 = sbP("kn4", [2, 32])
    knall = sbP("knall", [2, 64])
    cc_sc = sbP("cc_sc", [2, 32])
    negc = sbP("negc", [1, 32])
    rstd3a = sbP("rstd3a", [128, NTT])
    nmr3a = sbP("nmr3a", [128, NTT])
    amax_r = sbP("amax_r", [128, 1])
    hmax_r = sbP("hmax_r", [128, 1])
    hmt_s = sbP("hmt_s", [128, 1])
    sjr_p = sbP("sjr_p", [128, 32])
    grow = sbP("grow", [1, 128])
    ln_sc = sbP("ln_sc", [128, 2, D_])
    s_sum = sbP("s_sum", [128, 2]); s_sq = sbP("s_sq", [128, 2])
    s_mean = sbP("s_mean", [128, 2]); s_var = sbP("s_var", [128, 2])
    s_rstd = sbP("s_rstd", [128, 2]); s_nmr = sbP("s_nmr", [128, 2])
    s_t1 = sbP("s_t1", [128, 2])
    sc_a = sbP("sc_a", [1, 1]); sc_b = sbP("sc_b", [1, 1]); sc_c = sbP("sc_c", [1, 1])
    sxb = sbP("sxb", [128, 1]); invsxb = sbP("invsxb", [128, 1])
    shb = sbP("shb", [128, 1]); invshb = sbP("invshb", [128, 1])
    q2sc = sbP("q2sc", [128, 2]); q2bi = sbP("q2bi", [128, 2])
    eps_b = sbP("eps_b", [128, 1]); cmag_b = sbP("cmag_b", [128, 1])
    tiny_b = sbP("tiny_b", [128, 1])
    bdq_s = sbP("bdq_s", [128, 4]);   bdkv_s = sbP("bdkv_s", [128, 4])
    buq_s = sbP("buq_s", [128, 8]);   bqr_s = sbP("bqr_s", [128, 8])
    bkr_s = sbP("bkr_s", [128, 8]);   buk_s = sbP("buk_s", [128, 8])
    bin_s = sbP("bin_s", [128, 32])

    psum_last = {}

    t_smld = None
    for dst, src in [(bdq_s, bd_q), (bdkv_s, bd_kv), (buq_s, bu_q),
                     (bqr_s, bq_r), (bkr_s, bk_r), (buk_s, bu_k),
                     (bin_s, b_in), (pswap_s, pswap_in)]:
        t_smld = o("sync", lambda e, dst=dst, src=src: e.dma_start(
            out=dst[:, :], in_=src), "smld", 16)

    t_ones = o("vector", lambda e: e.memset(ones_row[:, :], 1.0), "sv")
    t_o2a = o("vector", lambda e: e.memset(ones2[:, :], 0.0), "sv")
    w("vector", t_o2a)
    o("vector", lambda e: e.memset(ones2[0:64, 0:1], 1.0), "sv")
    o("vector", lambda e: e.memset(ones2[64:128, 1:2], 1.0), "sv")
    t_o2b = o("vector", lambda e: e.memset(ones2b[:, :], 0.0), "sv")
    w("vector", t_o2b)
    o("vector", lambda e: e.memset(ones2b[0:64, 0:1], 1.0), "sv")
    t_ones2 = o("vector", lambda e: e.memset(ones2b[64:128, 1:2], 1.0), "sv")
    t_cs1 = o("vector", lambda e: e.memset(eps_b[:, :], EPS), "sv")
    t_cs2 = o("vector", lambda e: e.memset(cmag_b[:, :], CMAGIC), "sv")
    t_cs3 = o("vector", lambda e: e.memset(tiny_b[:, :], 1e-30), "sv")
    t_consts = [t_cs1, t_cs2, t_cs3]
    t_amz = o("vector", lambda e: e.memset(amax_r[:, :], 0.0), "sv")
    t_hmz = o("vector", lambda e: e.memset(hmax_r[:, :], 0.0), "sv")
    o("sync", lambda e: e.dma_start(out=idn_s[:, :], in_=idn_in), "smld", 16)
    t_idn = o("sync", lambda e: e.dma_start(out=idn32_s[:, :], in_=idn32_in), "smld", 16)
    t_small_loads = t_idn
    t_o21 = o("vector", lambda e: e.memset(ones21[:, :], 1.0), "sv")

    _ln_state = [{"t_n": None, "t_r2": None, "readers": []},
                 {"t_n": None, "t_r2": None, "readers": []}]
    _ln_ctr = [0]

    def ln_stats_from(src_ap, readers):
        g = _ln_ctr[0] % 2
        _ln_ctr[0] += 1
        stt = _ln_state[g]
        sl = lambda t: t[:, g:g + 1]
        lsc = ln_sc[:, g, :]
        prevs = [t for t in (stt["readers"] + list(readers)
                             + [stt["t_n"], stt["t_r2"]]) if t]
        ws("vector", prevs)
        t_s1 = o("vector", lambda e: e.tensor_scalar(
            lsc, src_ap, 0.0, 0.0, ALU.add, ALU.add, accum_out=sl(s_sum)), "sv")
        w("vector", t_s1)
        t_s2 = o("vector", lambda e: e.scalar_tensor_tensor(
            lsc, src_ap, 1.0, src_ap, ALU.bypass, ALU.mult,
            accum_out=sl(s_sq)), "sv")
        w("vector", t_s2)
        t_m = o("vector", lambda e: e.tensor_scalar(
            sl(s_mean), sl(s_sum), 1.0 / D_, None, ALU.mult), "sv")
        w("vector", t_m)
        t_v1 = o("vector", lambda e: e.scalar_tensor_tensor(
            sl(s_var), sl(s_mean), -1.0, sl(s_mean), ALU.mult, ALU.mult), "sv")
        w("vector", t_v1)
        t_v2 = o("vector", lambda e: e.tensor_scalar(
            sl(s_t1), sl(s_sq), 1.0 / D_, sl(s_var), ALU.mult, ALU.add), "sv")
        ws("scalar", prevs)
        w("scalar", t_v2)
        ws("scalar", t_consts)
        t_r1 = o("scalar", lambda e: e.activation(
            sl(s_var), sl(s_t1), AF.Ln, bias=eps_b[:, :], scale=1.0), "ss")
        w("scalar", t_r1)
        t_r2 = o("scalar", lambda e: e.activation(
            sl(s_rstd), sl(s_var), AF.Exp, bias=0.0, scale=-0.5), "ss")
        w("vector", t_r2)
        t_n = o("vector", lambda e: e.scalar_tensor_tensor(
            sl(s_nmr), sl(s_mean), -1.0, sl(s_rstd), ALU.mult, ALU.mult), "sv")
        stt["t_n"] = t_n
        stt["t_r2"] = t_r2
        stt["readers"] = []
        return (t_n, sl(s_rstd), sl(s_nmr), g)

    def ln_mark_readers(g, toks):
        _ln_state[g]["readers"] = [t for t in toks if t]

    stA = contextlib.ExitStack()
    sbA = mksb(stA)
    xa_t = sbA("xa_t", [128, 2, D_])
    ln_bf = sbA("ln_bf", [128, 2, D_], BF16)
    wst_f = sbA("wst_f", [128, 2, D_])
    wst_b = sbA("wst_b", [128, 2, D_], BF16)

    wq_h = sbA("wq_h", [128, 8, DFF // 2])
    M_h = sbA("M_h", [128, DFF // 2])
    A_t = sbA("A_t", [128, 32])
    at_sb = sbA("at_sb", [16, 128])
    srow_h = sbA("srow_h", [1, DFF // 2])
    Sbc_h = sbA("Sbc_h", [128, DFF // 2])
    wint_h = sbA("wint_h", [128, DFF // 2], BF16)
    wqo_f = sbA("wqo_f", [128, 2, D_])
    so_t = sbA("so_t", [128, 32])
    sor_t = sbA("sor_t", [128, 32])
    nCr_t = sbA("nCr_t", [128, 32])
    wob16 = sbA("wob16", [128, D_], BF16)
    w127 = sbA("w127", [128, 32])

    R = {}

    def gen_ln1():
        xa_reuse = [None, None]
        lnbf_reuse = [None, None]
        t_aff_prev = None
        t_st_ln = None
        for i in range(NKEY // 128):
            bi = i % 2
            w("sync", xa_reuse[bi])
            t_x = o("sync", lambda e, i=i, bi=bi: e.dma_start(
                out=xa_t[:, bi, :], in_=x_in[i * 128:(i + 1) * 128, :]), "xal%d" % bi, 16)
            w("vector", t_x)
            t_n, rstd_ap, nmr_ap, g = ln_stats_from(xa_t[:, bi, :], [t_aff_prev])
            ws("scalar", [t_n, lnbf_reuse[bi]])
            t_ap = o("scalar", lambda e, bi=bi, rstd_ap=rstd_ap, nmr_ap=nmr_ap: e.activation(
                ln_bf[:, bi, :], xa_t[:, bi, :], AF.Identity,
                bias=nmr_ap, scale=rstd_ap), "ss")
            ln_mark_readers(g, [t_ap])
            xa_reuse[bi] = t_ap
            t_aff_prev = t_ap
            w("gpsimd", t_ap)
            t_st_ln = o("gpsimd", lambda e, i=i, bi=bi: e.dma_start(
                out=ln1x_d[i * 128:(i + 1) * 128, :], in_=ln_bf[:, bi, :]), "lns%d" % bi, 16)
            lnbf_reuse[bi] = t_st_ln
            yield
        R["t_ln1_both"] = [lnbf_reuse[0], lnbf_reuse[1]]

    def gen_precast():
        st_tok = [None, None]
        cp_tok = [None, None]
        ci = 0
        tok = None
        for srcw, dst_d, nrc, ncols in [
                (wd_q, wdq_d, 8, 512), (wd_kv, wdkv_d, 8, 512),
                (wu_q, wuq_d, 4, 1024), (wq_r, wqr_d, 4, 1024),
                (wk_r, wkr_d, 8, 1024), (wu_k, wuk_d, 4, 1024),
                (wu_v, wuv_d, 4, 1024), (wo_w, wo_d, 8, 1024)]:
            for rc in range(nrc):
                bi = ci % 2
                ci += 1
                w("sync", st_tok[bi])
                t_l = o("sync", lambda e, rc=rc, srcw=srcw, ncols=ncols, bi=bi: e.dma_start(
                    out=wst_f[:, bi, 0:ncols], in_=srcw[rc * 128:(rc + 1) * 128, :]),
                    "pcl%d" % bi, 16)
                ws("scalar", [t_l, st_tok[bi]])
                t_c = o("scalar", lambda e, ncols=ncols, bi=bi: e.activation(
                    wst_b[:, bi, 0:ncols], wst_f[:, bi, 0:ncols], AF.Identity,
                    bias=0.0, scale=1.0), "ss")
                cp_tok[bi] = t_c
                w("gpsimd", t_c)
                tok = o("gpsimd", lambda e, rc=rc, dst_d=dst_d, ncols=ncols, bi=bi: e.dma_start(
                    out=dst_d[rc, :, :], in_=wst_b[:, bi, 0:ncols]), "pcs%d" % bi, 16)
                st_tok[bi] = tok
                yield
        R["t_precast"] = tok

    def gen_wout():
        wqo_reuse = [None, None]
        t_sc_pr = None
        t_p2_prev = None
        for jc in range(32):
            bi = jc % 2
            w("sync", wqo_reuse[bi])
            t_l = o("sync", lambda e, jc=jc, bi=bi: e.dma_start(
                out=wqo_f[:, bi, :], in_=w_out[jc * 128:(jc + 1) * 128, :]), "wqo%d" % bi, 16)
            ws("vector", [t_l, t_sc_pr])
            t_am = o("vector", lambda e, jc=jc, bi=bi: e.tensor_reduce(
                so_t[:, jc:jc + 1], wqo_f[:, bi, :], AX.X, ALU.max,
                apply_absolute_value=True), "sv")
            w("vector", t_am)
            t_oc = o("vector", lambda e, jc=jc: e.tensor_scalar(
                so_t[:, jc:jc + 1], so_t[:, jc:jc + 1], 1e-8, None, ALU.max), "sv")
            w("vector", t_oc)
            t_or = o("vector", lambda e, jc=jc: e.tensor_scalar(
                sor_t[:, jc:jc + 1], so_t[:, jc:jc + 1], 1.0 / 127.0, None, ALU.mult), "sv")
            t_o127 = o("vector", lambda e, jc=jc: e.reciprocal(
                w127[:, jc:jc + 1], so_t[:, jc:jc + 1]), "sv")
            ws("vector", [t_or, t_o127])
            t_ncr = o("vector", lambda e, jc=jc: e.tensor_scalar(
                nCr_t[:, jc:jc + 1], sor_t[:, jc:jc + 1], -CMAGIC, None, ALU.mult), "sv")
            t_osd = o("vector", lambda e, jc=jc: e.tensor_scalar(
                so_t[:, jc:jc + 1], w127[:, jc:jc + 1], 127.0, None, ALU.mult), "sv")
            t_sc_pr = t_osd
            ws("scalar", [t_osd, t_ncr, t_consts[1]])
            t_p1 = o("scalar", lambda e, jc=jc, bi=bi: e.activation(
                wqo_f[:, bi, :], wqo_f[:, bi, :], AF.Identity,
                bias=cmag_b[:, :], scale=so_t[:, jc:jc + 1]), "ss")
            ws("scalar", [t_p1, t_p2_prev])
            t_p2 = o("scalar", lambda e, jc=jc, bi=bi: e.activation(
                wob16[:, :], wqo_f[:, bi, :], AF.Identity,
                bias=nCr_t[:, jc:jc + 1], scale=sor_t[:, jc:jc + 1]), "ss")
            wqo_reuse[bi] = t_p2
            w("gpsimd", t_p2)
            t_p2_prev = o("gpsimd", lambda e, jc=jc: e.dma_start(
                out=wout_d[jc, :, :], in_=wob16[:, :]), "wts", 16)
            yield

    def gen_win():
        wq_reuse = [None] * 8
        wi_prev = [None, None]
        t_sbc_read = [None]
        t_fold_done = [None]
        t_ac = None
        for half in range(2):
            c0 = half * 2048
            mprev = None
            for fc in range(8):
                w("sync", wq_reuse[fc])
                t_l = o("sync", lambda e, fc=fc, c0=c0: e.dma_start(
                    out=wq_h[:, fc, :], in_=w_in[fc * 128:(fc + 1) * 128, c0:c0 + 2048]),
                    "wql%d" % fc, 16)
                if fc == 0:
                    ws("scalar", [t_l, mprev, t_fold_done[0]])
                    mprev = o("scalar", lambda e, fc=fc: e.activation(
                        M_h[:, :], wq_h[:, fc, :], AF.Abs, bias=0.0, scale=1.0), "ss")
                else:
                    ws("scalar", [t_l, t_sbc_read[0]])
                    t_abs = o("scalar", lambda e, fc=fc: e.activation(
                        Sbc_h[:, :], wq_h[:, fc, :], AF.Abs, bias=0.0, scale=1.0), "ss")
                    ws("vector", [t_abs, mprev])
                    mprev = o("vector", lambda e: e.tensor_tensor(
                        M_h[:, :], M_h[:, :], Sbc_h[:, :], ALU.max), "sv")
                    t_sbc_read[0] = mprev
                yield
            slots = [ps3[:, 0:128], ps3[:, 128:256], ps4[:, 0:128], ps4[:, 128:256]]
            slot_tok = [psum_last.get("ps3"), psum_last.get("ps3"),
                        psum_last.get("ps4"), psum_last.get("ps4")]
            t_red = None
            for c in range(16):
                sl_ = c % 4
                ws("tensor", [mprev, t_idn, slot_tok[sl_]])
                t_tr = o("tensor", lambda e, c=c, sl_=sl_: e.transpose(
                    slots[sl_], M_h[:, c * 128:(c + 1) * 128], idn32_s[:, :]), "st")
                w("vector", t_tr)
                t_red = o("vector", lambda e, c=c, half=half, sl_=sl_: e.tensor_reduce(
                    A_t[:, half * 16 + c: half * 16 + c + 1], slots[sl_], AX.X, ALU.max), "sv")
                slot_tok[sl_] = t_red
                if c % 4 == 3:
                    yield
            t_fold_done[0] = t_red
            psum_last["ps3"] = t_red
            psum_last["ps4"] = t_red
            w("vector", t_red)
            t_ac = o("vector", lambda e, half=half: e.tensor_scalar(
                A_t[:, half * 16:(half + 1) * 16], A_t[:, half * 16:(half + 1) * 16],
                1e-8, None, ALU.max), "sv")
            ws("tensor", [t_ac, psum_last.get("ps3")])
            t_t2 = o("tensor", lambda e, half=half: e.transpose(
                ps3[0:16, 0:128], A_t[:, half * 16:(half + 1) * 16], idn32_s[:, :]), "st")
            w("vector", t_t2)
            t_cpt = o("vector", lambda e: e.tensor_copy(at_sb[:, :], ps3[0:16, 0:128]), "sv")
            psum_last["ps3"] = t_cpt
            w("gpsimd", t_cpt)
            t_fl1 = o("gpsimd", lambda e: e.dma_start(
                out=flip_d[0:1, 0:2048].rearrange("o (a b) -> (o a) b", b=128),
                in_=at_sb[:, :]), "flst", 16)
            w("sync", t_fl1)
            t_fl2 = o("sync", lambda e: e.dma_start(
                out=srow_h[:, :], in_=flip_d[0:1, 0:2048]), "flrd", 16)
            w("vector", t_fl2)
            t_sr1 = o("vector", lambda e: e.reciprocal(srow_h[:, :], srow_h[:, :]), "sv")
            w("vector", t_sr1)
            t_sr2 = o("vector", lambda e: e.tensor_scalar(
                srow_h[:, :], srow_h[:, :], 127.0, None, ALU.mult), "sv")
            yield
            t_sbc = None
            for q in range(4):
                ws("tensor", [t_sr2, t_ones, psum_last.get("ps4")])
                t_mmq = o("tensor", lambda e, q=q: e.matmul(
                    ps4[:, 0:512], ones_row[:, :], srow_h[:, q * 512:(q + 1) * 512],
                    start=True, stop=True), "st")
                w("vector", t_mmq)
                t_sbc = o("vector", lambda e, q=q: e.tensor_copy(
                    Sbc_h[:, q * 512:(q + 1) * 512], ps4[:, 0:512]), "sv")
                psum_last["ps4"] = t_sbc
            yield
            t_wi_last = None
            for fc in range(8):
                ws("vector", [t_sbc, wi_prev[fc % 2]])
                t_m1 = o("vector", lambda e, fc=fc: e.tensor_tensor(
                    wq_h[:, fc, :], wq_h[:, fc, :], Sbc_h[:, :], ALU.mult), "sv")
                t_sbc_read[0] = t_m1
                ws("vector", [t_m1, wi_prev[1 - fc % 2]])
                t_m2 = o("vector", lambda e, fc=fc: e.tensor_scalar(
                    wint_h[:, :], wq_h[:, fc, :], CMAGIC, CMAGIC, ALU.add, ALU.subtract), "sv")
                wq_reuse[fc] = t_m2
                w("gpsimd", t_m2)
                t_wi_last = o("gpsimd", lambda e, fc=fc, half=half: e.dma_start(
                    out=wint_d[half * 16:(half + 1) * 16, :, fc, :].rearrange("j p c -> p j c"),
                    in_=wint_h[:, :].rearrange("p (j c) -> p j c", c=128)), "wis", 16)
                wi_prev[fc % 2] = t_wi_last
                yield
        w("vector", t_ac)
        t_sjr = o("vector", lambda e: e.tensor_scalar(
            sjr_p[:, :], A_t[:, :], 1.0 / 127.0, None, ALU.mult), "sv")
        R["t_sjr_done"] = t_sjr

    def drive(gens):
        gens = list(gens)
        while gens:
            nxt = []
            for gobj in gens:
                try:
                    next(gobj)
                    nxt.append(gobj)
                except StopIteration:
                    pass
            gens = nxt

    drive([gen_ln1(), gen_precast(), gen_wout(), gen_win()])
    t_ln1_both = R["t_ln1_both"]
    t_precast = R["t_precast"]
    t_sjr_done = R["t_sjr_done"]

    kb.join()
    stA.close()

    stKV = contextlib.ExitStack()
    sbKV = mksb(stKV)
    K_fm = sbKV("K_fm", [128, NPAIR, NKEY], BF16)
    V_tm = sbKV("V_tm", [128, NKC, NH * 65], BF16)
    t_vones = o("vector", lambda e: e.memset(
        V_tm[:, :, :].rearrange("p k (h v) -> p k h v", v=65)[:, :, :, 64:65], 1.0), "sv")

    stB = contextlib.ExitStack()
    sbB = mksb(stB)
    lx_t = sbB("lx_t", [128, 2, 8, 512], BF16)
    cs_t = sbB("cs_t", [128, 2, 2, 512], BF16)
    rp_t0 = sbB("rp_t0", [128, 2, 512], BF16)
    rp_t1 = sbB("rp_t1", [128, 2, 512], BF16)
    rp_t2 = sbB("rp_t2", [128, 2, 512], BF16)
    kc_bf = sbB("kc_bf", [128, 2, 512], BF16)
    sq_t = sbB("sq_t", [128, 2, 512], BF16)
    stB1 = contextlib.ExitStack()
    sbB1 = mksb(stB1)
    ckv_t = sbB1("ckv_t", [128, 4, 512], BF16)
    wdkv_bf = sbB1("wdkv_bf", [128, 8, 512], BF16)
    wuv_bf = sbB1("wuv_bf", [128, 4, 1024], BF16)
    wuk_bf = sbB1("wuk_bf", [128, 4, 1024], BF16)
    wkrp = sbB1("wkrp", [128, 2, 8, 128], BF16)
    vB_bc = sbB1("vB_bc", [128, D_], BF16)
    row_b = sbB1("row_b", [1, 512])

    vb_toks = []
    t_rb = None
    for h in range(2):
        w("sync", t_rb)
        t_rb = o("sync", lambda e, h=h: e.dma_start(
            out=row_b[:, :], in_=bu_v_row[0:1, h * 512:(h + 1) * 512]), "rowb", 16)
        ws("tensor", [t_rb, t_ones, psum_last.get("ps4")])
        vb_toks.append(o("tensor", lambda e: e.matmul(
            ps4[:, 0:512], ones_row[:, :], row_b[:, :],
            start=True, stop=True), "st"))
        w("vector", vb_toks[-1])
        t_vB = o("vector", lambda e, h=h: e.tensor_copy(
            vB_bc[:, h * 512:(h + 1) * 512], ps4[:, 0:512]), "sv")
        psum_last["ps4"] = t_vB
        t_rb_tok = vb_toks[-1]
        t_rb = t_rb if False else vb_toks[-1]

    def loadw(dst, src_d, nrc, ncols, prev, sem="wlb1"):
        tok = prev
        for rc in range(nrc):
            tok = o("sync", lambda e, rc=rc: e.dma_start(
                out=dst[:, rc, :], in_=src_d[rc, :, 0:ncols]), sem, 16)
        return tok

    ws("sync", [t_precast])
    t_wb = loadw(wdkv_bf, wdkv_d, 8, 512, None)
    t_wb = loadw(wuv_bf, wuv_d, 4, 1024, t_wb)
    t_wb = loadw(wuk_bf, wuk_d, 4, 1024, t_wb)
    t_wB1 = t_wb

    lx_reuse = None
    cs_reuse = [None, None]
    wkr_reuse = [None, None]
    rope_gate = [None, None]
    ckv_reuse = None
    t_last_rope = None
    t_last_vev = None
    lx_read = [None, None]
    u4_tok = {}
    kn_i = [0]
    kn_sqd = []
    kn_mm_prev = [None]
    kn_rd_toks = []

    def pump_kn():
        if kn_sqd:
            p0, s80, t_sq0 = kn_sqd.pop(0)
            ws("tensor", [t_sq0, psum_last.get(("kr", 0)), t_ones2])
            t_mmn = o("tensor", lambda e: e.matmul(
                ps4[0:2, 0:512], ones2b[:, :], sq_t[:, 1, :], start=True, stop=True), "st")
            kn_mm_prev[0] = t_mmn
            psum_last[("kr", 0)] = t_mmn
            w("vector", t_mmn)
            t_rdn = o("vector", lambda e, p0=p0, s80=s80: e.tensor_reduce(
                knall[:, p0 * 8 + s80:p0 * 8 + s80 + 1], ps4[0:2, 0:512], AX.X, ALU.max), "sv")
            psum_last[("kr", 0)] = t_rdn
            kn_rd_toks.append(t_rdn)
        j = kn_i[0]
        if j < 64:
            s8j, pj = j // 8, j % 8
            if (pj, s8j) in u4_tok:
                ws("vector", [u4_tok[(pj, s8j)], kn_mm_prev[0]])
                t_sqn = o("vector", lambda e, pj=pj, s8j=s8j: e.scalar_tensor_tensor(
                    sq_t[:, 1, :], K_fm[:, pj, s8j * 512:(s8j + 1) * 512], 1.0,
                    K_fm[:, pj, s8j * 512:(s8j + 1) * 512], ALU.bypass, ALU.mult), "sv")
                kn_sqd.append((pj, s8j, t_sqn))
                kn_i[0] += 1

    for kt in range(NKT):
        lbi = kt % 2
        ws("sync", t_ln1_both + [lx_read[lbi]])
        ttr = None
        for fc in range(8):
            ttr = o("sync", lambda e, kt=kt, fc=fc, lbi=lbi: e.dma_start_transpose(
                lx_t[:, lbi, fc, :],
                ln1x_d[kt * 512:(kt + 1) * 512, fc * 128:(fc + 1) * 128]), "lxt%d" % lbi, 16)
        t_lx = ttr
        ckv_toks = []
        for cc in range(4):
            psl = [psA[:, 0:512], psA[:, 512:1024], psB[:, 0:512], psB[:, 512:1024]][cc]
            deps = [t_lx, t_wB1, psum_last.get(("pAB", cc)), ckv_reuse]
            if cc >= 2:
                deps += [psum_last.get(("kc", 1)), psum_last.get(("kr", 1)),
                         psum_last.get(("sw", 1))]
            ws("tensor", deps)
            tokm = None
            for fc in range(8):
                def mmf(e, cc=cc, fc=fc, psl=psl, lbi=lbi):
                    return e.matmul(psl, wdkv_bf[:, fc, cc * 128:(cc + 1) * 128],
                                    lx_t[:, lbi, fc, :], start=(fc == 0), stop=(fc == 7))
                tokm = o("tensor", mmf, "st") if fc == 7 else o("tensor", mmf)
            w("scalar", tokm)
            t_ev = o("scalar", lambda e, cc=cc, psl=psl: e.activation(
                ckv_t[:, cc, :], psl, AF.Identity, bias=bdkv_s[:, cc:cc + 1], scale=1.0), "ss")
            psum_last[("pAB", cc)] = t_ev
            ckv_toks.append(t_ev)
        for tc in range(4):
            kcix = kt * 4 + tc
            for h2 in range(2):
                psl = ps1[:, 0:512] if h2 == 0 else ps2[:, 0:512]
                ws("tensor", ckv_toks + [psum_last.get(("pV", h2)),
                                         psum_last.get(("sw", h2))])
                tokm = None
                for cc in range(4):
                    def mmv(e, cc=cc, tc=tc, h2=h2, psl=psl):
                        return e.matmul(psl, ckv_t[:, cc, tc * 128:(tc + 1) * 128],
                                        wuv_bf[:, cc, h2 * 512:(h2 + 1) * 512],
                                        start=(cc == 0), stop=(cc == 3))
                    tokm = o("tensor", mmv, "st") if cc == 3 else o("tensor", mmv)
                ws("vector", [tokm, t_vB, t_vones])
                vv = V_tm[:, kcix, :].rearrange("p (h v) -> p h v", v=65)
                t_vev = o("vector", lambda e, h2=h2, psl=psl, vv=vv: e.tensor_tensor(
                    vv[:, h2 * 8:(h2 + 1) * 8, 0:64],
                    psl.rearrange("p (h v) -> p h v", v=64),
                    vB_bc[:, h2 * 512:(h2 + 1) * 512].rearrange("p (h v) -> p h v", v=64),
                    ALU.add), "sv")
                psum_last[("pV", h2)] = t_vev
                t_last_vev = t_vev
        for p in range(NPAIR):
            cbi = p % 2
            if kt >= 1:
                pump_kn()
            KCP = ps3[:, 0:512] if cbi == 0 else psB[:, 0:512]
            KRP = ps4[:, 0:512] if cbi == 0 else psB[:, 512:1024]
            SWP = ps1[:, 0:512] if cbi == 0 else ps2[:, 0:512]
            w("sync", wkr_reuse[cbi])
            t_wk = o("sync", lambda e, p=p, cbi=cbi: e.dma_start(
                out=wkrp[:, cbi, :, :],
                in_=wkr_d[:, :, p * 128:(p + 1) * 128]), "wkl%d" % cbi, 16)
            w("sync", cs_reuse[cbi])
            t_c1 = o("sync", lambda e, p=p, kt=kt, cbi=cbi: e.dma_start(
                out=cs_t[:, cbi, 0, :],
                in_=cos_d[p * 128:(p + 1) * 128, kt * 512:(kt + 1) * 512]), "csl%d" % cbi, 16)
            t_c2 = o("sync", lambda e, p=p, kt=kt, cbi=cbi: e.dma_start(
                out=cs_t[:, cbi, 1, :],
                in_=sin_d[p * 128:(p + 1) * 128, kt * 512:(kt + 1) * 512]), "csl%d" % cbi, 16)
            ws("tensor", ckv_toks + [psum_last.get(("kc", cbi))])
            tok_kc = None
            for cc in range(4):
                def mmk(e, cc=cc, p=p, KCP=KCP):
                    return e.matmul(KCP, wuk_bf[:, cc, p * 128:(p + 1) * 128],
                                    ckv_t[:, cc, :], start=(cc == 0), stop=(cc == 3))
                tok_kc = o("tensor", mmk, "st") if cc == 3 else o("tensor", mmk)
            ws("scalar", [tok_kc, rope_gate[cbi]])
            t_kc = o("scalar", lambda e, p=p, cbi=cbi, KCP=KCP: e.activation(
                kc_bf[:, cbi, :], KCP, AF.Identity, bias=buk_s[:, p:p + 1], scale=1.0), "ss")
            psum_last[("kc", cbi)] = t_kc
            ws("tensor", [t_lx, t_wk, psum_last.get(("kr", cbi))])
            tok_kr = None
            for fc in range(8):
                def mmr(e, fc=fc, cbi=cbi, KRP=KRP, lbi=lbi):
                    return e.matmul(KRP, wkrp[:, cbi, fc, :],
                                    lx_t[:, lbi, fc, :], start=(fc == 0), stop=(fc == 7))
                tok_kr = o("tensor", mmr, "st") if fc == 7 else o("tensor", mmr)
            wkr_reuse[cbi] = tok_kr
            ws("scalar", [tok_kr, rope_gate[cbi]])
            t_kr = o("scalar", lambda e, p=p, cbi=cbi, KRP=KRP: e.activation(
                rp_t0[:, cbi, :], KRP, AF.Identity, bias=bkr_s[:, p:p + 1], scale=1.0), "ss")
            psum_last[("kr", cbi)] = t_kr
            ws("tensor", [t_kr, psum_last.get(("sw", cbi)),
                          psum_last.get(("pV", cbi)), t_small_loads])
            tok_sw = o("tensor", lambda e, cbi=cbi, SWP=SWP: e.matmul(
                SWP, pswap_s[:, :], rp_t0[:, cbi, :], start=True, stop=True), "st")
            ws("vector", [t_kr, tok_sw, t_c1, t_c2, rope_gate[cbi]])
            t_u1 = o("vector", lambda e, cbi=cbi: e.tensor_tensor(
                rp_t1[:, cbi, :], rp_t0[:, cbi, :], cs_t[:, cbi, 0, :], ALU.mult), "sv")
            t_u2 = o("vector", lambda e, cbi=cbi, SWP=SWP: e.tensor_tensor(
                rp_t2[:, cbi, :], SWP, cs_t[:, cbi, 1, :], ALU.mult), "sv")
            psum_last[("sw", cbi)] = t_u2
            ws("vector", [t_u1, t_u2])
            t_u3 = o("vector", lambda e, cbi=cbi: e.tensor_tensor(
                rp_t1[:, cbi, :], rp_t1[:, cbi, :], rp_t2[:, cbi, :], ALU.add), "sv")
            ws("vector", [t_u3, t_kc])
            t_u4 = o("vector", lambda e, p=p, kt=kt, cbi=cbi: e.tensor_tensor(
                K_fm[:, p, kt * 512:(kt + 1) * 512], rp_t1[:, cbi, :], kc_bf[:, cbi, :], ALU.add), "sv")
            cs_reuse[cbi] = t_u2
            rope_gate[cbi] = t_u4
            t_last_rope = t_u4
            u4_tok[(p, kt)] = t_u4
            if p == NPAIR - 1:
                lx_read[lbi] = tok_kr
        ckv_reuse = t_u4
    t_K_done = t_u4
    t_V_done = t_last_vev


    kb.join()
    stB1.close()
    ctq_t = sbB("ctq_t", [128, 4, 2, 512], BF16)
    wdq_bf = sbB("wdq_bf", [128, 8, 512], BF16)
    wuq_bf = sbB("wuq_bf", [128, 4, 1024], BF16)
    wqr_bf = sbB("wqr_bf", [128, 4, 1024], BF16)
    q_bf = sbB("q_bf", [128, 2, 512], BF16)

    t_wb2 = loadw(wdq_bf, wdq_d, 8, 512, t_wB1, "wlb2")
    t_wb2 = loadw(wuq_bf, wuq_d, 4, 1024, t_wb2, "wlb2")
    t_wb2 = loadw(wqr_bf, wqr_d, 4, 1024, t_wb2, "wlb2")

    t_qspill = None
    qsp_gate = [None, None]
    qsp_last = [None, None]
    pend_qn = [None]
    nsq_gate = [None, None]
    ct_read = [None, None]

    def emit_qnorm(p, qt, cbi, t_u4tok):
        ws("vector", [t_u4tok, t_qn_pr[0]])
        t_sq2 = o("vector", lambda e, cbi=cbi: e.scalar_tensor_tensor(
            sq_t[:, 0, :], q_bf[:, cbi, :], 1.0, q_bf[:, cbi, :], ALU.bypass, ALU.mult), "sv")
        nsq_gate[cbi] = t_sq2
        ws("tensor", [t_sq2, psum_last.get(("kc", 0))])
        t_mm = o("tensor", lambda e: e.matmul(
            ps3[0:2, 0:512], ones2b[:, :], sq_t[:, 0, :], start=True, stop=True), "st")
        psum_last[("kc", 0)] = t_mm
        w("vector", t_mm)
        t_rd = o("vector", lambda e, p=p, qt=qt: e.tensor_reduce(
            qn2[:, p * 4 + qt:p * 4 + qt + 1], ps3[0:2, 0:512], AX.X, ALU.max), "sv")
        psum_last[("kc", 0)] = t_rd
        t_qn_pr[0] = t_rd

    t_qn_pr = [None]
    for qt in range(NQT):
        lbi = (NKT + qt) % 2
        ws("sync", t_ln1_both + [lx_read[lbi]])
        ttr = None
        for fc in range(8):
            ttr = o("sync", lambda e, qt=qt, fc=fc, lbi=lbi: e.dma_start_transpose(
                lx_t[:, lbi, fc, :],
                ln1x_d[qt * 512:(qt + 1) * 512, fc * 128:(fc + 1) * 128]), "lxt%d" % lbi, 16)
        t_lx = ttr
        ctq_toks = []
        for cc in range(4):
            psl = [psA[:, 0:512], psA[:, 512:1024], psB[:, 0:512], psB[:, 512:1024]][cc]
            deps = [t_lx, t_wb2, psum_last.get(("pAB", cc))]
            if cc >= 2:
                deps += [psum_last.get(("kc", 1)), psum_last.get(("kr", 1)),
                         psum_last.get(("sw", 1))]
            ws("tensor", deps)
            tokm = None
            for fc in range(8):
                def mmf(e, cc=cc, fc=fc, psl=psl, lbi=lbi):
                    return e.matmul(psl, wdq_bf[:, fc, cc * 128:(cc + 1) * 128],
                                    lx_t[:, lbi, fc, :], start=(fc == 0), stop=(fc == 7))
                tokm = o("tensor", mmf, "st") if fc == 7 else o("tensor", mmf)
            ws("scalar", [tokm, ct_read[qt % 2]])
            t_ev = o("scalar", lambda e, cc=cc, qt=qt, psl=psl: e.activation(
                ctq_t[:, cc, qt % 2, :], psl, AF.Identity,
                bias=bdq_s[:, cc:cc + 1], scale=1.0), "ss")
            psum_last[("pAB", cc)] = t_ev
            ctq_toks.append(t_ev)
        lq_reuse = t_ev
        for p in range(NPAIR):
            cbi = p % 2
            QCP = ps3[:, 0:512] if cbi == 0 else psB[:, 0:512]
            QRP = ps4[:, 0:512] if cbi == 0 else psB[:, 512:1024]
            SWP = ps1[:, 0:512] if cbi == 0 else ps2[:, 0:512]
            if qt < 2:
                pump_kn()
            if pend_qn[0] is not None:
                emit_qnorm(*pend_qn[0])
                pend_qn[0] = None
            w("sync", cs_reuse[cbi])
            t_c1 = o("sync", lambda e, p=p, qt=qt, cbi=cbi: e.dma_start(
                out=cs_t[:, cbi, 0, :],
                in_=cos_d[p * 128:(p + 1) * 128, qt * 512:(qt + 1) * 512]), "csl%d" % cbi, 16)
            t_c2 = o("sync", lambda e, p=p, qt=qt, cbi=cbi: e.dma_start(
                out=cs_t[:, cbi, 1, :],
                in_=sin_d[p * 128:(p + 1) * 128, qt * 512:(qt + 1) * 512]), "csl%d" % cbi, 16)
            ws("tensor", ctq_toks + [psum_last.get(("kc", cbi))])
            tok_qc = None
            for cc in range(4):
                def mmk(e, cc=cc, p=p, qt=qt, QCP=QCP):
                    return e.matmul(QCP, wuq_bf[:, cc, p * 128:(p + 1) * 128],
                                    ctq_t[:, cc, qt % 2, :],
                                    start=(cc == 0), stop=(cc == 3))
                tok_qc = o("tensor", mmk, "st") if cc == 3 else o("tensor", mmk)
            ws("scalar", [tok_qc, rope_gate[cbi]])
            t_qc = o("scalar", lambda e, p=p, cbi=cbi, QCP=QCP: e.activation(
                kc_bf[:, cbi, :], QCP, AF.Identity, bias=buq_s[:, p:p + 1], scale=1.0), "ss")
            psum_last[("kc", cbi)] = t_qc
            ws("tensor", ctq_toks + [psum_last.get(("kr", cbi))])
            tok_qr = None
            for cc in range(4):
                def mmr(e, cc=cc, p=p, qt=qt, QRP=QRP):
                    return e.matmul(QRP, wqr_bf[:, cc, p * 128:(p + 1) * 128],
                                    ctq_t[:, cc, qt % 2, :],
                                    start=(cc == 0), stop=(cc == 3))
                tok_qr = o("tensor", mmr, "st") if cc == 3 else o("tensor", mmr)
            ws("scalar", [tok_qr, rope_gate[cbi]])
            t_qr = o("scalar", lambda e, p=p, cbi=cbi, QRP=QRP: e.activation(
                rp_t0[:, cbi, :], QRP, AF.Identity, bias=bqr_s[:, p:p + 1], scale=1.0), "ss")
            psum_last[("kr", cbi)] = t_qr
            ws("tensor", [t_qr, t_qc, psum_last.get(("sw", cbi))])
            tok_sw = o("tensor", lambda e, cbi=cbi, SWP=SWP: e.matmul(
                SWP, pswap_s[:, :], rp_t0[:, cbi, :], start=True, stop=True), "st")
            ws("vector", [t_qr, tok_sw, t_c1, t_c2, rope_gate[cbi]])
            t_u1 = o("vector", lambda e, cbi=cbi: e.tensor_tensor(
                rp_t1[:, cbi, :], rp_t0[:, cbi, :], cs_t[:, cbi, 0, :], ALU.mult), "sv")
            t_u2 = o("vector", lambda e, cbi=cbi, SWP=SWP: e.tensor_tensor(
                rp_t2[:, cbi, :], SWP, cs_t[:, cbi, 1, :], ALU.mult), "sv")
            psum_last[("sw", cbi)] = t_u2
            ws("vector", [t_u1, t_u2])
            t_u3 = o("vector", lambda e, cbi=cbi: e.tensor_tensor(
                rp_t1[:, cbi, :], rp_t1[:, cbi, :], rp_t2[:, cbi, :], ALU.add), "sv")
            ws("vector", [t_u3, t_qc, qsp_gate[cbi], nsq_gate[cbi]])
            t_u4 = o("vector", lambda e, cbi=cbi: e.tensor_tensor(
                q_bf[:, cbi, :], rp_t1[:, cbi, :], kc_bf[:, cbi, :], ALU.add), "sv")
            cs_reuse[cbi] = t_u2
            rope_gate[cbi] = t_u4
            if p == NPAIR - 1:
                lx_read[lbi] = tok_qr
                ct_read[qt % 2] = tok_qr
            w("gpsimd", t_u4)
            t_qspill = o("gpsimd", lambda e, p=p, qt=qt, cbi=cbi: e.dma_start(
                out=q_d[p, :, qt * 512:(qt + 1) * 512], in_=q_bf[:, cbi, :]), "qsp%d" % cbi, 16)
            qsp_gate[cbi] = t_qspill
            qsp_last[cbi] = t_qspill
            pend_qn[0] = (p, qt, cbi, t_u4)
    emit_qnorm(*pend_qn[0])
    pend_qn[0] = None
    while kn_sqd or kn_i[0] < 64:
        pump_kn()
    t_kn = None
    ws("vector", kn_rd_toks)
    for p in range(NPAIR):
        t_kn = o("vector", lambda e, p=p: e.tensor_reduce(
            kn2[:, p:p + 1], knall[:, p * 8:(p + 1) * 8], AX.X, ALU.max), "sv")
    t_knorm = t_kn
    t_rd = t_qn_pr[0]
    t_Q_done = [qsp_last[0], qsp_last[1]]
    t_qnorm = t_rd

    ws("scalar", [t_qnorm, t_knorm])
    ws("scalar", t_consts)
    t_b1 = o("scalar", lambda e: e.activation(qn2[:, :], qn2[:, :], AF.Ln, bias=tiny_b[0:2, :], scale=1.0), "ss")
    w("scalar", t_b1)
    t_b2 = o("scalar", lambda e: e.activation(qn2[:, :], qn2[:, :], AF.Exp, bias=0.0, scale=0.5), "ss")
    w("scalar", t_b2)
    t_b3 = o("scalar", lambda e: e.activation(kn2[:, :], kn2[:, :], AF.Ln, bias=tiny_b[0:2, :], scale=1.0), "ss")
    w("scalar", t_b3)
    t_b4 = o("scalar", lambda e: e.activation(kn2[:, :], kn2[:, :], AF.Exp, bias=0.0, scale=0.5), "ss")
    kn4v = kn4[:, :].rearrange("p (q m) -> p q m", m=4)
    w("vector", t_b4)
    t_b5 = None
    for mq in range(4):
        w("vector", t_b5)
        t_b5 = o("vector", lambda e, mq=mq, kn4v=kn4v: e.tensor_copy(
            kn4v[:, :, mq], kn2[:, :]), "sv")
    ws("vector", [t_b5, t_b2])
    t_b6 = o("vector", lambda e: e.tensor_tensor(cc_sc[:, :], qn2[:, :], kn4[:, :], ALU.mult), "sv")
    ws("tensor", [t_b6, t_o21, psum_last.get("ps4")])
    t_b7 = o("tensor", lambda e: e.matmul(
        ps4[0:1, 0:32], ones21[:, :], cc_sc[:, :], start=True, stop=True), "st")
    w("vector", t_b7)
    t_b8 = o("vector", lambda e: e.tensor_scalar(
        negc[:, :], ps4[0:1, 0:32], -INV_SCALE * 1.0001, None, ALU.mult), "sv")
    psum_last["ps4"] = t_b8
    ws("tensor", [t_b8, psum_last.get("ps3")])
    t_b9 = o("tensor", lambda e: e.matmul(
        ps3[:, 0:32], ones_row[:, :], negc[:, :], start=True, stop=True), "st")
    w("vector", t_b9)
    t_bias_c = o("vector", lambda e: e.tensor_copy(bias_c[:, :], ps3[:, 0:32]), "sv")
    psum_last["ps3"] = t_bias_c

    kb.join()
    stB.close()

    stC = contextlib.ExitStack()
    sbC = mksb(stC)
    O_fm = sbC("O_fm", [128, 2, 8, QMAC], BF16)
    P_pr = sbC("P_pr", [128, 2, 1024], BF16)
    Qm = sbC("Qm", [128, NPAIR, QMAC], BF16)
    Oraw = sbC("Oraw", [128, 2, 512])
    wo_bf = sbC("wo_bf", [128, 8, D_], BF16)
    woB_bc = sbC("woB_bc", [128, D_])
    w_sb = sbC("w_sb", [128, D_])
    xr_t = sbC("xr_t", [128, D_])
    taff = sbC("taff", [128, D_])
    r_sb = sbC("r_sb", [128, 512])

    t_wol = loadw(wo_bf, wo_d, 8, 1024, None, "wlc")
    t_rb2 = o("sync", lambda e: e.dma_start(out=xr_t[0:1, 0:1024], in_=bo_row), "rowb", 16)
    t_woB = None
    for h in range(2):
        ws("tensor", [t_rb2, t_ones, psum_last.get("ps4")])
        t_mmb = o("tensor", lambda e, h=h: e.matmul(
            ps4[:, 0:512], ones_row[:, :], xr_t[0:1, h * 512:(h + 1) * 512],
            start=True, stop=True), "st")
        w("vector", t_mmb)
        t_woB = o("vector", lambda e, h=h: e.tensor_copy(
            woB_bc[:, h * 512:(h + 1) * 512], ps4[:, 0:512]), "sv")
        psum_last["ps4"] = t_woB

    if dbg:
        dbgw_d = dint("dbgw_d", [NTT, 128, D_])
        dbgo_d = dint("dbgo_d", [NMACRO, 128, 8, QMAC], BF16)
        dbgp_d = dint("dbgp_d", [128, 1024], BF16)
        dbgs_d = dint("dbgs_d", [NMACRO, NPAIR, 2, 512])
    s_toks = {0: None, 1: None}
    p_toks = {0: None, 1: None}
    av_read = {0: None, 1: None}
    oraw_gate = {0: None, 1: None}
    qm_pair = [None] * NPAIR
    CS = {"t_r_ps": None, "t_wo_ps": t_woB, "t_wo_ps3": None, "o_read": [None, None],
          "t_af": None, "xr_gate": [t_woB], "t_x1s": None, "t_am2": None}
    evac_toks = {m: [] for m in range(NMACRO)}

    def load_qm_pair(mm, p, gate):
        ws("sync", t_Q_done + [gate])
        qm_pair[p] = o("sync", lambda e, p=p, mm=mm: e.dma_start(
            out=Qm[:, p, :], in_=q_d[p, :, mm * 512:(mm + 1) * 512]), "qml%d" % p, 16)

    def emit_norm(m, p, t_cps):
        om = m % 2
        for j in range(2):
            t_cp = t_cps[j]
            if dbg:
                w("vector", t_cp)
                t_dsum = o("vector", lambda e, j=j: e.tensor_copy(
                    r_sb[0:1, 0:512], Oraw[64:65, j, :]), "sv")
                w("gpsimd", t_dsum)
                o("gpsimd", lambda e, m=m, p=p, j=j: e.dma_start(
                    out=dbgs_d[m, p, j, :].rearrange("(o f) -> o f", o=1),
                    in_=r_sb[0:1, 0:512]), "dbgs", 16)
                ws("vector", [t_dsum])
            ws("vector", [t_cp, CS["t_r_ps"], CS.get("t_oe_prev")])
            t_rc = o("vector", lambda e, j=j: e.reciprocal(
                r_sb[0:1, :], Oraw[64:65, j, :]), "sv")
            ws("tensor", [t_rc, psum_last.get("ps3")])
            t_ro = o("tensor", lambda e: e.matmul(
                ps3[:, 0:512], ones_row[:, :], r_sb[0:1, :],
                start=True, stop=True), "st")
            w("vector", t_ro)
            t_rcp = o("vector", lambda e: e.tensor_copy(r_sb[:, :], ps3[:, 0:512]), "sv")
            psum_last["ps3"] = t_rcp
            CS["t_r_ps"] = t_rcp
            ws("vector", [t_rcp, CS["o_read"][om]])
            t_oe = o("vector", lambda e, p=p, j=j, om=om: e.tensor_tensor(
                O_fm[j * 64:(j + 1) * 64, om, p, :], Oraw[0:64, j, :],
                r_sb[0:64, 0:512], ALU.mult), "sv")
            oraw_gate[j] = t_oe
            CS["t_oe_prev"] = t_oe
            evac_toks[m].append(t_oe)

    def emit_epi(m, tc4):
        om = m % 2
        tcg = m * 4 + tc4
        wo_half_toks = []
        tokm = None
        for h2 in range(2):
            WOP = ps4[:, 0:512] if h2 == 0 else ps3[:, 0:512]
            wkey = "ps4" if h2 == 0 else "ps3"
            prv = CS["t_wo_ps"] if h2 == 0 else CS["t_wo_ps3"]
            ws("tensor", evac_toks[m] + [t_wol, prv, psum_last.get(wkey)])
            for dc in range(8):
                def mmw(e, dc=dc, tc4=tc4, h2=h2, om=om, WOP=WOP):
                    return e.matmul(WOP, O_fm[:, om, dc, tc4 * 128:(tc4 + 1) * 128],
                                    wo_bf[:, dc, h2 * 512:(h2 + 1) * 512],
                                    start=(dc == 0), stop=(dc == 7))
                tokm = o("tensor", mmw, "st") if dc == 7 else o("tensor", mmw)
            ws("vector", [tokm, t_woB, CS["t_af"]])
            t_wv = o("vector", lambda e, h2=h2, WOP=WOP: e.tensor_tensor(
                w_sb[:, h2 * 512:(h2 + 1) * 512], WOP,
                woB_bc[:, h2 * 512:(h2 + 1) * 512], ALU.add), "sv")
            if h2 == 0:
                CS["t_wo_ps"] = t_wv
            else:
                CS["t_wo_ps3"] = t_wv
            psum_last[wkey] = t_wv
            wo_half_toks.append(t_wv)
        CS["o_read"][om] = tokm
        if dbg:
            ws("gpsimd", wo_half_toks)
            t_dw = o("gpsimd", lambda e, tcg=tcg: e.dma_start(
                out=dbgw_d[tcg], in_=w_sb[:, :]), "dbgs", 16)
            ws("vector", [t_dw])
            ws("scalar", [t_dw])
        ws("vector", wo_half_toks)
        t_ln2, rstd2_ap, nmr2_ap, g2 = ln_stats_from(w_sb[:, :], [CS["t_af"], CS["t_am2"]])
        ws("sync", CS["xr_gate"])
        t_xl = o("sync", lambda e, tcg=tcg: e.dma_start(
            out=xr_t[:, :], in_=x_in[tcg * 128:(tcg + 1) * 128, :]), "xrl", 16)
        ws("scalar", [t_ln2, CS["t_am2"]])
        t_af = o("scalar", lambda e, rstd2_ap=rstd2_ap, nmr2_ap=nmr2_ap: e.activation(
            taff[:, :], w_sb[:, :], AF.Identity, bias=nmr2_ap, scale=rstd2_ap), "ss")
        CS["t_af"] = t_af
        ws("vector", [t_af, t_xl, CS["t_x1s"]])
        t_x1 = o("vector", lambda e: e.tensor_tensor(
            xr_t[:, :], xr_t[:, :], taff[:, :], ALU.add), "sv")
        w("gpsimd", t_x1)
        CS["t_x1s"] = o("gpsimd", lambda e, tcg=tcg: e.dma_start(
            out=x1_d[tcg * 128:(tcg + 1) * 128, :], in_=xr_t[:, :]), "x1s", 16)
        w("vector", t_x1)
        ln_mark_readers(g2, [t_af, t_x1])
        t_ln3, rstd3_ap, nmr3_ap, g3 = ln_stats_from(xr_t[:, :], [t_af, t_x1])
        w("vector", t_ln3)
        t_sv3 = o("vector", lambda e, tcg=tcg, rstd3_ap=rstd3_ap: e.tensor_copy(
            rstd3a[:, tcg:tcg + 1], rstd3_ap), "sv")
        t_sn3 = o("vector", lambda e, tcg=tcg, nmr3_ap=nmr3_ap: e.tensor_copy(
            nmr3a[:, tcg:tcg + 1], nmr3_ap), "sv")
        ws("vector", [t_sv3, t_sn3, t_af, t_x1])
        t_x3 = o("vector", lambda e, rstd3_ap=rstd3_ap, nmr3_ap=nmr3_ap: e.tensor_scalar(
            taff[:, :], xr_t[:, :], rstd3_ap, nmr3_ap, ALU.mult, ALU.add), "sv")
        CS["xr_gate"] = [CS["t_x1s"], t_ln3, t_x3]
        w("vector", t_x3)
        t_am = o("vector", lambda e, g3=g3: e.tensor_reduce(
            s_t1[:, g3:g3 + 1], taff[:, :], AX.X, ALU.max,
            apply_absolute_value=True), "sv")
        ws("vector", [t_am, t_amz])
        CS["t_am2"] = o("vector", lambda e, g3=g3: e.tensor_tensor(
            amax_r[:, :], amax_r[:, :], s_t1[:, g3:g3 + 1], ALU.max), "sv")
        ln_mark_readers(g3, [t_sv3, t_sn3, t_x3, t_am])

    for p in range(NPAIR):
        load_qm_pair(0, p, None)
    pend_norm = None
    pend_epi = []
    for m in range(NMACRO):
        for p in range(NPAIR):
            hA, hB = 2 * p, 2 * p + 1
            t_avA = t_avB = None
            pend = [None]

            def emit_av(kc, t_ex):
                nonlocal t_avA, t_avB
                sbix = kc % 2
                av_deps = [t_ex]
                if kc == 0:
                    av_deps += [av_read[0], av_read[1]]
                ws("tensor", av_deps)
                t_avA = o("tensor", lambda e, kc=kc, hA=hA, sbix=sbix: e.matmul(
                    ps1[0:65, 0:512], V_tm[:, kc, hA * 65:hA * 65 + 65],
                    P_pr[:, sbix, 0:512],
                    start=(kc == 0), stop=(kc == NKC - 1)), "st")
                t_avB = o("tensor", lambda e, kc=kc, hB=hB, sbix=sbix: e.matmul(
                    ps2[0:65, 0:512], V_tm[:, kc, hB * 65:hB * 65 + 65],
                    P_pr[:, sbix, 512:1024],
                    start=(kc == 0), stop=(kc == NKC - 1)), "st")
                p_toks[sbix] = t_avB

            for kc in range(NKC):
                sbix = kc % 2
                S = psA if sbix == 0 else psB
                ws("tensor", [s_toks[sbix], qm_pair[p], t_K_done, t_bias_c])
                o("tensor", lambda e, p=p, kc=kc, S=S: e.matmul(
                    S[:, 0:512], K_fm[0:64, p, kc * 128:(kc + 1) * 128],
                    Qm[0:64, p, :], start=True, stop=True, tile_position=(0, 0)))
                t_qk = o("tensor", lambda e, p=p, kc=kc, S=S: e.matmul(
                    S[:, 512:1024], K_fm[64:128, p, kc * 128:(kc + 1) * 128],
                    Qm[64:128, p, :], start=True, stop=True, tile_position=(64, 0)), "st")
                if kc == NKC - 1 and m + 1 < NMACRO:
                    load_qm_pair(m + 1, p, t_qk)
                ws("scalar", [t_qk, p_toks[sbix]])
                t_ex = o("scalar", lambda e, p=p, m=m, S=S, sbix=sbix: e.activation(
                    P_pr[:, sbix, :], S[:, :], AF.Exp,
                    bias=bias_c[:, p * 4 + m:p * 4 + m + 1], scale=INV_SCALE), "ss")
                s_toks[sbix] = t_ex
                if dbg and m == 0 and p == 0 and kc == 0:
                    w("gpsimd", t_ex)
                    t_dp = o("gpsimd", lambda e, sbix=sbix: e.dma_start(
                        out=dbgp_d[:, :], in_=P_pr[:, sbix, :]), "dbgs", 16)
                    ws("scalar", [t_dp])
                if pend[0] is not None:
                    emit_av(*pend[0])
                pend[0] = (kc, t_ex)
                if kc == 6 and pend_norm is not None:
                    emit_norm(*pend_norm)
                    pend_norm = None
                if kc == 16 and pend_epi:
                    emit_epi(*pend_epi.pop(0))
            emit_av(*pend[0])
            t_cps = []
            for j, (avps, t_av) in enumerate([(ps1, t_avA), (ps2, t_avB)]):
                ws("vector", [t_av, oraw_gate[j]])
                t_cp = o("vector", lambda e, j=j, avps=avps: e.tensor_copy(
                    Oraw[0:65, j, :], avps[0:65, 0:512]), "sv")
                av_read[j] = t_cp
                t_cps.append(t_cp)
            pend_norm = (m, p, t_cps)
        pend_epi += [(m, tc4) for tc4 in range(4)]
    emit_norm(*pend_norm)
    pend_norm = None
    for item in pend_epi:
        emit_epi(*item)
    pend_epi = []
    t_x1s = CS["t_x1s"]
    t_am2 = CS["t_am2"]
    t_x1_last = t_x1s
    t_amax_done = t_am2

    kb.join()
    stC.close()
    stKV.close()

    stD = contextlib.ExitStack()
    sbD = mksb(stD)
    xsimT = sbD("xsimT", [128, 8, NTOK], BF16)
    wjt = sbD("wjt", [128, 2, 8, 128], BF16)
    wos_s = sbD("wos_s", [128, 32, D_], BF16)
    hg_t = sbD("hg_t", [128, 2, 512], BF16)
    hq_f = sbD("hq_f", [128, 4, 512], BF16)
    hq_w = sbD("hq_w", [128, 2, 512])
    hq_bf = sbD("hq_bf", [128, 4, 512], BF16)
    xq_t = sbD("xq_t", [128, 2, D_])
    xq_bf = sbD("xq_bf", [128, 2, D_], BF16)
    y_sb = sbD("y_sb", [128, D_])
    boB_bc = sbD("boB_bc", [128, D_])
    w_sb2 = sbD("w_sb2", [128, D_])
    xr_t2 = sbD("xr_t2", [128, D_])
    taff2 = sbD("taff2", [128, D_])

    def amax_collective(src_col, cc_in_h, cc_out_h, dst_s, dst_inv, prev_toks):
        ws("gpsimd", prev_toks)
        t_f1 = o("gpsimd", lambda e: e.dma_start(
            out=flip_d[0:1, 0:128], in_=src_col[:, :]), "flst", 16)
        w("sync", t_f1)
        t_f2 = o("sync", lambda e: e.dma_start(out=grow[:, :], in_=flip_d[0:1, 0:128]), "flrd", 16)
        w("gpsimd", t_f2)
        t_c1 = o("gpsimd", lambda e: e.dma_start(out=cc_in_h.ap(), in_=grow[:, :]), "ccb", 16)
        w("gpsimd", t_c1)
        t_cc = o("gpsimd", lambda e: e.collective_compute(
            "AllReduce", ALU.max, replica_groups=[list(range(8))],
            ins=[cc_in_h.ap().opt()], outs=[cc_out_h.ap().opt()]), "sg")
        w("gpsimd", t_cc)
        t_c2 = o("gpsimd", lambda e: e.dma_start(out=grow[:, :], in_=cc_out_h.ap()), "ccb", 16)
        w("vector", t_c2)
        t_g1 = o("vector", lambda e: e.tensor_reduce(sc_a[:, :], grow[:, :], AX.X, ALU.max), "sv")
        w("vector", t_g1)
        t_g2 = o("vector", lambda e: e.tensor_scalar(sc_a[:, :], sc_a[:, :], 1e-8, None, ALU.max), "sv")
        w("vector", t_g2)
        t_g3 = o("vector", lambda e: e.reciprocal(sc_c[:, :], sc_a[:, :]), "sv")
        w("vector", t_g3)
        t_g4 = o("vector", lambda e: e.tensor_scalar(
            sc_b[:, :], sc_c[:, :], QMAX_A16, None, ALU.mult), "sv")
        w("vector", t_g4)
        t_g6 = o("vector", lambda e: e.tensor_scalar(
            sc_c[:, :], sc_a[:, :], 1.0 / QMAX_A16, None, ALU.mult), "sv")
        ws("tensor", [t_g6, psum_last.get("ps3")])
        o("tensor", lambda e: e.matmul(ps3[:, 0:1], ones_row[:, :], sc_b[:, :], start=True, stop=True))
        t_g8 = o("tensor", lambda e: e.matmul(ps3[:, 1:2], ones_row[:, :], sc_c[:, :], start=True, stop=True), "st")
        w("vector", t_g8)
        t_g9 = o("vector", lambda e: e.tensor_copy(dst_s[:, :], ps3[:, 0:1]), "sv")
        w("vector", t_g9)
        t_g10 = o("vector", lambda e: e.tensor_copy(dst_inv[:, :], ps3[:, 1:2]), "sv")
        psum_last["ps3"] = t_g10
        return t_g10

    t_sx = amax_collective(amax_r, cc1_in_h, cc1_out_h, sxb, invsxb, [t_amax_done])

    xq_g3 = [None, None]
    xq_g4 = [None, None]
    xq_st = [None, None]
    for tcg in range(NTT):
        bq = tcg % 2
        ws("sync", [xq_g4[bq]])
        t_l = o("sync", lambda e, tcg=tcg, bq=bq: e.dma_start(
            out=xq_t[:, bq, :], in_=x1_d[tcg * 128:(tcg + 1) * 128, :]), "xql%d" % bq, 16)
        ws("vector", [t_l, t_sx, xq_g3[bq]])
        t_q1 = o("vector", lambda e, tcg=tcg, bq=bq: e.tensor_tensor(
            q2sc[:, bq:bq + 1], rstd3a[:, tcg:tcg + 1], sxb[:, :], ALU.mult), "sv")
        t_q2 = o("vector", lambda e, tcg=tcg, bq=bq: e.tensor_tensor(
            q2bi[:, bq:bq + 1], nmr3a[:, tcg:tcg + 1], sxb[:, :], ALU.mult), "sv")
        w("vector", t_q2)
        t_q2b = o("vector", lambda e, bq=bq: e.tensor_scalar(
            q2bi[:, bq:bq + 1], q2bi[:, bq:bq + 1], CMAGIC, None, ALU.add), "sv")
        ws("scalar", [t_l, t_q1, t_q2b, xq_g4[bq]])
        t_q3 = o("scalar", lambda e, bq=bq: e.activation(
            xq_t[:, bq, :], xq_t[:, bq, :], AF.Identity,
            bias=q2bi[:, bq:bq + 1], scale=q2sc[:, bq:bq + 1]), "ss")
        xq_g3[bq] = t_q3
        ws("vector", [t_q3, xq_st[bq]])
        t_q4 = o("vector", lambda e, bq=bq: e.tensor_scalar(
            xq_bf[:, bq, :], xq_t[:, bq, :], CMAGIC, invsxb[:, :], ALU.subtract, ALU.mult), "sv")
        xq_g4[bq] = t_q4
        w("gpsimd", t_q4)
        t_xq_tok = o("gpsimd", lambda e, tcg=tcg, bq=bq: e.dma_start(
            out=xsim_d[tcg * 128:(tcg + 1) * 128, :], in_=xq_bf[:, bq, :]), "xss%d" % bq, 16)
        xq_st[bq] = t_xq_tok
    ws("scalar", [xq_st[0], xq_st[1]])
    t_xt = None
    for fc in range(8):
        t_xt = o("scalar", lambda e, fc=fc: e.dma_start_transpose(
            xsimT[:, fc, :], xsim_d[:, fc * 128:(fc + 1) * 128]), "dsc", 16)
    t_xsimT = t_xt

    t_wosl = None
    for jc in range(32):
        t_wosl = o("sync", lambda e, jc=jc: e.dma_start(
            out=wos_s[:, jc, :], in_=wout_d[jc, :, :]), "wosl", 16)
    t_wos = t_wosl

    hb_reuse = [None, None]
    hm_reuse = [None, None]
    wj_reuse = [None, None]
    mm1_ps = {}
    t_hm2 = None
    t_hs = None
    for jc in range(32):
        bj = jc % 2
        w("sync", wj_reuse[bj])
        t_wl = o("sync", lambda e, jc=jc, bj=bj: e.dma_start(
            out=wjt[:, bj, :, :], in_=wint_d[jc, :, :, :]), "wjl%d" % bj, 16)
        for t4 in range(4):
            psl = [psA[:, 0:512], psA[:, 512:1024], psB[:, 0:512], psB[:, 512:1024]][t4]
            ws("tensor", [t_wl, t_xsimT, mm1_ps.get(t4)])
            tokm = None
            for fc in range(8):
                def mm1(e, fc=fc, t4=t4, bj=bj, psl=psl):
                    return e.matmul(psl, wjt[:, bj, fc, :],
                                    xsimT[:, fc, t4 * 512:(t4 + 1) * 512],
                                    start=(fc == 0), stop=(fc == 7))
                tokm = o("tensor", mm1, "st") if fc == 7 else o("tensor", mm1)
            if t4 == 3:
                wj_reuse[bj] = tokm
            bh = t4 % 2
            ws("scalar", [tokm, hb_reuse[bh], t_sjr_done, hm_reuse[bh]])
            t_ge = o("scalar", lambda e, jc=jc, bh=bh, psl=psl: e.activation(
                hg_t[:, bh, :], psl, AF.Gelu, bias=bin_s[:, jc:jc + 1],
                scale=sjr_p[:, jc:jc + 1]), "ss")
            mm1_ps[t4] = t_ge
            ws("vector", [t_ge, t_hm2])
            t_hm = o("vector", lambda e, bh=bh: e.tensor_reduce(
                hmt_s[:, :], hg_t[:, bh, :], AX.X, ALU.max, apply_absolute_value=True), "sv")
            hm_reuse[bh] = t_hm
            ws("vector", [t_hm, t_hmz])
            t_hm2 = o("vector", lambda e: e.tensor_tensor(
                hmax_r[:, :], hmax_r[:, :], hmt_s[:, :], ALU.max), "sv")
            w("gpsimd", t_ge)
            t_hs = o("gpsimd", lambda e, jc=jc, t4=t4, bh=bh: e.dma_start(
                out=hid_d[jc, :, t4 * 512:(t4 + 1) * 512], in_=hg_t[:, bh, :]), "his%d" % bh, 16)
            hb_reuse[bh] = t_hs
    t_hid_done = t_hs
    t_hmax_done = t_hm2

    kb.join()
    t_sh = amax_collective(hmax_r, cc2_in_h, cc2_out_h, shb, invshb, [t_hmax_done])

    w("gpsimd", t_sh)
    t_bor = o("gpsimd", lambda e: e.dma_start(out=xq_t[0:1, 0, :], in_=bout_row), "rowb2", 16)
    bo_toks = []
    for h2 in range(2):
        ws("tensor", [t_bor, psum_last.get("ps3"), psum_last.get("ps4"), t_sh])
        bo_toks.append(o("tensor", lambda e, h2=h2: e.matmul(
            ps3[:, 0:512] if h2 == 0 else ps4[:, 0:512], ones_row[:, :],
            xq_t[0:1, 0, h2 * 512:(h2 + 1) * 512], start=True, stop=True), "st"))
    ws("vector", bo_toks)
    t_boB = o("vector", lambda e: e.tensor_copy(boB_bc[:, 0:512], ps3[:, 0:512]), "sv")
    t_boB2 = o("vector", lambda e: e.tensor_copy(boB_bc[:, 512:1024], ps4[:, 0:512]), "sv")
    psum_last["ps3"] = t_boB
    psum_last["ps4"] = t_boB2

    ps_pairs = [(psA[:, 0:512], psA[:, 512:1024]),
                (psB[:, 0:512], psB[:, 512:1024]),
                (ps1[:, 0:512], ps2[:, 0:512]),
                (ps3[:, 0:512], ps4[:, 0:512])]
    hqf_reuse = [None, None, None, None]
    hqw_gate = [None, None]
    mm2_last = {}
    y_store = []
    t_ln4_prev = None
    t_y_prev = None
    t_ys = None
    for tb in range(4):
        mm2_toks = {}
        for jc in range(32):
            bh = jc % 4
            w("sync", hqf_reuse[bh])
            t_hl = o("sync", lambda e, jc=jc, tb=tb, bh=bh: e.dma_start(
                out=hq_f[:, bh, :], in_=hid_d[jc, :, tb * 512:(tb + 1) * 512]), "hql%d" % bh, 16)
            bw = bh % 2
            ws("scalar", [t_hl, t_sh, hqw_gate[bw]])
            t_tq = o("scalar", lambda e, bh=bh, bw=bw: e.activation(
                hq_w[:, bw, :], hq_f[:, bh, :], AF.Identity, bias=cmag_b[:, :], scale=shb[:, :]), "ss")
            ws("vector", [t_tq, hqf_reuse[bh]])
            t_hq = o("vector", lambda e, bh=bh, bw=bw: e.tensor_scalar(
                hq_bf[:, bh, :], hq_w[:, bw, :], CMAGIC, invshb[:, :],
                ALU.subtract, ALU.mult), "sv")
            hqw_gate[bw] = t_hq
            tk = None
            for tc4 in range(4):
                pa, pb = ps_pairs[tc4]
                deps = [t_hq, t_wos]
                if jc == 0:
                    deps.append(mm2_last.get(tc4))
                ws("tensor", deps)
                def mma(e, jc=jc, tc4=tc4, bh=bh, pa=pa):
                    return e.matmul(pa, hq_bf[:, bh, tc4 * 128:(tc4 + 1) * 128],
                                    wos_s[:, jc, 0:512], start=(jc == 0), stop=(jc == 31))
                def mmb(e, jc=jc, tc4=tc4, bh=bh, pb=pb):
                    return e.matmul(pb, hq_bf[:, bh, tc4 * 128:(tc4 + 1) * 128],
                                    wos_s[:, jc, 512:1024], start=(jc == 0), stop=(jc == 31))
                o("tensor", mma)
                tk = o("tensor", mmb, "st")
                if jc == 31:
                    mm2_toks[tc4] = tk
            hqf_reuse[bh] = tk
        for tc4 in range(4):
            tcg = tb * 4 + tc4
            pa, pb = ps_pairs[tc4]
            ws("vector", [mm2_toks[tc4], t_boB, t_boB2, t_ys, t_ln4_prev])
            t_w1 = o("vector", lambda e, pa=pa: e.tensor_tensor(
                w_sb2[:, 0:512], pa, boB_bc[:, 0:512], ALU.add), "sv")
            w("vector", t_w1)
            t_w2 = o("vector", lambda e, pb=pb: e.tensor_tensor(
                w_sb2[:, 512:1024], pb, boB_bc[:, 512:1024], ALU.add), "sv")
            mm2_last[tc4] = t_w2
            w("vector", t_w2)
            t_ln4, rstd4_ap, nmr4_ap, g4 = ln_stats_from(w_sb2[:, :], [t_ln4_prev, t_y_prev])
            w("sync", t_y_prev)
            t_xl = o("sync", lambda e, tcg=tcg: e.dma_start(
                out=xr_t2[:, :], in_=x1_d[tcg * 128:(tcg + 1) * 128, :]), "xr2", 16)
            ws("scalar", [t_ln4, t_y_prev])
            t_af = o("scalar", lambda e, rstd4_ap=rstd4_ap, nmr4_ap=nmr4_ap: e.activation(
                taff2[:, :], w_sb2[:, :], AF.Identity, bias=nmr4_ap, scale=rstd4_ap), "ss")
            t_ln4_prev = t_af
            ws("vector", [t_af, t_xl])
            t_y = o("vector", lambda e: e.tensor_tensor(
                y_sb[:, :], xr_t2[:, :], taff2[:, :], ALU.add), "sv")
            t_y_prev = t_y
            ln_mark_readers(g4, [t_af, t_y])
            w("gpsimd", t_y)
            t_ys = o("gpsimd", lambda e, tcg=tcg: e.dma_start(
                out=y_out[tcg * 128:(tcg + 1) * 128, :], in_=y_sb[:, :]), "yst", 16)
            y_store.append(t_ys)

    if dbg:
        dbg_specs = [
            ("dbg_w", dbgw_d, [NTT, 128, D_], F32),
            ("dbg_o", dbgo_d, [NMACRO, 128, 8, QMAC], BF16),
            ("dbg_p00", dbgp_d, [128, 1024], BF16),
            ("dbg_sums", dbgs_d, [NMACRO, NPAIR, 2, 512], F32),
            ("dbg_ln1x", ln1x_d, [NKEY, D_], BF16),
            ("dbg_x1", x1_d, [NTOK, D_], F32),
            ("dbg_xsim", xsim_d, [NTOK, D_], BF16),
            ("dbg_q", q_d, [NPAIR, 128, NTOK], BF16),
            ("dbg_hid0", hid_d[0], [128, NTOK], BF16),
            ("dbg_hid7", hid_d[7], [128, NTOK], BF16),
            ("dbg_wint0", wint_d[0], [8, 128, 128], BF16),
            ("dbg_wout0", wout_d[0], [128, D_], BF16),
        ]
        ws("gpsimd", [y_store[-1], t_x1_last, t_hid_done, t_xq_tok])
        for nm, srcap, shp, dt in dbg_specs:
            tdg = o("gpsimd", lambda e, nm=nm, srcap=srcap, shp=shp, dt=dt: e.dma_start(
                out=nc.dram_tensor(nm, shp, dt, kind="ExternalOutput").ap(), in_=srcap), "dbgs", 16)
        for nm, srcap, shp in [("dbg_biasc", bias_c[:, :], [128, 32]),
                               ("dbg_qn2", qn2[:, :], [2, 32]),
                               ("dbg_kn2", kn2[:, :], [2, NPAIR]),
                               ("dbg_sjr", sjr_p[:, :], [128, 32]),
                               ("dbg_sxb", sxb[:, :], [128, 1]),
                               ("dbg_shb", shb[:, :], [128, 1]),
                               ("dbg_rstd3", rstd3a[:, :], [128, NTT]),
                               ("dbg_nmr3", nmr3a[:, :], [128, NTT])]:
            tdg = o("gpsimd", lambda e, nm=nm, srcap=srcap, shp=shp: e.dma_start(
                out=nc.dram_tensor(nm, shp, F32 if "wint" not in nm else BF16,
                                   kind="ExternalOutput").ap(), in_=srcap), "dbgs", 16)
        w("gpsimd", tdg)

    ws("gpsimd", [y_store[-1], t_x1_last, t_hid_done])

    with nc.Block() as block:
        block.sync(kb.runner("sync"))
        block.vector(kb.runner("vector"))
        block.scalar(kb.runner("scalar"))
        block.tensor(kb.runner("tensor"))
        block.gpsimd(kb.runner("gpsimd"))
    stD.close()
    kb.stack.close()
    return nc


def _headperm():
    pi = np.empty(D_, np.int64)
    for h in range(NH):
        for j in range(32):
            pi[64 * h + j] = 64 * h + 2 * j
            pi[64 * h + 32 + j] = 64 * h + 2 * j + 1
    return pi


def _rope_tables(positions):
    pos = np.asarray(positions, np.float32)
    ids = np.arange(512, dtype=np.float32)
    theta = np.float32(10000.0) ** (np.float32(-2.0) * ids) / np.float32(512.0)
    f = np.arange(D_)
    c = f // 128
    r = f % 128
    head = 2 * c + r // 64
    j = r % 64
    pg = 32 * head + (j % 32)
    ang = pos[None, :] * theta[pg][:, None]
    cos = np.cos(ang).astype(np.float32)
    sin = np.sin(ang).astype(np.float32)
    sin[j < 32, :] = -sin[j < 32, :]
    return (cos.astype(ml_dtypes.bfloat16), sin.astype(ml_dtypes.bfloat16))


def _pswap():
    m = np.zeros((128, 128), np.float32)
    for r in range(128):
        jj = r % 64
        partner = r + 32 if jj < 32 else r - 32
        m[partner, r] = 1.0
    return m.astype(ml_dtypes.bfloat16)


def _bias_pm(b, chunks):
    return np.ascontiguousarray(np.asarray(b, np.float32).reshape(chunks, 128).T)


_NC_CACHE = [None]


def kernel(L0_attn_x, L0_attn_Wd_q_w, L0_attn_Wd_q_b, L0_attn_Wu_q_w, L0_attn_Wu_q_b,
           L0_attn_Wq_r_w, L0_attn_Wq_r_b, L0_attn_Wk_r_w, L0_attn_Wk_r_b,
           L0_attn_Wd_kv_w, L0_attn_Wd_kv_b, L0_attn_Wu_k_w, L0_attn_Wu_k_b,
           L0_attn_Wu_v_w, L0_attn_Wu_v_b, L0_attn_Wo_w, L0_attn_Wo_b,
           L0_ffn_W_in, L0_ffn_b_in, L0_ffn_W_out, L0_ffn_b_out):
    f32 = lambda a: np.ascontiguousarray(np.asarray(a), dtype=np.float32)
    x = f32(L0_attn_x)
    pi = _headperm()
    shared = {
        "wd_q": f32(L0_attn_Wd_q_w), "wd_kv": f32(L0_attn_Wd_kv_w),
        "wu_q": np.ascontiguousarray(f32(L0_attn_Wu_q_w)[:, pi]),
        "wq_r": np.ascontiguousarray(f32(L0_attn_Wq_r_w)[:, pi]),
        "wk_r": np.ascontiguousarray(f32(L0_attn_Wk_r_w)[:, pi]),
        "wu_k": np.ascontiguousarray(f32(L0_attn_Wu_k_w)[:, pi]),
        "wu_v": f32(L0_attn_Wu_v_w), "wo": f32(L0_attn_Wo_w),
        "w_in": f32(L0_ffn_W_in), "w_out": f32(L0_ffn_W_out),
        "bd_q": _bias_pm(L0_attn_Wd_q_b, 4), "bd_kv": _bias_pm(L0_attn_Wd_kv_b, 4),
        "bu_q": _bias_pm(np.asarray(L0_attn_Wu_q_b, np.float32)[pi], 8),
        "bq_r": _bias_pm(np.asarray(L0_attn_Wq_r_b, np.float32)[pi], 8),
        "bk_r": _bias_pm(np.asarray(L0_attn_Wk_r_b, np.float32)[pi], 8),
        "bu_k": _bias_pm(np.asarray(L0_attn_Wu_k_b, np.float32)[pi], 8),
        "b_in": _bias_pm(L0_ffn_b_in, 32),
        "bu_v_row": f32(L0_attn_Wu_v_b).reshape(1, D_),
        "bo_row": f32(L0_attn_Wo_b).reshape(1, D_),
        "bout_row": f32(L0_ffn_b_out).reshape(1, D_),
        "pswap": _pswap(),
        "idn": np.eye(128, dtype=np.float32).astype(ml_dtypes.bfloat16),
        "idn32": np.eye(128, dtype=np.float32),
    }
    in_maps = []
    for core in range(8):
        bc, hc = core // 2, core % 2
        order = np.concatenate([np.arange(hc * NTOK, (hc + 1) * NTOK),
                                np.arange((1 - hc) * NTOK, (2 - hc) * NTOK)])
        cosb, sinb = _rope_tables(order)
        m = dict(shared)
        m["x"] = np.ascontiguousarray(x[bc][order])
        m["cos_d"] = np.ascontiguousarray(cosb)
        m["sin_d"] = np.ascontiguousarray(sinb)
        in_maps.append(m)

    if _NC_CACHE[0] is None:
        _NC_CACHE[0] = build_core_program()
    nc = _NC_CACHE[0]
    res = run_bass_kernel_spmd(nc, in_maps, core_ids=list(range(8)))
    out = np.empty((B_, S_, D_), np.float32)
    for core in range(8):
        bc, hc = core // 2, core % 2
        out[bc, hc * NTOK:(hc + 1) * NTOK, :] = res.results[core]["y"]
    return out


# version 15
# speedup vs baseline: 1.0566x; 1.0480x over previous
import contextlib
import numpy as np
import ml_dtypes

import concourse.bass as bass
import concourse.mybir as mybir
from concourse.bass_utils import run_bass_kernel_spmd

F32 = mybir.dt.float32
BF16 = mybir.dt.bfloat16
AF = mybir.ActivationFunctionType
ALU = mybir.AluOpType
AX = mybir.AxisListType

ENGS = ("sync", "vector", "scalar", "tensor", "gpsimd")

B_, S_, D_ = 4, 4096, 1024
NTOK, NKEY = 2048, 4096
NH, HD = 16, 64
DQ = DKV = 512
DFF = 4096
SCALE = float(np.sqrt(np.float32(NH + DKV + DQ)))
INV_SCALE = float(np.float32(1.0) / np.float32(SCALE))
EPS = 1e-5
CMAGIC = 12582912.0
QMAX_A16 = 32767.0
NMACRO, QMAC = 4, 512
NKC = NKEY // 128
NPAIR = NH // 2
NTT = NTOK // 128
NKT = NKEY // 512
NQT = NTOK // 512


class KB:

    def __init__(self, nc):
        self.nc = nc
        self.ops = {e: [] for e in ENGS}
        self.sems = {}
        self.cnt = {}
        self.waited = {e: {} for e in ENGS}
        self.dma_sems = set()
        self.stack = contextlib.ExitStack()

    def sem(self, name):
        if name not in self.sems:
            self.sems[name] = self.stack.enter_context(self.nc.semaphore(name))
            self.cnt[name] = 0
        return self.sems[name]

    def op(self, eng, fn, sem=None, n=1):
        if sem is None:
            self.ops[eng].append(fn)
            return None
        if n == 16:
            self.dma_sems.add(sem)
        h = self.sem(sem)
        self.cnt[sem] += n
        val = self.cnt[sem]

        def wrapped(e, fn=fn, h=h, n=n):
            fn(e).then_inc(h, n)
        self.ops[eng].append(wrapped)
        return (sem, val)

    def wait(self, eng, tok):
        if tok is None:
            return
        name, val = tok
        if self.waited[eng].get(name, 0) >= val:
            return
        self.waited[eng][name] = val
        h = self.sem(name)
        self.ops[eng].append(lambda e, h=h, val=val: e.wait_ge(h, val))

    def waits(self, eng, toks):
        best = {}
        for t in toks:
            if t is None:
                continue
            name, val = t
            if best.get(name, -1) < val:
                best[name] = val
        for name, val in best.items():
            self.wait(eng, (name, val))

    def join(self):
        names = [s for s in self.cnt if not s.startswith("jn_")]
        for e in ENGS:
            for sem in sorted(names):
                self.wait(e, (sem, self.cnt[sem]))
        toks = [self.op(e, lambda en: en.nop(), sem="jn_" + e) for e in ENGS]
        for e in ENGS:
            for t in toks:
                self.wait(e, t)

    def runner(self, e):
        ops = self.ops[e]

        def run(eng):
            for f in ops:
                f(eng)
        return run


def build_core_program(dbg=False):
    nc = bass.Bass("TRN2", target_bir_lowering=False, debug=False)
    kb = KB(nc)
    o, w, ws = kb.op, kb.wait, kb.waits

    def din(name, shape, dt=F32):
        return nc.dram_tensor(name, shape, dt, kind="ExternalInput").ap()

    def dint(name, shape, dt=F32):
        return nc.dram_tensor(name, shape, dt).ap()

    x_in = din("x", [NKEY, D_])
    wd_q = din("wd_q", [D_, DQ]);   wd_kv = din("wd_kv", [D_, DKV])
    wu_q = din("wu_q", [DQ, D_]);   wq_r = din("wq_r", [DQ, D_])
    wk_r = din("wk_r", [D_, D_]);   wu_k = din("wu_k", [DKV, D_])
    wu_v = din("wu_v", [DKV, D_]);  wo_w = din("wo", [D_, D_])
    w_in = din("w_in", [D_, DFF]);  w_out = din("w_out", [DFF, D_])
    bd_q = din("bd_q", [128, 4]);   bd_kv = din("bd_kv", [128, 4])
    bu_q = din("bu_q", [128, 8]);   bq_r = din("bq_r", [128, 8])
    bk_r = din("bk_r", [128, 8]);   bu_k = din("bu_k", [128, 8])
    b_in = din("b_in", [128, 32])
    bu_v_row = din("bu_v_row", [1, D_])
    bo_row = din("bo_row", [1, D_])
    bout_row = din("bout_row", [1, D_])
    cos_d = din("cos_d", [D_, NKEY], BF16)
    sin_d = din("sin_d", [D_, NKEY], BF16)
    pswap_in = din("pswap", [128, 128], BF16)
    idn_in = din("idn", [128, 128], BF16)
    idn32_in = din("idn32", [128, 128])
    y_out = nc.dram_tensor("y", [NTOK, D_], F32, kind="ExternalOutput").ap()

    ln1x_d = dint("ln1x_d", [NKEY, D_], BF16)
    x1_d = dint("x1_d", [NTOK, D_])
    xsim_d = dint("xsim_d", [NTOK, D_], BF16)
    hid_d = dint("hid_d", [32, 128, NTOK], BF16)
    wint_d = dint("wint_d", [32, 128, 8, 128], BF16)
    wout_d = dint("wout_d", [32, 128, D_], BF16)
    q_d = dint("q_d", [NPAIR, 128, NTOK], BF16)
    wdq_d = dint("wdq_d", [8, 128, DQ], BF16)
    wdkv_d = dint("wdkv_d", [8, 128, DKV], BF16)
    wuq_d = dint("wuq_d", [4, 128, D_], BF16)
    wqr_d = dint("wqr_d", [4, 128, D_], BF16)
    wkr_d = dint("wkr_d", [8, 128, D_], BF16)
    wuk_d = dint("wuk_d", [4, 128, D_], BF16)
    wuv_d = dint("wuv_d", [4, 128, D_], BF16)
    wo_d = dint("wo_d", [8, 128, D_], BF16)
    flip_d = dint("flip_d", [1, 4096])
    cc1_in_h = nc.dram_tensor("cc1_in", [1, 128], F32)
    cc1_out_h = nc.dram_tensor("cc1_out", [1, 128], F32)
    cc2_in_h = nc.dram_tensor("cc2_in", [1, 128], F32)
    cc2_out_h = nc.dram_tensor("cc2_out", [1, 128], F32)

    def mksb(stk):
        def f(name, shape, dt=F32):
            return stk.enter_context(nc.sbuf_tensor(name, shape, dt))
        return f

    sbP = mksb(kb.stack)

    def pst(name, shape):
        return kb.stack.enter_context(nc.psum_tensor(name, shape, F32))

    psA = pst("psA", [128, 1024])
    psB = pst("psB", [128, 1024])
    ps1 = pst("ps1", [128, 512])
    ps2 = pst("ps2", [128, 512])
    ps3 = pst("ps3", [128, 512])
    ps4 = pst("ps4", [128, 512])

    ones_row = sbP("ones_row", [1, 128])
    ones2 = sbP("ones2", [128, 2])
    ones2b = sbP("ones2b", [128, 2], BF16)
    ones21 = sbP("ones21", [2, 1])
    pswap_s = sbP("pswap_s", [128, 128], BF16)
    idn_s = sbP("idn_s", [128, 128], BF16)
    idn32_s = sbP("idn32_s", [128, 128])
    bias_c = sbP("bias_c", [128, 32])
    qn2 = sbP("qn2", [2, 32])
    kn2 = sbP("kn2", [2, NPAIR])
    kn4 = sbP("kn4", [2, 32])
    knall = sbP("knall", [2, 64])
    cc_sc = sbP("cc_sc", [2, 32])
    negc = sbP("negc", [1, 32])
    rstd3a = sbP("rstd3a", [128, NTT])
    nmr3a = sbP("nmr3a", [128, NTT])
    amax_r = sbP("amax_r", [128, 1])
    hmax_r = sbP("hmax_r", [128, 1])
    hmt_s = sbP("hmt_s", [128, 1])
    sjr_p = sbP("sjr_p", [128, 32])
    grow = sbP("grow", [1, 128])
    ln_sc = sbP("ln_sc", [128, 2, D_])
    s_sum = sbP("s_sum", [128, 2]); s_sq = sbP("s_sq", [128, 2])
    s_mean = sbP("s_mean", [128, 2]); s_var = sbP("s_var", [128, 2])
    s_rstd = sbP("s_rstd", [128, 2]); s_nmr = sbP("s_nmr", [128, 2])
    s_t1 = sbP("s_t1", [128, 2])
    sc_a = sbP("sc_a", [1, 1]); sc_b = sbP("sc_b", [1, 1]); sc_c = sbP("sc_c", [1, 1])
    sxb = sbP("sxb", [128, 1]); invsxb = sbP("invsxb", [128, 1])
    shb = sbP("shb", [128, 1]); invshb = sbP("invshb", [128, 1])
    q2sc = sbP("q2sc", [128, 2]); q2bi = sbP("q2bi", [128, 2])
    eps_b = sbP("eps_b", [128, 1]); cmag_b = sbP("cmag_b", [128, 1])
    tiny_b = sbP("tiny_b", [128, 1])
    bdq_s = sbP("bdq_s", [128, 4]);   bdkv_s = sbP("bdkv_s", [128, 4])
    buq_s = sbP("buq_s", [128, 8]);   bqr_s = sbP("bqr_s", [128, 8])
    bkr_s = sbP("bkr_s", [128, 8]);   buk_s = sbP("buk_s", [128, 8])
    bin_s = sbP("bin_s", [128, 32])

    psum_last = {}

    t_smld = None
    for dst, src in [(bdq_s, bd_q), (bdkv_s, bd_kv), (buq_s, bu_q),
                     (bqr_s, bq_r), (bkr_s, bk_r), (buk_s, bu_k),
                     (bin_s, b_in), (pswap_s, pswap_in)]:
        t_smld = o("sync", lambda e, dst=dst, src=src: e.dma_start(
            out=dst[:, :], in_=src), "smld", 16)

    t_ones = o("vector", lambda e: e.memset(ones_row[:, :], 1.0), "sv")
    t_o2a = o("vector", lambda e: e.memset(ones2[:, :], 0.0), "sv")
    w("vector", t_o2a)
    o("vector", lambda e: e.memset(ones2[0:64, 0:1], 1.0), "sv")
    o("vector", lambda e: e.memset(ones2[64:128, 1:2], 1.0), "sv")
    t_o2b = o("vector", lambda e: e.memset(ones2b[:, :], 0.0), "sv")
    w("vector", t_o2b)
    o("vector", lambda e: e.memset(ones2b[0:64, 0:1], 1.0), "sv")
    t_ones2 = o("vector", lambda e: e.memset(ones2b[64:128, 1:2], 1.0), "sv")
    t_cs1 = o("vector", lambda e: e.memset(eps_b[:, :], EPS), "sv")
    t_cs2 = o("vector", lambda e: e.memset(cmag_b[:, :], CMAGIC), "sv")
    t_cs3 = o("vector", lambda e: e.memset(tiny_b[:, :], 1e-30), "sv")
    t_consts = [t_cs1, t_cs2, t_cs3]
    t_amz = o("vector", lambda e: e.memset(amax_r[:, :], 0.0), "sv")
    t_hmz = o("vector", lambda e: e.memset(hmax_r[:, :], 0.0), "sv")
    o("sync", lambda e: e.dma_start(out=idn_s[:, :], in_=idn_in), "smld", 16)
    t_idn = o("sync", lambda e: e.dma_start(out=idn32_s[:, :], in_=idn32_in), "smld", 16)
    t_small_loads = t_idn
    t_o21 = o("vector", lambda e: e.memset(ones21[:, :], 1.0), "sv")

    _ln_state = [{"t_n": None, "t_r2": None, "readers": []},
                 {"t_n": None, "t_r2": None, "readers": []}]
    _ln_ctr = [0]

    def ln_stats_from(src_ap, readers):
        g = _ln_ctr[0] % 2
        _ln_ctr[0] += 1
        stt = _ln_state[g]
        sl = lambda t: t[:, g:g + 1]
        lsc = ln_sc[:, g, :]
        prevs = [t for t in (stt["readers"] + list(readers)
                             + [stt["t_n"], stt["t_r2"]]) if t]
        ws("vector", prevs)
        t_s1 = o("vector", lambda e: e.tensor_scalar(
            lsc, src_ap, 0.0, 0.0, ALU.add, ALU.add, accum_out=sl(s_sum)), "sv")
        w("vector", t_s1)
        t_s2 = o("vector", lambda e: e.scalar_tensor_tensor(
            lsc, src_ap, 1.0, src_ap, ALU.bypass, ALU.mult,
            accum_out=sl(s_sq)), "sv")
        w("vector", t_s2)
        t_m = o("vector", lambda e: e.tensor_scalar(
            sl(s_mean), sl(s_sum), 1.0 / D_, None, ALU.mult), "sv")
        w("vector", t_m)
        t_v1 = o("vector", lambda e: e.scalar_tensor_tensor(
            sl(s_var), sl(s_mean), -1.0, sl(s_mean), ALU.mult, ALU.mult), "sv")
        w("vector", t_v1)
        t_v2 = o("vector", lambda e: e.tensor_scalar(
            sl(s_t1), sl(s_sq), 1.0 / D_, sl(s_var), ALU.mult, ALU.add), "sv")
        ws("scalar", prevs)
        w("scalar", t_v2)
        ws("scalar", t_consts)
        t_r1 = o("scalar", lambda e: e.activation(
            sl(s_var), sl(s_t1), AF.Ln, bias=eps_b[:, :], scale=1.0), "ss")
        w("scalar", t_r1)
        t_r2 = o("scalar", lambda e: e.activation(
            sl(s_rstd), sl(s_var), AF.Exp, bias=0.0, scale=-0.5), "ss")
        w("vector", t_r2)
        t_n = o("vector", lambda e: e.scalar_tensor_tensor(
            sl(s_nmr), sl(s_mean), -1.0, sl(s_rstd), ALU.mult, ALU.mult), "sv")
        stt["t_n"] = t_n
        stt["t_r2"] = t_r2
        stt["readers"] = []
        return (t_n, sl(s_rstd), sl(s_nmr), g)

    def ln_mark_readers(g, toks):
        _ln_state[g]["readers"] = [t for t in toks if t]

    stA = contextlib.ExitStack()
    sbA = mksb(stA)
    xa_t = sbA("xa_t", [128, 2, D_])
    ln_bf = sbA("ln_bf", [128, 2, D_], BF16)
    wst_f = sbA("wst_f", [128, 2, D_])
    wst_b = sbA("wst_b", [128, 2, D_], BF16)

    wq_h = sbA("wq_h", [128, 8, DFF // 2])
    M_h = sbA("M_h", [128, DFF // 2])
    A_t = sbA("A_t", [128, 32])
    at_sb = sbA("at_sb", [16, 128])
    srow_h = sbA("srow_h", [1, DFF // 2])
    Sbc_h = sbA("Sbc_h", [128, DFF // 2])
    wint_h = sbA("wint_h", [128, DFF // 2], BF16)
    wqo_f = sbA("wqo_f", [128, 2, D_])
    so_t = sbA("so_t", [128, 32])
    sor_t = sbA("sor_t", [128, 32])
    nCr_t = sbA("nCr_t", [128, 32])
    wob16 = sbA("wob16", [128, D_], BF16)
    w127 = sbA("w127", [128, 32])

    R = {}

    def gen_ln1():
        xa_reuse = [None, None]
        lnbf_reuse = [None, None]
        t_aff_prev = None
        t_st_ln = None
        for i in range(NKEY // 128):
            bi = i % 2
            w("sync", xa_reuse[bi])
            t_x = o("sync", lambda e, i=i, bi=bi: e.dma_start(
                out=xa_t[:, bi, :], in_=x_in[i * 128:(i + 1) * 128, :]), "xal%d" % bi, 16)
            w("vector", t_x)
            t_n, rstd_ap, nmr_ap, g = ln_stats_from(xa_t[:, bi, :], [t_aff_prev])
            ws("scalar", [t_n, lnbf_reuse[bi]])
            t_ap = o("scalar", lambda e, bi=bi, rstd_ap=rstd_ap, nmr_ap=nmr_ap: e.activation(
                ln_bf[:, bi, :], xa_t[:, bi, :], AF.Identity,
                bias=nmr_ap, scale=rstd_ap), "ss")
            ln_mark_readers(g, [t_ap])
            xa_reuse[bi] = t_ap
            t_aff_prev = t_ap
            w("gpsimd", t_ap)
            t_st_ln = o("gpsimd", lambda e, i=i, bi=bi: e.dma_start(
                out=ln1x_d[i * 128:(i + 1) * 128, :], in_=ln_bf[:, bi, :]), "lns%d" % bi, 16)
            lnbf_reuse[bi] = t_st_ln
            yield
        R["t_ln1_both"] = [lnbf_reuse[0], lnbf_reuse[1]]

    def gen_precast():
        st_tok = [None, None]
        cp_tok = [None, None]
        ci = 0
        tok = None
        for srcw, dst_d, nrc, ncols in [
                (wd_q, wdq_d, 8, 512), (wd_kv, wdkv_d, 8, 512),
                (wu_q, wuq_d, 4, 1024), (wq_r, wqr_d, 4, 1024),
                (wk_r, wkr_d, 8, 1024), (wu_k, wuk_d, 4, 1024),
                (wu_v, wuv_d, 4, 1024), (wo_w, wo_d, 8, 1024)]:
            for rc in range(nrc):
                bi = ci % 2
                ci += 1
                w("sync", st_tok[bi])
                t_l = o("sync", lambda e, rc=rc, srcw=srcw, ncols=ncols, bi=bi: e.dma_start(
                    out=wst_f[:, bi, 0:ncols], in_=srcw[rc * 128:(rc + 1) * 128, :]),
                    "pcl%d" % bi, 16)
                ws("scalar", [t_l, st_tok[bi]])
                t_c = o("scalar", lambda e, ncols=ncols, bi=bi: e.activation(
                    wst_b[:, bi, 0:ncols], wst_f[:, bi, 0:ncols], AF.Identity,
                    bias=0.0, scale=1.0), "ss")
                cp_tok[bi] = t_c
                w("gpsimd", t_c)
                tok = o("gpsimd", lambda e, rc=rc, dst_d=dst_d, ncols=ncols, bi=bi: e.dma_start(
                    out=dst_d[rc, :, :], in_=wst_b[:, bi, 0:ncols]), "pcs%d" % bi, 16)
                st_tok[bi] = tok
                yield
        R["t_precast"] = tok

    def gen_wout():
        wqo_reuse = [None, None]
        t_sc_pr = None
        t_p2_prev = None
        for jc in range(32):
            bi = jc % 2
            w("sync", wqo_reuse[bi])
            t_l = o("sync", lambda e, jc=jc, bi=bi: e.dma_start(
                out=wqo_f[:, bi, :], in_=w_out[jc * 128:(jc + 1) * 128, :]), "wqo%d" % bi, 16)
            ws("vector", [t_l, t_sc_pr])
            t_am = o("vector", lambda e, jc=jc, bi=bi: e.tensor_reduce(
                so_t[:, jc:jc + 1], wqo_f[:, bi, :], AX.X, ALU.max,
                apply_absolute_value=True), "sv")
            w("vector", t_am)
            t_oc = o("vector", lambda e, jc=jc: e.tensor_scalar(
                so_t[:, jc:jc + 1], so_t[:, jc:jc + 1], 1e-8, None, ALU.max), "sv")
            w("vector", t_oc)
            t_or = o("vector", lambda e, jc=jc: e.tensor_scalar(
                sor_t[:, jc:jc + 1], so_t[:, jc:jc + 1], 1.0 / 127.0, None, ALU.mult), "sv")
            t_o127 = o("vector", lambda e, jc=jc: e.reciprocal(
                w127[:, jc:jc + 1], so_t[:, jc:jc + 1]), "sv")
            ws("vector", [t_or, t_o127])
            t_ncr = o("vector", lambda e, jc=jc: e.tensor_scalar(
                nCr_t[:, jc:jc + 1], sor_t[:, jc:jc + 1], -CMAGIC, None, ALU.mult), "sv")
            t_osd = o("vector", lambda e, jc=jc: e.tensor_scalar(
                so_t[:, jc:jc + 1], w127[:, jc:jc + 1], 127.0, None, ALU.mult), "sv")
            t_sc_pr = t_osd
            ws("scalar", [t_osd, t_ncr, t_consts[1]])
            t_p1 = o("scalar", lambda e, jc=jc, bi=bi: e.activation(
                wqo_f[:, bi, :], wqo_f[:, bi, :], AF.Identity,
                bias=cmag_b[:, :], scale=so_t[:, jc:jc + 1]), "ss")
            ws("scalar", [t_p1, t_p2_prev])
            t_p2 = o("scalar", lambda e, jc=jc, bi=bi: e.activation(
                wob16[:, :], wqo_f[:, bi, :], AF.Identity,
                bias=nCr_t[:, jc:jc + 1], scale=sor_t[:, jc:jc + 1]), "ss")
            wqo_reuse[bi] = t_p2
            w("gpsimd", t_p2)
            t_p2_prev = o("gpsimd", lambda e, jc=jc: e.dma_start(
                out=wout_d[jc, :, :], in_=wob16[:, :]), "wts", 16)
            yield

    def gen_win():
        wq_reuse = [None] * 8
        wi_prev = [None, None]
        t_sbc_read = [None]
        t_fold_done = [None]
        t_ac = None
        for half in range(2):
            c0 = half * 2048
            mprev = None
            for fc in range(8):
                w("sync", wq_reuse[fc])
                t_l = o("sync", lambda e, fc=fc, c0=c0: e.dma_start(
                    out=wq_h[:, fc, :], in_=w_in[fc * 128:(fc + 1) * 128, c0:c0 + 2048]),
                    "wql%d" % fc, 16)
                if fc == 0:
                    ws("scalar", [t_l, mprev, t_fold_done[0]])
                    mprev = o("scalar", lambda e, fc=fc: e.activation(
                        M_h[:, :], wq_h[:, fc, :], AF.Abs, bias=0.0, scale=1.0), "ss")
                else:
                    ws("scalar", [t_l, t_sbc_read[0]])
                    t_abs = o("scalar", lambda e, fc=fc: e.activation(
                        Sbc_h[:, :], wq_h[:, fc, :], AF.Abs, bias=0.0, scale=1.0), "ss")
                    ws("vector", [t_abs, mprev])
                    mprev = o("vector", lambda e: e.tensor_tensor(
                        M_h[:, :], M_h[:, :], Sbc_h[:, :], ALU.max), "sv")
                    t_sbc_read[0] = mprev
                yield
            slots = [ps3[:, 0:128], ps3[:, 128:256], ps4[:, 0:128], ps4[:, 128:256]]
            slot_tok = [psum_last.get("ps3"), psum_last.get("ps3"),
                        psum_last.get("ps4"), psum_last.get("ps4")]
            t_red = None
            for c in range(16):
                sl_ = c % 4
                ws("tensor", [mprev, t_idn, slot_tok[sl_]])
                t_tr = o("tensor", lambda e, c=c, sl_=sl_: e.transpose(
                    slots[sl_], M_h[:, c * 128:(c + 1) * 128], idn32_s[:, :]), "st")
                w("vector", t_tr)
                t_red = o("vector", lambda e, c=c, half=half, sl_=sl_: e.tensor_reduce(
                    A_t[:, half * 16 + c: half * 16 + c + 1], slots[sl_], AX.X, ALU.max), "sv")
                slot_tok[sl_] = t_red
                if c % 4 == 3:
                    yield
            t_fold_done[0] = t_red
            psum_last["ps3"] = t_red
            psum_last["ps4"] = t_red
            w("vector", t_red)
            t_ac = o("vector", lambda e, half=half: e.tensor_scalar(
                A_t[:, half * 16:(half + 1) * 16], A_t[:, half * 16:(half + 1) * 16],
                1e-8, None, ALU.max), "sv")
            ws("tensor", [t_ac, psum_last.get("ps3")])
            t_t2 = o("tensor", lambda e, half=half: e.transpose(
                ps3[0:16, 0:128], A_t[:, half * 16:(half + 1) * 16], idn32_s[:, :]), "st")
            w("vector", t_t2)
            t_cpt = o("vector", lambda e: e.tensor_copy(at_sb[:, :], ps3[0:16, 0:128]), "sv")
            psum_last["ps3"] = t_cpt
            w("gpsimd", t_cpt)
            t_fl1 = o("gpsimd", lambda e: e.dma_start(
                out=flip_d[0:1, 0:2048].rearrange("o (a b) -> (o a) b", b=128),
                in_=at_sb[:, :]), "flst", 16)
            w("sync", t_fl1)
            t_fl2 = o("sync", lambda e: e.dma_start(
                out=srow_h[:, :], in_=flip_d[0:1, 0:2048]), "flrd", 16)
            w("vector", t_fl2)
            t_sr1 = o("vector", lambda e: e.reciprocal(srow_h[:, :], srow_h[:, :]), "sv")
            w("vector", t_sr1)
            t_sr2 = o("vector", lambda e: e.tensor_scalar(
                srow_h[:, :], srow_h[:, :], 127.0, None, ALU.mult), "sv")
            yield
            t_sbc = None
            for q in range(4):
                ws("tensor", [t_sr2, t_ones, psum_last.get("ps4")])
                t_mmq = o("tensor", lambda e, q=q: e.matmul(
                    ps4[:, 0:512], ones_row[:, :], srow_h[:, q * 512:(q + 1) * 512],
                    start=True, stop=True), "st")
                w("vector", t_mmq)
                t_sbc = o("vector", lambda e, q=q: e.tensor_copy(
                    Sbc_h[:, q * 512:(q + 1) * 512], ps4[:, 0:512]), "sv")
                psum_last["ps4"] = t_sbc
            yield
            t_wi_last = None
            for fc in range(8):
                ws("vector", [t_sbc, wi_prev[fc % 2]])
                t_m1 = o("vector", lambda e, fc=fc: e.tensor_tensor(
                    wq_h[:, fc, :], wq_h[:, fc, :], Sbc_h[:, :], ALU.mult), "sv")
                t_sbc_read[0] = t_m1
                ws("vector", [t_m1, wi_prev[1 - fc % 2]])
                t_m2 = o("vector", lambda e, fc=fc: e.tensor_scalar(
                    wint_h[:, :], wq_h[:, fc, :], CMAGIC, CMAGIC, ALU.add, ALU.subtract), "sv")
                wq_reuse[fc] = t_m2
                w("gpsimd", t_m2)
                t_wi_last = o("gpsimd", lambda e, fc=fc, half=half: e.dma_start(
                    out=wint_d[half * 16:(half + 1) * 16, :, fc, :].rearrange("j p c -> p j c"),
                    in_=wint_h[:, :].rearrange("p (j c) -> p j c", c=128)), "wis", 16)
                wi_prev[fc % 2] = t_wi_last
                yield
        w("vector", t_ac)
        t_sjr = o("vector", lambda e: e.tensor_scalar(
            sjr_p[:, :], A_t[:, :], 1.0 / 127.0, None, ALU.mult), "sv")
        R["t_sjr_done"] = t_sjr

    def drive(gens):
        gens = list(gens)
        while gens:
            nxt = []
            for gobj in gens:
                try:
                    next(gobj)
                    nxt.append(gobj)
                except StopIteration:
                    pass
            gens = nxt

    drive([gen_ln1(), gen_precast(), gen_wout(), gen_win()])
    t_ln1_both = R["t_ln1_both"]
    t_precast = R["t_precast"]
    t_sjr_done = R["t_sjr_done"]

    kb.join()
    stA.close()

    stKV = contextlib.ExitStack()
    sbKV = mksb(stKV)
    K_fm = sbKV("K_fm", [128, NPAIR, NKEY], BF16)
    V_tm = sbKV("V_tm", [128, NKC, NH * 65], BF16)
    t_vones = o("vector", lambda e: e.memset(
        V_tm[:, :, :].rearrange("p k (h v) -> p k h v", v=65)[:, :, :, 64:65], 1.0), "sv")

    stB = contextlib.ExitStack()
    sbB = mksb(stB)
    lx_t = sbB("lx_t", [128, 2, 8, 512], BF16)
    cs_t = sbB("cs_t", [128, 2, 2, 512], BF16)
    rp_t0 = sbB("rp_t0", [128, 2, 512], BF16)
    rp_t1 = sbB("rp_t1", [128, 2, 512], BF16)
    rp_t2 = sbB("rp_t2", [128, 2, 512], BF16)
    kc_bf = sbB("kc_bf", [128, 2, 512], BF16)
    sq_t = sbB("sq_t", [128, 2, 512], BF16)
    stB1 = contextlib.ExitStack()
    sbB1 = mksb(stB1)
    ckv_t = sbB1("ckv_t", [128, 4, 512], BF16)
    wdkv_bf = sbB1("wdkv_bf", [128, 8, 512], BF16)
    wuv_bf = sbB1("wuv_bf", [128, 4, 1024], BF16)
    wuk_bf = sbB1("wuk_bf", [128, 4, 1024], BF16)
    wkrp = sbB1("wkrp", [128, 2, 8, 128], BF16)
    vB_bc = sbB1("vB_bc", [128, D_], BF16)
    row_b = sbB1("row_b", [1, 512])

    vb_toks = []
    t_rb = None
    for h in range(2):
        w("sync", t_rb)
        t_rb = o("sync", lambda e, h=h: e.dma_start(
            out=row_b[:, :], in_=bu_v_row[0:1, h * 512:(h + 1) * 512]), "rowb", 16)
        ws("tensor", [t_rb, t_ones, psum_last.get("ps4")])
        vb_toks.append(o("tensor", lambda e: e.matmul(
            ps4[:, 0:512], ones_row[:, :], row_b[:, :],
            start=True, stop=True), "st"))
        w("vector", vb_toks[-1])
        t_vB = o("vector", lambda e, h=h: e.tensor_copy(
            vB_bc[:, h * 512:(h + 1) * 512], ps4[:, 0:512]), "sv")
        psum_last["ps4"] = t_vB
        t_rb_tok = vb_toks[-1]
        t_rb = t_rb if False else vb_toks[-1]

    def loadw(dst, src_d, nrc, ncols, prev, sem="wlb1"):
        tok = prev
        for rc in range(nrc):
            tok = o("sync", lambda e, rc=rc: e.dma_start(
                out=dst[:, rc, :], in_=src_d[rc, :, 0:ncols]), sem, 16)
        return tok

    ws("sync", [t_precast])
    t_wb = loadw(wdkv_bf, wdkv_d, 8, 512, None)
    t_wb = loadw(wuv_bf, wuv_d, 4, 1024, t_wb)
    t_wb = loadw(wuk_bf, wuk_d, 4, 1024, t_wb)
    t_wB1 = t_wb

    lx_reuse = None
    cs_reuse = [None, None]
    wkr_reuse = [None, None]
    rope_gate = [None, None]
    ckv_reuse = None
    t_last_rope = None
    t_last_vev = None
    lx_read = [None, None]
    u4_tok = {}
    kn_i = [0]
    kn_sqd = []
    kn_mm_prev = [None]
    kn_rd_toks = []

    def pump_kn():
        if kn_sqd:
            p0, s80, t_sq0 = kn_sqd.pop(0)
            ws("tensor", [t_sq0, psum_last.get(("kr", 0)), t_ones2])
            t_mmn = o("tensor", lambda e: e.matmul(
                ps4[0:2, 0:512], ones2b[:, :], sq_t[:, 1, :], start=True, stop=True), "st")
            kn_mm_prev[0] = t_mmn
            psum_last[("kr", 0)] = t_mmn
            w("vector", t_mmn)
            t_rdn = o("vector", lambda e, p0=p0, s80=s80: e.tensor_reduce(
                knall[:, p0 * 8 + s80:p0 * 8 + s80 + 1], ps4[0:2, 0:512], AX.X, ALU.max), "sv")
            psum_last[("kr", 0)] = t_rdn
            kn_rd_toks.append(t_rdn)
        j = kn_i[0]
        if j < 64:
            s8j, pj = j // 8, j % 8
            if (pj, s8j) in u4_tok:
                ws("vector", [u4_tok[(pj, s8j)], kn_mm_prev[0]])
                t_sqn = o("vector", lambda e, pj=pj, s8j=s8j: e.scalar_tensor_tensor(
                    sq_t[:, 1, :], K_fm[:, pj, s8j * 512:(s8j + 1) * 512], 1.0,
                    K_fm[:, pj, s8j * 512:(s8j + 1) * 512], ALU.bypass, ALU.mult), "sv")
                kn_sqd.append((pj, s8j, t_sqn))
                kn_i[0] += 1

    for kt in range(NKT):
        lbi = kt % 2
        ws("sync", t_ln1_both + [lx_read[lbi]])
        ttr = None
        for fc in range(8):
            ttr = o("sync", lambda e, kt=kt, fc=fc, lbi=lbi: e.dma_start_transpose(
                lx_t[:, lbi, fc, :],
                ln1x_d[kt * 512:(kt + 1) * 512, fc * 128:(fc + 1) * 128]), "lxt%d" % lbi, 16)
        t_lx = ttr
        ckv_toks = []
        for cc in range(4):
            psl = [psA[:, 0:512], psA[:, 512:1024], psB[:, 0:512], psB[:, 512:1024]][cc]
            deps = [t_lx, t_wB1, psum_last.get(("pAB", cc)), ckv_reuse]
            if cc >= 2:
                deps += [psum_last.get(("kc", 1)), psum_last.get(("kr", 1)),
                         psum_last.get(("sw", 1))]
            ws("tensor", deps)
            tokm = None
            for fc in range(8):
                def mmf(e, cc=cc, fc=fc, psl=psl, lbi=lbi):
                    return e.matmul(psl, wdkv_bf[:, fc, cc * 128:(cc + 1) * 128],
                                    lx_t[:, lbi, fc, :], start=(fc == 0), stop=(fc == 7))
                tokm = o("tensor", mmf, "st") if fc == 7 else o("tensor", mmf)
            w("scalar", tokm)
            t_ev = o("scalar", lambda e, cc=cc, psl=psl: e.activation(
                ckv_t[:, cc, :], psl, AF.Identity, bias=bdkv_s[:, cc:cc + 1], scale=1.0), "ss")
            psum_last[("pAB", cc)] = t_ev
            ckv_toks.append(t_ev)
        for tc in range(4):
            kcix = kt * 4 + tc
            for h2 in range(2):
                psl = ps1[:, 0:512] if h2 == 0 else ps2[:, 0:512]
                ws("tensor", ckv_toks + [psum_last.get(("pV", h2)),
                                         psum_last.get(("sw", h2))])
                tokm = None
                for cc in range(4):
                    def mmv(e, cc=cc, tc=tc, h2=h2, psl=psl):
                        return e.matmul(psl, ckv_t[:, cc, tc * 128:(tc + 1) * 128],
                                        wuv_bf[:, cc, h2 * 512:(h2 + 1) * 512],
                                        start=(cc == 0), stop=(cc == 3))
                    tokm = o("tensor", mmv, "st") if cc == 3 else o("tensor", mmv)
                ws("vector", [tokm, t_vB, t_vones])
                vv = V_tm[:, kcix, :].rearrange("p (h v) -> p h v", v=65)
                t_vev = o("vector", lambda e, h2=h2, psl=psl, vv=vv: e.tensor_tensor(
                    vv[:, h2 * 8:(h2 + 1) * 8, 0:64],
                    psl.rearrange("p (h v) -> p h v", v=64),
                    vB_bc[:, h2 * 512:(h2 + 1) * 512].rearrange("p (h v) -> p h v", v=64),
                    ALU.add), "sv")
                psum_last[("pV", h2)] = t_vev
                t_last_vev = t_vev
        for p in range(NPAIR):
            cbi = p % 2
            if kt >= 1:
                pump_kn()
            KCP = ps3[:, 0:512] if cbi == 0 else psB[:, 0:512]
            KRP = ps4[:, 0:512] if cbi == 0 else psB[:, 512:1024]
            SWP = ps1[:, 0:512] if cbi == 0 else ps2[:, 0:512]
            w("sync", wkr_reuse[cbi])
            t_wk = o("sync", lambda e, p=p, cbi=cbi: e.dma_start(
                out=wkrp[:, cbi, :, :],
                in_=wkr_d[:, :, p * 128:(p + 1) * 128]), "wkl%d" % cbi, 16)
            w("sync", cs_reuse[cbi])
            t_c1 = o("sync", lambda e, p=p, kt=kt, cbi=cbi: e.dma_start(
                out=cs_t[:, cbi, 0, :],
                in_=cos_d[p * 128:(p + 1) * 128, kt * 512:(kt + 1) * 512]), "csl%d" % cbi, 16)
            t_c2 = o("sync", lambda e, p=p, kt=kt, cbi=cbi: e.dma_start(
                out=cs_t[:, cbi, 1, :],
                in_=sin_d[p * 128:(p + 1) * 128, kt * 512:(kt + 1) * 512]), "csl%d" % cbi, 16)
            ws("tensor", ckv_toks + [psum_last.get(("kc", cbi))])
            tok_kc = None
            for cc in range(4):
                def mmk(e, cc=cc, p=p, KCP=KCP):
                    return e.matmul(KCP, wuk_bf[:, cc, p * 128:(p + 1) * 128],
                                    ckv_t[:, cc, :], start=(cc == 0), stop=(cc == 3))
                tok_kc = o("tensor", mmk, "st") if cc == 3 else o("tensor", mmk)
            ws("scalar", [tok_kc, rope_gate[cbi]])
            t_kc = o("scalar", lambda e, p=p, cbi=cbi, KCP=KCP: e.activation(
                kc_bf[:, cbi, :], KCP, AF.Identity, bias=buk_s[:, p:p + 1], scale=1.0), "ss")
            psum_last[("kc", cbi)] = t_kc
            ws("tensor", [t_lx, t_wk, psum_last.get(("kr", cbi))])
            tok_kr = None
            for fc in range(8):
                def mmr(e, fc=fc, cbi=cbi, KRP=KRP, lbi=lbi):
                    return e.matmul(KRP, wkrp[:, cbi, fc, :],
                                    lx_t[:, lbi, fc, :], start=(fc == 0), stop=(fc == 7))
                tok_kr = o("tensor", mmr, "st") if fc == 7 else o("tensor", mmr)
            wkr_reuse[cbi] = tok_kr
            ws("scalar", [tok_kr, rope_gate[cbi]])
            t_kr = o("scalar", lambda e, p=p, cbi=cbi, KRP=KRP: e.activation(
                rp_t0[:, cbi, :], KRP, AF.Identity, bias=bkr_s[:, p:p + 1], scale=1.0), "ss")
            psum_last[("kr", cbi)] = t_kr
            ws("tensor", [t_kr, psum_last.get(("sw", cbi)),
                          psum_last.get(("pV", cbi)), t_small_loads])
            tok_sw = o("tensor", lambda e, cbi=cbi, SWP=SWP: e.matmul(
                SWP, pswap_s[:, :], rp_t0[:, cbi, :], start=True, stop=True), "st")
            ws("vector", [t_kr, tok_sw, t_c1, t_c2, rope_gate[cbi]])
            t_u1 = o("vector", lambda e, cbi=cbi: e.tensor_tensor(
                rp_t1[:, cbi, :], rp_t0[:, cbi, :], cs_t[:, cbi, 0, :], ALU.mult), "sv")
            t_u2 = o("vector", lambda e, cbi=cbi, SWP=SWP: e.tensor_tensor(
                rp_t2[:, cbi, :], SWP, cs_t[:, cbi, 1, :], ALU.mult), "sv")
            psum_last[("sw", cbi)] = t_u2
            ws("vector", [t_u1, t_u2])
            t_u3 = o("vector", lambda e, cbi=cbi: e.tensor_tensor(
                rp_t1[:, cbi, :], rp_t1[:, cbi, :], rp_t2[:, cbi, :], ALU.add), "sv")
            ws("vector", [t_u3, t_kc])
            t_u4 = o("vector", lambda e, p=p, kt=kt, cbi=cbi: e.tensor_tensor(
                K_fm[:, p, kt * 512:(kt + 1) * 512], rp_t1[:, cbi, :], kc_bf[:, cbi, :], ALU.add), "sv")
            cs_reuse[cbi] = t_u2
            rope_gate[cbi] = t_u4
            t_last_rope = t_u4
            u4_tok[(p, kt)] = t_u4
            if p == NPAIR - 1:
                lx_read[lbi] = tok_kr
        ckv_reuse = t_u4
    t_K_done = t_u4
    t_V_done = t_last_vev


    kb.join()
    stB1.close()
    ctq_t = sbB("ctq_t", [128, 4, 2, 512], BF16)
    wdq_bf = sbB("wdq_bf", [128, 8, 512], BF16)
    wuq_bf = sbB("wuq_bf", [128, 4, 1024], BF16)
    wqr_bf = sbB("wqr_bf", [128, 4, 1024], BF16)
    q_bf = sbB("q_bf", [128, 2, 512], BF16)

    t_wb2 = loadw(wdq_bf, wdq_d, 8, 512, t_wB1, "wlb2")
    t_wb2 = loadw(wuq_bf, wuq_d, 4, 1024, t_wb2, "wlb2")
    t_wb2 = loadw(wqr_bf, wqr_d, 4, 1024, t_wb2, "wlb2")

    t_qspill = None
    qsp_gate = [None, None]
    qsp_last = [None, None]
    pend_qn = [None]
    nsq_gate = [None, None]
    ct_read = [None, None]

    def emit_qnorm(p, qt, cbi, t_u4tok):
        ws("vector", [t_u4tok, t_qn_pr[0]])
        t_sq2 = o("vector", lambda e, cbi=cbi: e.scalar_tensor_tensor(
            sq_t[:, 0, :], q_bf[:, cbi, :], 1.0, q_bf[:, cbi, :], ALU.bypass, ALU.mult), "sv")
        nsq_gate[cbi] = t_sq2
        ws("tensor", [t_sq2, psum_last.get(("kc", 0))])
        t_mm = o("tensor", lambda e: e.matmul(
            ps3[0:2, 0:512], ones2b[:, :], sq_t[:, 0, :], start=True, stop=True), "st")
        psum_last[("kc", 0)] = t_mm
        w("vector", t_mm)
        t_rd = o("vector", lambda e, p=p, qt=qt: e.tensor_reduce(
            qn2[:, p * 4 + qt:p * 4 + qt + 1], ps3[0:2, 0:512], AX.X, ALU.max), "sv")
        psum_last[("kc", 0)] = t_rd
        t_qn_pr[0] = t_rd

    t_qn_pr = [None]
    for qt in range(NQT):
        lbi = (NKT + qt) % 2
        ws("sync", t_ln1_both + [lx_read[lbi]])
        ttr = None
        for fc in range(8):
            ttr = o("sync", lambda e, qt=qt, fc=fc, lbi=lbi: e.dma_start_transpose(
                lx_t[:, lbi, fc, :],
                ln1x_d[qt * 512:(qt + 1) * 512, fc * 128:(fc + 1) * 128]), "lxt%d" % lbi, 16)
        t_lx = ttr
        ctq_toks = []
        for cc in range(4):
            psl = [psA[:, 0:512], psA[:, 512:1024], psB[:, 0:512], psB[:, 512:1024]][cc]
            deps = [t_lx, t_wb2, psum_last.get(("pAB", cc))]
            if cc >= 2:
                deps += [psum_last.get(("kc", 1)), psum_last.get(("kr", 1)),
                         psum_last.get(("sw", 1))]
            ws("tensor", deps)
            tokm = None
            for fc in range(8):
                def mmf(e, cc=cc, fc=fc, psl=psl, lbi=lbi):
                    return e.matmul(psl, wdq_bf[:, fc, cc * 128:(cc + 1) * 128],
                                    lx_t[:, lbi, fc, :], start=(fc == 0), stop=(fc == 7))
                tokm = o("tensor", mmf, "st") if fc == 7 else o("tensor", mmf)
            ws("scalar", [tokm, ct_read[qt % 2]])
            t_ev = o("scalar", lambda e, cc=cc, qt=qt, psl=psl: e.activation(
                ctq_t[:, cc, qt % 2, :], psl, AF.Identity,
                bias=bdq_s[:, cc:cc + 1], scale=1.0), "ss")
            psum_last[("pAB", cc)] = t_ev
            ctq_toks.append(t_ev)
        lq_reuse = t_ev
        for p in range(NPAIR):
            cbi = p % 2
            QCP = ps3[:, 0:512] if cbi == 0 else psB[:, 0:512]
            QRP = ps4[:, 0:512] if cbi == 0 else psB[:, 512:1024]
            SWP = ps1[:, 0:512] if cbi == 0 else ps2[:, 0:512]
            if qt < 2:
                pump_kn()
            if pend_qn[0] is not None:
                emit_qnorm(*pend_qn[0])
                pend_qn[0] = None
            w("sync", cs_reuse[cbi])
            t_c1 = o("sync", lambda e, p=p, qt=qt, cbi=cbi: e.dma_start(
                out=cs_t[:, cbi, 0, :],
                in_=cos_d[p * 128:(p + 1) * 128, qt * 512:(qt + 1) * 512]), "csl%d" % cbi, 16)
            t_c2 = o("sync", lambda e, p=p, qt=qt, cbi=cbi: e.dma_start(
                out=cs_t[:, cbi, 1, :],
                in_=sin_d[p * 128:(p + 1) * 128, qt * 512:(qt + 1) * 512]), "csl%d" % cbi, 16)
            ws("tensor", ctq_toks + [psum_last.get(("kc", cbi))])
            tok_qc = None
            for cc in range(4):
                def mmk(e, cc=cc, p=p, qt=qt, QCP=QCP):
                    return e.matmul(QCP, wuq_bf[:, cc, p * 128:(p + 1) * 128],
                                    ctq_t[:, cc, qt % 2, :],
                                    start=(cc == 0), stop=(cc == 3))
                tok_qc = o("tensor", mmk, "st") if cc == 3 else o("tensor", mmk)
            ws("scalar", [tok_qc, rope_gate[cbi]])
            t_qc = o("scalar", lambda e, p=p, cbi=cbi, QCP=QCP: e.activation(
                kc_bf[:, cbi, :], QCP, AF.Identity, bias=buq_s[:, p:p + 1], scale=1.0), "ss")
            psum_last[("kc", cbi)] = t_qc
            ws("tensor", ctq_toks + [psum_last.get(("kr", cbi))])
            tok_qr = None
            for cc in range(4):
                def mmr(e, cc=cc, p=p, qt=qt, QRP=QRP):
                    return e.matmul(QRP, wqr_bf[:, cc, p * 128:(p + 1) * 128],
                                    ctq_t[:, cc, qt % 2, :],
                                    start=(cc == 0), stop=(cc == 3))
                tok_qr = o("tensor", mmr, "st") if cc == 3 else o("tensor", mmr)
            ws("scalar", [tok_qr, rope_gate[cbi]])
            t_qr = o("scalar", lambda e, p=p, cbi=cbi, QRP=QRP: e.activation(
                rp_t0[:, cbi, :], QRP, AF.Identity, bias=bqr_s[:, p:p + 1], scale=1.0), "ss")
            psum_last[("kr", cbi)] = t_qr
            ws("tensor", [t_qr, t_qc, psum_last.get(("sw", cbi))])
            tok_sw = o("tensor", lambda e, cbi=cbi, SWP=SWP: e.matmul(
                SWP, pswap_s[:, :], rp_t0[:, cbi, :], start=True, stop=True), "st")
            ws("vector", [t_qr, tok_sw, t_c1, t_c2, rope_gate[cbi]])
            t_u1 = o("vector", lambda e, cbi=cbi: e.tensor_tensor(
                rp_t1[:, cbi, :], rp_t0[:, cbi, :], cs_t[:, cbi, 0, :], ALU.mult), "sv")
            t_u2 = o("vector", lambda e, cbi=cbi, SWP=SWP: e.tensor_tensor(
                rp_t2[:, cbi, :], SWP, cs_t[:, cbi, 1, :], ALU.mult), "sv")
            psum_last[("sw", cbi)] = t_u2
            ws("vector", [t_u1, t_u2])
            t_u3 = o("vector", lambda e, cbi=cbi: e.tensor_tensor(
                rp_t1[:, cbi, :], rp_t1[:, cbi, :], rp_t2[:, cbi, :], ALU.add), "sv")
            ws("vector", [t_u3, t_qc, qsp_gate[cbi], nsq_gate[cbi]])
            t_u4 = o("vector", lambda e, cbi=cbi: e.tensor_tensor(
                q_bf[:, cbi, :], rp_t1[:, cbi, :], kc_bf[:, cbi, :], ALU.add), "sv")
            cs_reuse[cbi] = t_u2
            rope_gate[cbi] = t_u4
            if p == NPAIR - 1:
                lx_read[lbi] = tok_qr
                ct_read[qt % 2] = tok_qr
            w("gpsimd", t_u4)
            t_qspill = o("gpsimd", lambda e, p=p, qt=qt, cbi=cbi: e.dma_start(
                out=q_d[p, :, qt * 512:(qt + 1) * 512], in_=q_bf[:, cbi, :]), "qsp%d" % cbi, 16)
            qsp_gate[cbi] = t_qspill
            qsp_last[cbi] = t_qspill
            pend_qn[0] = (p, qt, cbi, t_u4)
    emit_qnorm(*pend_qn[0])
    pend_qn[0] = None
    while kn_sqd or kn_i[0] < 64:
        pump_kn()
    t_kn = None
    ws("vector", kn_rd_toks)
    for p in range(NPAIR):
        t_kn = o("vector", lambda e, p=p: e.tensor_reduce(
            kn2[:, p:p + 1], knall[:, p * 8:(p + 1) * 8], AX.X, ALU.max), "sv")
    t_knorm = t_kn
    t_rd = t_qn_pr[0]
    t_Q_done = [qsp_last[0], qsp_last[1]]
    t_qnorm = t_rd

    ws("scalar", [t_qnorm, t_knorm])
    ws("scalar", t_consts)
    t_b1 = o("scalar", lambda e: e.activation(qn2[:, :], qn2[:, :], AF.Ln, bias=tiny_b[0:2, :], scale=1.0), "ss")
    w("scalar", t_b1)
    t_b2 = o("scalar", lambda e: e.activation(qn2[:, :], qn2[:, :], AF.Exp, bias=0.0, scale=0.5), "ss")
    w("scalar", t_b2)
    t_b3 = o("scalar", lambda e: e.activation(kn2[:, :], kn2[:, :], AF.Ln, bias=tiny_b[0:2, :], scale=1.0), "ss")
    w("scalar", t_b3)
    t_b4 = o("scalar", lambda e: e.activation(kn2[:, :], kn2[:, :], AF.Exp, bias=0.0, scale=0.5), "ss")
    kn4v = kn4[:, :].rearrange("p (q m) -> p q m", m=4)
    w("vector", t_b4)
    t_b5 = None
    for mq in range(4):
        w("vector", t_b5)
        t_b5 = o("vector", lambda e, mq=mq, kn4v=kn4v: e.tensor_copy(
            kn4v[:, :, mq], kn2[:, :]), "sv")
    ws("vector", [t_b5, t_b2])
    t_b6 = o("vector", lambda e: e.tensor_tensor(cc_sc[:, :], qn2[:, :], kn4[:, :], ALU.mult), "sv")
    ws("tensor", [t_b6, t_o21, psum_last.get("ps4")])
    t_b7 = o("tensor", lambda e: e.matmul(
        ps4[0:1, 0:32], ones21[:, :], cc_sc[:, :], start=True, stop=True), "st")
    w("vector", t_b7)
    t_b8 = o("vector", lambda e: e.tensor_scalar(
        negc[:, :], ps4[0:1, 0:32], -INV_SCALE * 1.0001, None, ALU.mult), "sv")
    psum_last["ps4"] = t_b8
    ws("tensor", [t_b8, psum_last.get("ps3")])
    t_b9 = o("tensor", lambda e: e.matmul(
        ps3[:, 0:32], ones_row[:, :], negc[:, :], start=True, stop=True), "st")
    w("vector", t_b9)
    t_bias_c = o("vector", lambda e: e.tensor_copy(bias_c[:, :], ps3[:, 0:32]), "sv")
    psum_last["ps3"] = t_bias_c

    kb.join()
    stB.close()

    stC = contextlib.ExitStack()
    sbC = mksb(stC)
    O_fm = sbC("O_fm", [128, 2, 8, QMAC], BF16)
    P_pr = sbC("P_pr", [128, 2, 1024], BF16)
    Qm = sbC("Qm", [128, NPAIR, QMAC], BF16)
    Oraw = sbC("Oraw", [128, 2, 512])
    wo_bf = sbC("wo_bf", [128, 8, D_], BF16)
    woB_bc = sbC("woB_bc", [128, D_])
    w_sb = sbC("w_sb", [128, D_])
    xr_t = sbC("xr_t", [128, D_])
    taff = sbC("taff", [128, D_])
    r_sb = sbC("r_sb", [128, 512])

    t_wol = loadw(wo_bf, wo_d, 8, 1024, None, "wlc")
    t_rb2 = o("sync", lambda e: e.dma_start(out=xr_t[0:1, 0:1024], in_=bo_row), "rowb", 16)
    t_woB = None
    for h in range(2):
        ws("tensor", [t_rb2, t_ones, psum_last.get("ps4")])
        t_mmb = o("tensor", lambda e, h=h: e.matmul(
            ps4[:, 0:512], ones_row[:, :], xr_t[0:1, h * 512:(h + 1) * 512],
            start=True, stop=True), "st")
        w("vector", t_mmb)
        t_woB = o("vector", lambda e, h=h: e.tensor_copy(
            woB_bc[:, h * 512:(h + 1) * 512], ps4[:, 0:512]), "sv")
        psum_last["ps4"] = t_woB

    if dbg:
        dbgw_d = dint("dbgw_d", [NTT, 128, D_])
        dbgo_d = dint("dbgo_d", [NMACRO, 128, 8, QMAC], BF16)
        dbgp_d = dint("dbgp_d", [128, 1024], BF16)
        dbgs_d = dint("dbgs_d", [NMACRO, NPAIR, 2, 512])
    s_toks = {0: None, 1: None}
    p_toks = {0: None, 1: None}
    av_read = {0: None, 1: None}
    oraw_gate = {0: None, 1: None}
    qm_pair = [None] * NPAIR
    CS = {"t_r_ps": None, "t_wo_ps": t_woB, "t_wo_ps3": None, "o_read": [None, None],
          "t_af": None, "xr_gate": [t_woB], "t_x1s": None, "t_am2": None}
    evac_toks = {m: [] for m in range(NMACRO)}

    def load_qm_pair(mm, p, gate):
        ws("sync", t_Q_done + [gate])
        qm_pair[p] = o("sync", lambda e, p=p, mm=mm: e.dma_start(
            out=Qm[:, p, :], in_=q_d[p, :, mm * 512:(mm + 1) * 512]), "qml%d" % p, 16)

    def norm_recip(pn, j):
        if dbg:
            w("vector", pn["cps"][j])
            t_dsum = o("vector", lambda e, j=j: e.tensor_copy(
                r_sb[0:1, 0:512], Oraw[64:65, j, :]), "sv")
            w("gpsimd", t_dsum)
            o("gpsimd", lambda e, m=pn["m"], p=pn["p"], j=j: e.dma_start(
                out=dbgs_d[m, p, j, :].rearrange("(o f) -> o f", o=1),
                in_=r_sb[0:1, 0:512]), "dbgs", 16)
            ws("vector", [t_dsum])
        ws("vector", [pn["cps"][j], CS["t_r_ps"], CS.get("t_oe_prev")])
        pn["rc%d" % j] = o("vector", lambda e, j=j: e.reciprocal(
            r_sb[0:1, :], Oraw[64:65, j, :]), "sv")

    def norm_bcast(pn, j):
        om = pn["m"] % 2
        ws("tensor", [pn["rc%d" % j], psum_last.get("ps3")])
        t_ro = o("tensor", lambda e: e.matmul(
            ps3[:, 0:512], ones_row[:, :], r_sb[0:1, :],
            start=True, stop=True), "st")
        w("vector", t_ro)
        t_rcp = o("vector", lambda e: e.tensor_copy(r_sb[:, :], ps3[:, 0:512]), "sv")
        psum_last["ps3"] = t_rcp
        CS["t_r_ps"] = t_rcp
        ws("vector", [t_rcp, CS["o_read"][om]])
        t_oe = o("vector", lambda e, p=pn["p"], j=j, om=om: e.tensor_tensor(
            O_fm[j * 64:(j + 1) * 64, om, p, :], Oraw[0:64, j, :],
            r_sb[0:64, 0:512], ALU.mult), "sv")
        oraw_gate[j] = t_oe
        CS["t_oe_prev"] = t_oe
        evac_toks[pn["m"]].append(t_oe)

    def emit_epi(m, tc4):
        om = m % 2
        tcg = m * 4 + tc4
        wo_half_toks = []
        tokm = None
        for h2 in range(2):
            WOP = ps4[:, 0:512] if h2 == 0 else ps3[:, 0:512]
            wkey = "ps4" if h2 == 0 else "ps3"
            prv = CS["t_wo_ps"] if h2 == 0 else CS["t_wo_ps3"]
            ws("tensor", evac_toks[m] + [t_wol, prv, psum_last.get(wkey)])
            for dc in range(8):
                def mmw(e, dc=dc, tc4=tc4, h2=h2, om=om, WOP=WOP):
                    return e.matmul(WOP, O_fm[:, om, dc, tc4 * 128:(tc4 + 1) * 128],
                                    wo_bf[:, dc, h2 * 512:(h2 + 1) * 512],
                                    start=(dc == 0), stop=(dc == 7))
                tokm = o("tensor", mmw, "st") if dc == 7 else o("tensor", mmw)
            ws("vector", [tokm, t_woB, CS["t_af"]])
            t_wv = o("vector", lambda e, h2=h2, WOP=WOP: e.tensor_tensor(
                w_sb[:, h2 * 512:(h2 + 1) * 512], WOP,
                woB_bc[:, h2 * 512:(h2 + 1) * 512], ALU.add), "sv")
            if h2 == 0:
                CS["t_wo_ps"] = t_wv
            else:
                CS["t_wo_ps3"] = t_wv
            psum_last[wkey] = t_wv
            wo_half_toks.append(t_wv)
        CS["o_read"][om] = tokm
        if dbg:
            ws("gpsimd", wo_half_toks)
            t_dw = o("gpsimd", lambda e, tcg=tcg: e.dma_start(
                out=dbgw_d[tcg], in_=w_sb[:, :]), "dbgs", 16)
            ws("vector", [t_dw])
            ws("scalar", [t_dw])
        ws("vector", wo_half_toks)
        t_ln2, rstd2_ap, nmr2_ap, g2 = ln_stats_from(w_sb[:, :], [CS["t_af"], CS["t_am2"]])
        ws("sync", CS["xr_gate"])
        t_xl = o("sync", lambda e, tcg=tcg: e.dma_start(
            out=xr_t[:, :], in_=x_in[tcg * 128:(tcg + 1) * 128, :]), "xrl", 16)
        ws("scalar", [t_ln2, CS["t_am2"]])
        t_af = o("scalar", lambda e, rstd2_ap=rstd2_ap, nmr2_ap=nmr2_ap: e.activation(
            taff[:, :], w_sb[:, :], AF.Identity, bias=nmr2_ap, scale=rstd2_ap), "ss")
        CS["t_af"] = t_af
        ws("vector", [t_af, t_xl, CS["t_x1s"]])
        t_x1 = o("vector", lambda e: e.tensor_tensor(
            xr_t[:, :], xr_t[:, :], taff[:, :], ALU.add), "sv")
        w("gpsimd", t_x1)
        CS["t_x1s"] = o("gpsimd", lambda e, tcg=tcg: e.dma_start(
            out=x1_d[tcg * 128:(tcg + 1) * 128, :], in_=xr_t[:, :]), "x1s", 16)
        w("vector", t_x1)
        ln_mark_readers(g2, [t_af, t_x1])
        t_ln3, rstd3_ap, nmr3_ap, g3 = ln_stats_from(xr_t[:, :], [t_af, t_x1])
        w("vector", t_ln3)
        t_sv3 = o("vector", lambda e, tcg=tcg, rstd3_ap=rstd3_ap: e.tensor_copy(
            rstd3a[:, tcg:tcg + 1], rstd3_ap), "sv")
        t_sn3 = o("vector", lambda e, tcg=tcg, nmr3_ap=nmr3_ap: e.tensor_copy(
            nmr3a[:, tcg:tcg + 1], nmr3_ap), "sv")
        ws("vector", [t_sv3, t_sn3, t_af, t_x1])
        t_x3 = o("vector", lambda e, rstd3_ap=rstd3_ap, nmr3_ap=nmr3_ap: e.tensor_scalar(
            taff[:, :], xr_t[:, :], rstd3_ap, nmr3_ap, ALU.mult, ALU.add), "sv")
        CS["xr_gate"] = [CS["t_x1s"], t_ln3, t_x3]
        w("vector", t_x3)
        t_am = o("vector", lambda e, g3=g3: e.tensor_reduce(
            s_t1[:, g3:g3 + 1], taff[:, :], AX.X, ALU.max,
            apply_absolute_value=True), "sv")
        ws("vector", [t_am, t_amz])
        CS["t_am2"] = o("vector", lambda e, g3=g3: e.tensor_tensor(
            amax_r[:, :], amax_r[:, :], s_t1[:, g3:g3 + 1], ALU.max), "sv")
        ln_mark_readers(g3, [t_sv3, t_sn3, t_x3, t_am])

    for p in range(NPAIR):
        load_qm_pair(0, p, None)
    pend_norm = None
    pend_epi = []
    for m in range(NMACRO):
        for p in range(NPAIR):
            hA, hB = 2 * p, 2 * p + 1
            t_avA = t_avB = None
            pend = [None]

            def emit_av(kc, t_ex):
                nonlocal t_avA, t_avB
                sbix = kc % 2
                av_deps = [t_ex]
                if kc == 0:
                    av_deps += [av_read[0], av_read[1]]
                ws("tensor", av_deps)
                t_avA = o("tensor", lambda e, kc=kc, hA=hA, sbix=sbix: e.matmul(
                    ps1[0:65, 0:512], V_tm[:, kc, hA * 65:hA * 65 + 65],
                    P_pr[:, sbix, 0:512],
                    start=(kc == 0), stop=(kc == NKC - 1)), "st")
                t_avB = o("tensor", lambda e, kc=kc, hB=hB, sbix=sbix: e.matmul(
                    ps2[0:65, 0:512], V_tm[:, kc, hB * 65:hB * 65 + 65],
                    P_pr[:, sbix, 512:1024],
                    start=(kc == 0), stop=(kc == NKC - 1)), "st")
                p_toks[sbix] = t_avB

            for kc in range(NKC):
                sbix = kc % 2
                S = psA if sbix == 0 else psB
                ws("tensor", [s_toks[sbix], qm_pair[p], t_K_done, t_bias_c])
                o("tensor", lambda e, p=p, kc=kc, S=S: e.matmul(
                    S[:, 0:512], K_fm[0:64, p, kc * 128:(kc + 1) * 128],
                    Qm[0:64, p, :], start=True, stop=True, tile_position=(0, 0)))
                t_qk = o("tensor", lambda e, p=p, kc=kc, S=S: e.matmul(
                    S[:, 512:1024], K_fm[64:128, p, kc * 128:(kc + 1) * 128],
                    Qm[64:128, p, :], start=True, stop=True, tile_position=(64, 0)), "st")
                if kc == NKC - 1 and m + 1 < NMACRO:
                    load_qm_pair(m + 1, p, t_qk)
                ws("scalar", [t_qk, p_toks[sbix]])
                t_ex = o("scalar", lambda e, p=p, m=m, S=S, sbix=sbix: e.activation(
                    P_pr[:, sbix, :], S[:, :], AF.Exp,
                    bias=bias_c[:, p * 4 + m:p * 4 + m + 1], scale=INV_SCALE), "ss")
                s_toks[sbix] = t_ex
                if dbg and m == 0 and p == 0 and kc == 0:
                    w("gpsimd", t_ex)
                    t_dp = o("gpsimd", lambda e, sbix=sbix: e.dma_start(
                        out=dbgp_d[:, :], in_=P_pr[:, sbix, :]), "dbgs", 16)
                    ws("scalar", [t_dp])
                if pend[0] is not None:
                    emit_av(*pend[0])
                pend[0] = (kc, t_ex)
                if kc == 2 and pend_norm is not None:
                    norm_recip(pend_norm, 0)
                if kc == 6 and pend_norm is not None:
                    norm_bcast(pend_norm, 0)
                    norm_recip(pend_norm, 1)
                if kc == 10 and pend_norm is not None:
                    norm_bcast(pend_norm, 1)
                    pend_norm = None
                if kc == 16 and pend_epi:
                    emit_epi(*pend_epi.pop(0))
            emit_av(*pend[0])
            t_cps = []
            for j, (avps, t_av) in enumerate([(ps1, t_avA), (ps2, t_avB)]):
                ws("vector", [t_av, oraw_gate[j]])
                t_cp = o("vector", lambda e, j=j, avps=avps: e.tensor_copy(
                    Oraw[0:65, j, :], avps[0:65, 0:512]), "sv")
                av_read[j] = t_cp
                t_cps.append(t_cp)
            pend_norm = {"m": m, "p": p, "cps": t_cps}
        pend_epi += [(m, tc4) for tc4 in range(4)]
    norm_recip(pend_norm, 0)
    norm_bcast(pend_norm, 0)
    norm_recip(pend_norm, 1)
    norm_bcast(pend_norm, 1)
    pend_norm = None
    for item in pend_epi:
        emit_epi(*item)
    pend_epi = []
    t_x1s = CS["t_x1s"]
    t_am2 = CS["t_am2"]
    t_x1_last = t_x1s
    t_amax_done = t_am2

    kb.join()
    stC.close()
    stKV.close()

    stD = contextlib.ExitStack()
    sbD = mksb(stD)
    xsimT = sbD("xsimT", [128, 8, NTOK], BF16)
    wjt = sbD("wjt", [128, 2, 8, 128], BF16)
    wos_s = sbD("wos_s", [128, 32, D_], BF16)
    hg_t = sbD("hg_t", [128, 2, 512], BF16)
    hq_f = sbD("hq_f", [128, 4, 512], BF16)
    hq_w = sbD("hq_w", [128, 2, 512])
    hq_bf = sbD("hq_bf", [128, 4, 512], BF16)
    xq_t = sbD("xq_t", [128, 2, D_])
    xq_bf = sbD("xq_bf", [128, 2, D_], BF16)
    y_sb = sbD("y_sb", [128, D_])
    boB_bc = sbD("boB_bc", [128, D_])
    w_sb2 = sbD("w_sb2", [128, D_])
    xr_t2 = sbD("xr_t2", [128, D_])
    taff2 = sbD("taff2", [128, D_])

    def amax_collective(src_col, cc_in_h, cc_out_h, dst_s, dst_inv, prev_toks):
        ws("gpsimd", prev_toks)
        t_f1 = o("gpsimd", lambda e: e.dma_start(
            out=flip_d[0:1, 0:128], in_=src_col[:, :]), "flst", 16)
        w("sync", t_f1)
        t_f2 = o("sync", lambda e: e.dma_start(out=grow[:, :], in_=flip_d[0:1, 0:128]), "flrd", 16)
        w("gpsimd", t_f2)
        t_c1 = o("gpsimd", lambda e: e.dma_start(out=cc_in_h.ap(), in_=grow[:, :]), "ccb", 16)
        w("gpsimd", t_c1)
        t_cc = o("gpsimd", lambda e: e.collective_compute(
            "AllReduce", ALU.max, replica_groups=[list(range(8))],
            ins=[cc_in_h.ap().opt()], outs=[cc_out_h.ap().opt()]), "sg")
        w("gpsimd", t_cc)
        t_c2 = o("gpsimd", lambda e: e.dma_start(out=grow[:, :], in_=cc_out_h.ap()), "ccb", 16)
        w("vector", t_c2)
        t_g1 = o("vector", lambda e: e.tensor_reduce(sc_a[:, :], grow[:, :], AX.X, ALU.max), "sv")
        w("vector", t_g1)
        t_g2 = o("vector", lambda e: e.tensor_scalar(sc_a[:, :], sc_a[:, :], 1e-8, None, ALU.max), "sv")
        w("vector", t_g2)
        t_g3 = o("vector", lambda e: e.reciprocal(sc_c[:, :], sc_a[:, :]), "sv")
        w("vector", t_g3)
        t_g4 = o("vector", lambda e: e.tensor_scalar(
            sc_b[:, :], sc_c[:, :], QMAX_A16, None, ALU.mult), "sv")
        w("vector", t_g4)
        t_g6 = o("vector", lambda e: e.tensor_scalar(
            sc_c[:, :], sc_a[:, :], 1.0 / QMAX_A16, None, ALU.mult), "sv")
        ws("tensor", [t_g6, psum_last.get("ps3")])
        o("tensor", lambda e: e.matmul(ps3[:, 0:1], ones_row[:, :], sc_b[:, :], start=True, stop=True))
        t_g8 = o("tensor", lambda e: e.matmul(ps3[:, 1:2], ones_row[:, :], sc_c[:, :], start=True, stop=True), "st")
        w("vector", t_g8)
        t_g9 = o("vector", lambda e: e.tensor_copy(dst_s[:, :], ps3[:, 0:1]), "sv")
        w("vector", t_g9)
        t_g10 = o("vector", lambda e: e.tensor_copy(dst_inv[:, :], ps3[:, 1:2]), "sv")
        psum_last["ps3"] = t_g10
        return t_g10

    t_sx = amax_collective(amax_r, cc1_in_h, cc1_out_h, sxb, invsxb, [t_amax_done])

    xq_g3 = [None, None]
    xq_g4 = [None, None]
    xq_st = [None, None]
    for tcg in range(NTT):
        bq = tcg % 2
        ws("sync", [xq_g4[bq]])
        t_l = o("sync", lambda e, tcg=tcg, bq=bq: e.dma_start(
            out=xq_t[:, bq, :], in_=x1_d[tcg * 128:(tcg + 1) * 128, :]), "xql%d" % bq, 16)
        ws("vector", [t_l, t_sx, xq_g3[bq]])
        t_q1 = o("vector", lambda e, tcg=tcg, bq=bq: e.tensor_tensor(
            q2sc[:, bq:bq + 1], rstd3a[:, tcg:tcg + 1], sxb[:, :], ALU.mult), "sv")
        t_q2 = o("vector", lambda e, tcg=tcg, bq=bq: e.tensor_tensor(
            q2bi[:, bq:bq + 1], nmr3a[:, tcg:tcg + 1], sxb[:, :], ALU.mult), "sv")
        w("vector", t_q2)
        t_q2b = o("vector", lambda e, bq=bq: e.tensor_scalar(
            q2bi[:, bq:bq + 1], q2bi[:, bq:bq + 1], CMAGIC, None, ALU.add), "sv")
        ws("scalar", [t_l, t_q1, t_q2b, xq_g4[bq]])
        t_q3 = o("scalar", lambda e, bq=bq: e.activation(
            xq_t[:, bq, :], xq_t[:, bq, :], AF.Identity,
            bias=q2bi[:, bq:bq + 1], scale=q2sc[:, bq:bq + 1]), "ss")
        xq_g3[bq] = t_q3
        ws("vector", [t_q3, xq_st[bq]])
        t_q4 = o("vector", lambda e, bq=bq: e.tensor_scalar(
            xq_bf[:, bq, :], xq_t[:, bq, :], CMAGIC, invsxb[:, :], ALU.subtract, ALU.mult), "sv")
        xq_g4[bq] = t_q4
        w("gpsimd", t_q4)
        t_xq_tok = o("gpsimd", lambda e, tcg=tcg, bq=bq: e.dma_start(
            out=xsim_d[tcg * 128:(tcg + 1) * 128, :], in_=xq_bf[:, bq, :]), "xss%d" % bq, 16)
        xq_st[bq] = t_xq_tok
    ws("scalar", [xq_st[0], xq_st[1]])
    t_xt = None
    for fc in range(8):
        t_xt = o("scalar", lambda e, fc=fc: e.dma_start_transpose(
            xsimT[:, fc, :], xsim_d[:, fc * 128:(fc + 1) * 128]), "dsc", 16)
    t_xsimT = t_xt

    t_wosl = None
    for jc in range(32):
        t_wosl = o("sync", lambda e, jc=jc: e.dma_start(
            out=wos_s[:, jc, :], in_=wout_d[jc, :, :]), "wosl", 16)
    t_wos = t_wosl

    hb_reuse = [None, None]
    hm_reuse = [None, None]
    wj_reuse = [None, None]
    mm1_ps = {}
    t_hm2 = None
    t_hs = None
    for jc in range(32):
        bj = jc % 2
        w("sync", wj_reuse[bj])
        t_wl = o("sync", lambda e, jc=jc, bj=bj: e.dma_start(
            out=wjt[:, bj, :, :], in_=wint_d[jc, :, :, :]), "wjl%d" % bj, 16)
        for t4 in range(4):
            psl = [psA[:, 0:512], psA[:, 512:1024], psB[:, 0:512], psB[:, 512:1024]][t4]
            ws("tensor", [t_wl, t_xsimT, mm1_ps.get(t4)])
            tokm = None
            for fc in range(8):
                def mm1(e, fc=fc, t4=t4, bj=bj, psl=psl):
                    return e.matmul(psl, wjt[:, bj, fc, :],
                                    xsimT[:, fc, t4 * 512:(t4 + 1) * 512],
                                    start=(fc == 0), stop=(fc == 7))
                tokm = o("tensor", mm1, "st") if fc == 7 else o("tensor", mm1)
            if t4 == 3:
                wj_reuse[bj] = tokm
            bh = t4 % 2
            ws("scalar", [tokm, hb_reuse[bh], t_sjr_done, hm_reuse[bh]])
            t_ge = o("scalar", lambda e, jc=jc, bh=bh, psl=psl: e.activation(
                hg_t[:, bh, :], psl, AF.Gelu, bias=bin_s[:, jc:jc + 1],
                scale=sjr_p[:, jc:jc + 1]), "ss")
            mm1_ps[t4] = t_ge
            ws("vector", [t_ge, t_hm2])
            t_hm = o("vector", lambda e, bh=bh: e.tensor_reduce(
                hmt_s[:, :], hg_t[:, bh, :], AX.X, ALU.max, apply_absolute_value=True), "sv")
            hm_reuse[bh] = t_hm
            ws("vector", [t_hm, t_hmz])
            t_hm2 = o("vector", lambda e: e.tensor_tensor(
                hmax_r[:, :], hmax_r[:, :], hmt_s[:, :], ALU.max), "sv")
            w("gpsimd", t_ge)
            t_hs = o("gpsimd", lambda e, jc=jc, t4=t4, bh=bh: e.dma_start(
                out=hid_d[jc, :, t4 * 512:(t4 + 1) * 512], in_=hg_t[:, bh, :]), "his%d" % bh, 16)
            hb_reuse[bh] = t_hs
    t_hid_done = t_hs
    t_hmax_done = t_hm2

    kb.join()
    t_sh = amax_collective(hmax_r, cc2_in_h, cc2_out_h, shb, invshb, [t_hmax_done])

    w("gpsimd", t_sh)
    t_bor = o("gpsimd", lambda e: e.dma_start(out=xq_t[0:1, 0, :], in_=bout_row), "rowb2", 16)
    bo_toks = []
    for h2 in range(2):
        ws("tensor", [t_bor, psum_last.get("ps3"), psum_last.get("ps4"), t_sh])
        bo_toks.append(o("tensor", lambda e, h2=h2: e.matmul(
            ps3[:, 0:512] if h2 == 0 else ps4[:, 0:512], ones_row[:, :],
            xq_t[0:1, 0, h2 * 512:(h2 + 1) * 512], start=True, stop=True), "st"))
    ws("vector", bo_toks)
    t_boB = o("vector", lambda e: e.tensor_copy(boB_bc[:, 0:512], ps3[:, 0:512]), "sv")
    t_boB2 = o("vector", lambda e: e.tensor_copy(boB_bc[:, 512:1024], ps4[:, 0:512]), "sv")
    psum_last["ps3"] = t_boB
    psum_last["ps4"] = t_boB2

    ps_pairs = [(psA[:, 0:512], psA[:, 512:1024]),
                (psB[:, 0:512], psB[:, 512:1024]),
                (ps1[:, 0:512], ps2[:, 0:512]),
                (ps3[:, 0:512], ps4[:, 0:512])]
    hqf_reuse = [None, None, None, None]
    hqw_gate = [None, None]
    mm2_last = {}
    y_store = []
    t_ln4_prev = None
    t_y_prev = None
    t_ys = None
    for tb in range(4):
        mm2_toks = {}
        for jc in range(32):
            bh = jc % 4
            w("sync", hqf_reuse[bh])
            t_hl = o("sync", lambda e, jc=jc, tb=tb, bh=bh: e.dma_start(
                out=hq_f[:, bh, :], in_=hid_d[jc, :, tb * 512:(tb + 1) * 512]), "hql%d" % bh, 16)
            bw = bh % 2
            ws("scalar", [t_hl, t_sh, hqw_gate[bw]])
            t_tq = o("scalar", lambda e, bh=bh, bw=bw: e.activation(
                hq_w[:, bw, :], hq_f[:, bh, :], AF.Identity, bias=cmag_b[:, :], scale=shb[:, :]), "ss")
            ws("vector", [t_tq, hqf_reuse[bh]])
            t_hq = o("vector", lambda e, bh=bh, bw=bw: e.tensor_scalar(
                hq_bf[:, bh, :], hq_w[:, bw, :], CMAGIC, invshb[:, :],
                ALU.subtract, ALU.mult), "sv")
            hqw_gate[bw] = t_hq
            tk = None
            for tc4 in range(4):
                pa, pb = ps_pairs[tc4]
                deps = [t_hq, t_wos]
                if jc == 0:
                    deps.append(mm2_last.get(tc4))
                ws("tensor", deps)
                def mma(e, jc=jc, tc4=tc4, bh=bh, pa=pa):
                    return e.matmul(pa, hq_bf[:, bh, tc4 * 128:(tc4 + 1) * 128],
                                    wos_s[:, jc, 0:512], start=(jc == 0), stop=(jc == 31))
                def mmb(e, jc=jc, tc4=tc4, bh=bh, pb=pb):
                    return e.matmul(pb, hq_bf[:, bh, tc4 * 128:(tc4 + 1) * 128],
                                    wos_s[:, jc, 512:1024], start=(jc == 0), stop=(jc == 31))
                o("tensor", mma)
                tk = o("tensor", mmb, "st")
                if jc == 31:
                    mm2_toks[tc4] = tk
            hqf_reuse[bh] = tk
        for tc4 in range(4):
            tcg = tb * 4 + tc4
            pa, pb = ps_pairs[tc4]
            ws("vector", [mm2_toks[tc4], t_boB, t_boB2, t_ys, t_ln4_prev])
            t_w1 = o("vector", lambda e, pa=pa: e.tensor_tensor(
                w_sb2[:, 0:512], pa, boB_bc[:, 0:512], ALU.add), "sv")
            w("vector", t_w1)
            t_w2 = o("vector", lambda e, pb=pb: e.tensor_tensor(
                w_sb2[:, 512:1024], pb, boB_bc[:, 512:1024], ALU.add), "sv")
            mm2_last[tc4] = t_w2
            w("vector", t_w2)
            t_ln4, rstd4_ap, nmr4_ap, g4 = ln_stats_from(w_sb2[:, :], [t_ln4_prev, t_y_prev])
            w("sync", t_y_prev)
            t_xl = o("sync", lambda e, tcg=tcg: e.dma_start(
                out=xr_t2[:, :], in_=x1_d[tcg * 128:(tcg + 1) * 128, :]), "xr2", 16)
            ws("scalar", [t_ln4, t_y_prev])
            t_af = o("scalar", lambda e, rstd4_ap=rstd4_ap, nmr4_ap=nmr4_ap: e.activation(
                taff2[:, :], w_sb2[:, :], AF.Identity, bias=nmr4_ap, scale=rstd4_ap), "ss")
            t_ln4_prev = t_af
            ws("vector", [t_af, t_xl])
            t_y = o("vector", lambda e: e.tensor_tensor(
                y_sb[:, :], xr_t2[:, :], taff2[:, :], ALU.add), "sv")
            t_y_prev = t_y
            ln_mark_readers(g4, [t_af, t_y])
            w("gpsimd", t_y)
            t_ys = o("gpsimd", lambda e, tcg=tcg: e.dma_start(
                out=y_out[tcg * 128:(tcg + 1) * 128, :], in_=y_sb[:, :]), "yst", 16)
            y_store.append(t_ys)

    if dbg:
        dbg_specs = [
            ("dbg_w", dbgw_d, [NTT, 128, D_], F32),
            ("dbg_o", dbgo_d, [NMACRO, 128, 8, QMAC], BF16),
            ("dbg_p00", dbgp_d, [128, 1024], BF16),
            ("dbg_sums", dbgs_d, [NMACRO, NPAIR, 2, 512], F32),
            ("dbg_ln1x", ln1x_d, [NKEY, D_], BF16),
            ("dbg_x1", x1_d, [NTOK, D_], F32),
            ("dbg_xsim", xsim_d, [NTOK, D_], BF16),
            ("dbg_q", q_d, [NPAIR, 128, NTOK], BF16),
            ("dbg_hid0", hid_d[0], [128, NTOK], BF16),
            ("dbg_hid7", hid_d[7], [128, NTOK], BF16),
            ("dbg_wint0", wint_d[0], [8, 128, 128], BF16),
            ("dbg_wout0", wout_d[0], [128, D_], BF16),
        ]
        ws("gpsimd", [y_store[-1], t_x1_last, t_hid_done, t_xq_tok])
        for nm, srcap, shp, dt in dbg_specs:
            tdg = o("gpsimd", lambda e, nm=nm, srcap=srcap, shp=shp, dt=dt: e.dma_start(
                out=nc.dram_tensor(nm, shp, dt, kind="ExternalOutput").ap(), in_=srcap), "dbgs", 16)
        for nm, srcap, shp in [("dbg_biasc", bias_c[:, :], [128, 32]),
                               ("dbg_qn2", qn2[:, :], [2, 32]),
                               ("dbg_kn2", kn2[:, :], [2, NPAIR]),
                               ("dbg_sjr", sjr_p[:, :], [128, 32]),
                               ("dbg_sxb", sxb[:, :], [128, 1]),
                               ("dbg_shb", shb[:, :], [128, 1]),
                               ("dbg_rstd3", rstd3a[:, :], [128, NTT]),
                               ("dbg_nmr3", nmr3a[:, :], [128, NTT])]:
            tdg = o("gpsimd", lambda e, nm=nm, srcap=srcap, shp=shp: e.dma_start(
                out=nc.dram_tensor(nm, shp, F32 if "wint" not in nm else BF16,
                                   kind="ExternalOutput").ap(), in_=srcap), "dbgs", 16)
        w("gpsimd", tdg)

    ws("gpsimd", [y_store[-1], t_x1_last, t_hid_done])

    with nc.Block() as block:
        block.sync(kb.runner("sync"))
        block.vector(kb.runner("vector"))
        block.scalar(kb.runner("scalar"))
        block.tensor(kb.runner("tensor"))
        block.gpsimd(kb.runner("gpsimd"))
    stD.close()
    kb.stack.close()
    return nc


def _headperm():
    pi = np.empty(D_, np.int64)
    for h in range(NH):
        for j in range(32):
            pi[64 * h + j] = 64 * h + 2 * j
            pi[64 * h + 32 + j] = 64 * h + 2 * j + 1
    return pi


def _rope_tables(positions):
    pos = np.asarray(positions, np.float32)
    ids = np.arange(512, dtype=np.float32)
    theta = np.float32(10000.0) ** (np.float32(-2.0) * ids) / np.float32(512.0)
    f = np.arange(D_)
    c = f // 128
    r = f % 128
    head = 2 * c + r // 64
    j = r % 64
    pg = 32 * head + (j % 32)
    ang = pos[None, :] * theta[pg][:, None]
    cos = np.cos(ang).astype(np.float32)
    sin = np.sin(ang).astype(np.float32)
    sin[j < 32, :] = -sin[j < 32, :]
    return (cos.astype(ml_dtypes.bfloat16), sin.astype(ml_dtypes.bfloat16))


def _pswap():
    m = np.zeros((128, 128), np.float32)
    for r in range(128):
        jj = r % 64
        partner = r + 32 if jj < 32 else r - 32
        m[partner, r] = 1.0
    return m.astype(ml_dtypes.bfloat16)


def _bias_pm(b, chunks):
    return np.ascontiguousarray(np.asarray(b, np.float32).reshape(chunks, 128).T)


_NC_CACHE = [None]


def kernel(L0_attn_x, L0_attn_Wd_q_w, L0_attn_Wd_q_b, L0_attn_Wu_q_w, L0_attn_Wu_q_b,
           L0_attn_Wq_r_w, L0_attn_Wq_r_b, L0_attn_Wk_r_w, L0_attn_Wk_r_b,
           L0_attn_Wd_kv_w, L0_attn_Wd_kv_b, L0_attn_Wu_k_w, L0_attn_Wu_k_b,
           L0_attn_Wu_v_w, L0_attn_Wu_v_b, L0_attn_Wo_w, L0_attn_Wo_b,
           L0_ffn_W_in, L0_ffn_b_in, L0_ffn_W_out, L0_ffn_b_out):
    f32 = lambda a: np.ascontiguousarray(np.asarray(a), dtype=np.float32)
    x = f32(L0_attn_x)
    pi = _headperm()
    shared = {
        "wd_q": f32(L0_attn_Wd_q_w), "wd_kv": f32(L0_attn_Wd_kv_w),
        "wu_q": np.ascontiguousarray(f32(L0_attn_Wu_q_w)[:, pi]),
        "wq_r": np.ascontiguousarray(f32(L0_attn_Wq_r_w)[:, pi]),
        "wk_r": np.ascontiguousarray(f32(L0_attn_Wk_r_w)[:, pi]),
        "wu_k": np.ascontiguousarray(f32(L0_attn_Wu_k_w)[:, pi]),
        "wu_v": f32(L0_attn_Wu_v_w), "wo": f32(L0_attn_Wo_w),
        "w_in": f32(L0_ffn_W_in), "w_out": f32(L0_ffn_W_out),
        "bd_q": _bias_pm(L0_attn_Wd_q_b, 4), "bd_kv": _bias_pm(L0_attn_Wd_kv_b, 4),
        "bu_q": _bias_pm(np.asarray(L0_attn_Wu_q_b, np.float32)[pi], 8),
        "bq_r": _bias_pm(np.asarray(L0_attn_Wq_r_b, np.float32)[pi], 8),
        "bk_r": _bias_pm(np.asarray(L0_attn_Wk_r_b, np.float32)[pi], 8),
        "bu_k": _bias_pm(np.asarray(L0_attn_Wu_k_b, np.float32)[pi], 8),
        "b_in": _bias_pm(L0_ffn_b_in, 32),
        "bu_v_row": f32(L0_attn_Wu_v_b).reshape(1, D_),
        "bo_row": f32(L0_attn_Wo_b).reshape(1, D_),
        "bout_row": f32(L0_ffn_b_out).reshape(1, D_),
        "pswap": _pswap(),
        "idn": np.eye(128, dtype=np.float32).astype(ml_dtypes.bfloat16),
        "idn32": np.eye(128, dtype=np.float32),
    }
    in_maps = []
    for core in range(8):
        bc, hc = core // 2, core % 2
        order = np.concatenate([np.arange(hc * NTOK, (hc + 1) * NTOK),
                                np.arange((1 - hc) * NTOK, (2 - hc) * NTOK)])
        cosb, sinb = _rope_tables(order)
        m = dict(shared)
        m["x"] = np.ascontiguousarray(x[bc][order])
        m["cos_d"] = np.ascontiguousarray(cosb)
        m["sin_d"] = np.ascontiguousarray(sinb)
        in_maps.append(m)

    if _NC_CACHE[0] is None:
        _NC_CACHE[0] = build_core_program()
    nc = _NC_CACHE[0]
    res = run_bass_kernel_spmd(nc, in_maps, core_ids=list(range(8)))
    out = np.empty((B_, S_, D_), np.float32)
    for core in range(8):
        bc, hc = core // 2, core % 2
        out[bc, hc * NTOK:(hc + 1) * NTOK, :] = res.results[core]["y"]
    return out


# version 16 (final)
# speedup vs baseline: 1.0783x; 1.0205x over previous
import contextlib
import numpy as np
import ml_dtypes

import concourse.bass as bass
import concourse.mybir as mybir
from concourse.bass_utils import run_bass_kernel_spmd

F32 = mybir.dt.float32
BF16 = mybir.dt.bfloat16
AF = mybir.ActivationFunctionType
ALU = mybir.AluOpType
AX = mybir.AxisListType

ENGS = ("sync", "vector", "scalar", "tensor", "gpsimd")

B_, S_, D_ = 4, 4096, 1024
NTOK, NKEY = 2048, 4096
NH, HD = 16, 64
DQ = DKV = 512
DFF = 4096
SCALE = float(np.sqrt(np.float32(NH + DKV + DQ)))
INV_SCALE = float(np.float32(1.0) / np.float32(SCALE))
EPS = 1e-5
CMAGIC = 12582912.0
QMAX_A16 = 32767.0
NMACRO, QMAC = 4, 512
NKC = NKEY // 128
NPAIR = NH // 2
NTT = NTOK // 128
NKT = NKEY // 512
NQT = NTOK // 512


class KB:

    def __init__(self, nc):
        self.nc = nc
        self.ops = {e: [] for e in ENGS}
        self.sems = {}
        self.cnt = {}
        self.waited = {e: {} for e in ENGS}
        self.dma_sems = set()
        self.stack = contextlib.ExitStack()

    def sem(self, name):
        if name not in self.sems:
            self.sems[name] = self.stack.enter_context(self.nc.semaphore(name))
            self.cnt[name] = 0
        return self.sems[name]

    def op(self, eng, fn, sem=None, n=1):
        if sem is None:
            self.ops[eng].append(fn)
            return None
        if n == 16:
            self.dma_sems.add(sem)
        h = self.sem(sem)
        self.cnt[sem] += n
        val = self.cnt[sem]

        def wrapped(e, fn=fn, h=h, n=n):
            fn(e).then_inc(h, n)
        self.ops[eng].append(wrapped)
        return (sem, val)

    def wait(self, eng, tok):
        if tok is None:
            return
        name, val = tok
        if self.waited[eng].get(name, 0) >= val:
            return
        self.waited[eng][name] = val
        h = self.sem(name)
        self.ops[eng].append(lambda e, h=h, val=val: e.wait_ge(h, val))

    def waits(self, eng, toks):
        best = {}
        for t in toks:
            if t is None:
                continue
            name, val = t
            if best.get(name, -1) < val:
                best[name] = val
        for name, val in best.items():
            self.wait(eng, (name, val))

    def join(self):
        names = [s for s in self.cnt if not s.startswith("jn_")]
        for e in ENGS:
            for sem in sorted(names):
                self.wait(e, (sem, self.cnt[sem]))
        toks = [self.op(e, lambda en: en.nop(), sem="jn_" + e) for e in ENGS]
        for e in ENGS:
            for t in toks:
                self.wait(e, t)

    def runner(self, e):
        ops = self.ops[e]

        def run(eng):
            for f in ops:
                f(eng)
        return run


def build_core_program(dbg=False):
    nc = bass.Bass("TRN2", target_bir_lowering=False, debug=False)
    kb = KB(nc)
    o, w, ws = kb.op, kb.wait, kb.waits

    def din(name, shape, dt=F32):
        return nc.dram_tensor(name, shape, dt, kind="ExternalInput").ap()

    def dint(name, shape, dt=F32):
        return nc.dram_tensor(name, shape, dt).ap()

    x_in = din("x", [NKEY, D_])
    wd_q = din("wd_q", [D_, DQ]);   wd_kv = din("wd_kv", [D_, DKV])
    wu_q = din("wu_q", [DQ, D_]);   wq_r = din("wq_r", [DQ, D_])
    wk_r = din("wk_r", [D_, D_]);   wu_k = din("wu_k", [DKV, D_])
    wu_v = din("wu_v", [DKV, D_]);  wo_w = din("wo", [D_, D_])
    w_in = din("w_in", [D_, DFF]);  w_out = din("w_out", [DFF, D_])
    bd_q = din("bd_q", [128, 4]);   bd_kv = din("bd_kv", [128, 4])
    bu_q = din("bu_q", [128, 8]);   bq_r = din("bq_r", [128, 8])
    bk_r = din("bk_r", [128, 8]);   bu_k = din("bu_k", [128, 8])
    b_in = din("b_in", [128, 32])
    bu_v_row = din("bu_v_row", [1, D_])
    bo_row = din("bo_row", [1, D_])
    bout_row = din("bout_row", [1, D_])
    cos_d = din("cos_d", [D_, NKEY], BF16)
    sin_d = din("sin_d", [D_, NKEY], BF16)
    pswap_in = din("pswap", [128, 128], BF16)
    idn_in = din("idn", [128, 128], BF16)
    idn32_in = din("idn32", [128, 128])
    y_out = nc.dram_tensor("y", [NTOK, D_], F32, kind="ExternalOutput").ap()

    ln1x_d = dint("ln1x_d", [NKEY, D_], BF16)
    x1_d = dint("x1_d", [NTOK, D_])
    xsim_d = dint("xsim_d", [NTOK, D_], BF16)
    hid_d = dint("hid_d", [32, 128, NTOK], BF16)
    wint_d = dint("wint_d", [32, 128, 8, 128], BF16)
    wout_d = dint("wout_d", [32, 128, D_], BF16)
    q_d = dint("q_d", [NPAIR, 128, NTOK], BF16)
    wdq_d = dint("wdq_d", [8, 128, DQ], BF16)
    wdkv_d = dint("wdkv_d", [8, 128, DKV], BF16)
    wuq_d = dint("wuq_d", [4, 128, D_], BF16)
    wqr_d = dint("wqr_d", [4, 128, D_], BF16)
    wkr_d = dint("wkr_d", [8, 128, D_], BF16)
    wuk_d = dint("wuk_d", [4, 128, D_], BF16)
    wuv_d = dint("wuv_d", [4, 128, D_], BF16)
    wo_d = dint("wo_d", [8, 128, D_], BF16)
    flip_d = dint("flip_d", [1, 4096])
    cc1_in_h = nc.dram_tensor("cc1_in", [1, 128], F32)
    cc1_out_h = nc.dram_tensor("cc1_out", [1, 128], F32)
    cc2_in_h = nc.dram_tensor("cc2_in", [1, 128], F32)
    cc2_out_h = nc.dram_tensor("cc2_out", [1, 128], F32)

    def mksb(stk):
        def f(name, shape, dt=F32):
            return stk.enter_context(nc.sbuf_tensor(name, shape, dt))
        return f

    sbP = mksb(kb.stack)

    def pst(name, shape):
        return kb.stack.enter_context(nc.psum_tensor(name, shape, F32))

    psA = pst("psA", [128, 1024])
    psB = pst("psB", [128, 1024])
    ps1 = pst("ps1", [128, 512])
    ps2 = pst("ps2", [128, 512])
    ps3 = pst("ps3", [128, 512])
    ps4 = pst("ps4", [128, 512])

    ones_row = sbP("ones_row", [1, 128])
    ones2 = sbP("ones2", [128, 2])
    ones2b = sbP("ones2b", [128, 2], BF16)
    ones21 = sbP("ones21", [2, 1])
    pswap_s = sbP("pswap_s", [128, 128], BF16)
    idn_s = sbP("idn_s", [128, 128], BF16)
    idn32_s = sbP("idn32_s", [128, 128])
    bias_c = sbP("bias_c", [128, 32])
    qn2 = sbP("qn2", [2, 32])
    kn2 = sbP("kn2", [2, NPAIR])
    kn4 = sbP("kn4", [2, 32])
    knall = sbP("knall", [2, 64])
    cc_sc = sbP("cc_sc", [2, 32])
    negc = sbP("negc", [1, 32])
    rstd3a = sbP("rstd3a", [128, NTT])
    nmr3a = sbP("nmr3a", [128, NTT])
    amax_r = sbP("amax_r", [128, 1])
    hmax_r = sbP("hmax_r", [128, 1])
    hmt_s = sbP("hmt_s", [128, 1])
    sjr_p = sbP("sjr_p", [128, 32])
    grow = sbP("grow", [1, 128])
    ln_sc = sbP("ln_sc", [128, 2, D_])
    s_sum = sbP("s_sum", [128, 2]); s_sq = sbP("s_sq", [128, 2])
    s_mean = sbP("s_mean", [128, 2]); s_var = sbP("s_var", [128, 2])
    s_rstd = sbP("s_rstd", [128, 2]); s_nmr = sbP("s_nmr", [128, 2])
    s_t1 = sbP("s_t1", [128, 2])
    sc_a = sbP("sc_a", [1, 1]); sc_b = sbP("sc_b", [1, 1]); sc_c = sbP("sc_c", [1, 1])
    sxb = sbP("sxb", [128, 1]); invsxb = sbP("invsxb", [128, 1])
    shb = sbP("shb", [128, 1]); invshb = sbP("invshb", [128, 1])
    q2sc = sbP("q2sc", [128, 2]); q2bi = sbP("q2bi", [128, 2])
    eps_b = sbP("eps_b", [128, 1]); cmag_b = sbP("cmag_b", [128, 1])
    tiny_b = sbP("tiny_b", [128, 1])
    bdq_s = sbP("bdq_s", [128, 4]);   bdkv_s = sbP("bdkv_s", [128, 4])
    buq_s = sbP("buq_s", [128, 8]);   bqr_s = sbP("bqr_s", [128, 8])
    bkr_s = sbP("bkr_s", [128, 8]);   buk_s = sbP("buk_s", [128, 8])
    bin_s = sbP("bin_s", [128, 32])

    psum_last = {}

    t_smld = None
    for dst, src in [(bdq_s, bd_q), (bdkv_s, bd_kv), (buq_s, bu_q),
                     (bqr_s, bq_r), (bkr_s, bk_r), (buk_s, bu_k),
                     (bin_s, b_in), (pswap_s, pswap_in)]:
        t_smld = o("sync", lambda e, dst=dst, src=src: e.dma_start(
            out=dst[:, :], in_=src), "smld", 16)

    t_ones = o("vector", lambda e: e.memset(ones_row[:, :], 1.0), "sv")
    t_o2a = o("vector", lambda e: e.memset(ones2[:, :], 0.0), "sv")
    w("vector", t_o2a)
    o("vector", lambda e: e.memset(ones2[0:64, 0:1], 1.0), "sv")
    o("vector", lambda e: e.memset(ones2[64:128, 1:2], 1.0), "sv")
    t_o2b = o("vector", lambda e: e.memset(ones2b[:, :], 0.0), "sv")
    w("vector", t_o2b)
    o("vector", lambda e: e.memset(ones2b[0:64, 0:1], 1.0), "sv")
    t_ones2 = o("vector", lambda e: e.memset(ones2b[64:128, 1:2], 1.0), "sv")
    t_cs1 = o("vector", lambda e: e.memset(eps_b[:, :], EPS), "sv")
    t_cs2 = o("vector", lambda e: e.memset(cmag_b[:, :], CMAGIC), "sv")
    t_cs3 = o("vector", lambda e: e.memset(tiny_b[:, :], 1e-30), "sv")
    t_consts = [t_cs1, t_cs2, t_cs3]
    t_amz = o("vector", lambda e: e.memset(amax_r[:, :], 0.0), "sv")
    t_hmz = o("vector", lambda e: e.memset(hmax_r[:, :], 0.0), "sv")
    o("sync", lambda e: e.dma_start(out=idn_s[:, :], in_=idn_in), "smld", 16)
    t_idn = o("sync", lambda e: e.dma_start(out=idn32_s[:, :], in_=idn32_in), "smld", 16)
    t_small_loads = t_idn
    t_o21 = o("vector", lambda e: e.memset(ones21[:, :], 1.0), "sv")

    _ln_state = [{"t_n": None, "t_r2": None, "readers": []},
                 {"t_n": None, "t_r2": None, "readers": []}]
    _ln_ctr = [0]

    def ln_stats_from(src_ap, readers):
        g = _ln_ctr[0] % 2
        _ln_ctr[0] += 1
        stt = _ln_state[g]
        sl = lambda t: t[:, g:g + 1]
        lsc = ln_sc[:, g, :]
        prevs = [t for t in (stt["readers"] + list(readers)
                             + [stt["t_n"], stt["t_r2"]]) if t]
        ws("vector", prevs)
        t_s1 = o("vector", lambda e: e.tensor_scalar(
            lsc, src_ap, 0.0, 0.0, ALU.add, ALU.add, accum_out=sl(s_sum)), "sv")
        w("vector", t_s1)
        t_s2 = o("vector", lambda e: e.scalar_tensor_tensor(
            lsc, src_ap, 1.0, src_ap, ALU.bypass, ALU.mult,
            accum_out=sl(s_sq)), "sv")
        w("vector", t_s2)
        t_m = o("vector", lambda e: e.tensor_scalar(
            sl(s_mean), sl(s_sum), 1.0 / D_, None, ALU.mult), "sv")
        w("vector", t_m)
        t_v1 = o("vector", lambda e: e.scalar_tensor_tensor(
            sl(s_var), sl(s_mean), -1.0, sl(s_mean), ALU.mult, ALU.mult), "sv")
        w("vector", t_v1)
        t_v2 = o("vector", lambda e: e.tensor_scalar(
            sl(s_t1), sl(s_sq), 1.0 / D_, sl(s_var), ALU.mult, ALU.add), "sv")
        ws("scalar", prevs)
        w("scalar", t_v2)
        ws("scalar", t_consts)
        t_r1 = o("scalar", lambda e: e.activation(
            sl(s_var), sl(s_t1), AF.Ln, bias=eps_b[:, :], scale=1.0), "ss")
        w("scalar", t_r1)
        t_r2 = o("scalar", lambda e: e.activation(
            sl(s_rstd), sl(s_var), AF.Exp, bias=0.0, scale=-0.5), "ss")
        w("vector", t_r2)
        t_n = o("vector", lambda e: e.scalar_tensor_tensor(
            sl(s_nmr), sl(s_mean), -1.0, sl(s_rstd), ALU.mult, ALU.mult), "sv")
        stt["t_n"] = t_n
        stt["t_r2"] = t_r2
        stt["readers"] = []
        return (t_n, sl(s_rstd), sl(s_nmr), g)

    def ln_mark_readers(g, toks):
        _ln_state[g]["readers"] = [t for t in toks if t]

    stA = contextlib.ExitStack()
    sbA = mksb(stA)
    xa_t = sbA("xa_t", [128, 2, D_])
    ln_bf = sbA("ln_bf", [128, 2, D_], BF16)
    wst_f = sbA("wst_f", [128, 2, D_])
    wst_b = sbA("wst_b", [128, 2, D_], BF16)

    wq_h = sbA("wq_h", [128, 8, DFF // 2])
    M_h = sbA("M_h", [128, DFF // 2])
    A_t = sbA("A_t", [128, 32])
    at_sb = sbA("at_sb", [16, 128])
    srow_h = sbA("srow_h", [1, DFF // 2])
    Sbc_h = sbA("Sbc_h", [128, DFF // 2])
    wint_h = sbA("wint_h", [128, DFF // 2], BF16)
    wqo_f = sbA("wqo_f", [128, 2, D_])
    so_t = sbA("so_t", [128, 32])
    sor_t = sbA("sor_t", [128, 32])
    nCr_t = sbA("nCr_t", [128, 32])
    wob16 = sbA("wob16", [128, D_], BF16)
    w127 = sbA("w127", [128, 32])

    R = {}

    def gen_ln1():
        xa_reuse = [None, None]
        lnbf_reuse = [None, None]
        t_aff_prev = None
        t_st_ln = None
        for i in range(NKEY // 128):
            bi = i % 2
            w("sync", xa_reuse[bi])
            t_x = o("sync", lambda e, i=i, bi=bi: e.dma_start(
                out=xa_t[:, bi, :], in_=x_in[i * 128:(i + 1) * 128, :]), "xal%d" % bi, 16)
            w("vector", t_x)
            t_n, rstd_ap, nmr_ap, g = ln_stats_from(xa_t[:, bi, :], [t_aff_prev])
            ws("scalar", [t_n, lnbf_reuse[bi]])
            t_ap = o("scalar", lambda e, bi=bi, rstd_ap=rstd_ap, nmr_ap=nmr_ap: e.activation(
                ln_bf[:, bi, :], xa_t[:, bi, :], AF.Identity,
                bias=nmr_ap, scale=rstd_ap), "ss")
            ln_mark_readers(g, [t_ap])
            xa_reuse[bi] = t_ap
            t_aff_prev = t_ap
            w("gpsimd", t_ap)
            t_st_ln = o("gpsimd", lambda e, i=i, bi=bi: e.dma_start(
                out=ln1x_d[i * 128:(i + 1) * 128, :], in_=ln_bf[:, bi, :]), "lns%d" % bi, 16)
            lnbf_reuse[bi] = t_st_ln
            yield
        R["t_ln1_both"] = [lnbf_reuse[0], lnbf_reuse[1]]

    def gen_precast():
        st_tok = [None, None]
        cp_tok = [None, None]
        ci = 0
        tok = None
        for srcw, dst_d, nrc, ncols in [
                (wd_q, wdq_d, 8, 512), (wd_kv, wdkv_d, 8, 512),
                (wu_q, wuq_d, 4, 1024), (wq_r, wqr_d, 4, 1024),
                (wk_r, wkr_d, 8, 1024), (wu_k, wuk_d, 4, 1024),
                (wu_v, wuv_d, 4, 1024), (wo_w, wo_d, 8, 1024)]:
            for rc in range(nrc):
                bi = ci % 2
                ci += 1
                w("sync", st_tok[bi])
                t_l = o("sync", lambda e, rc=rc, srcw=srcw, ncols=ncols, bi=bi: e.dma_start(
                    out=wst_f[:, bi, 0:ncols], in_=srcw[rc * 128:(rc + 1) * 128, :]),
                    "pcl%d" % bi, 16)
                ws("scalar", [t_l, st_tok[bi]])
                t_c = o("scalar", lambda e, ncols=ncols, bi=bi: e.activation(
                    wst_b[:, bi, 0:ncols], wst_f[:, bi, 0:ncols], AF.Identity,
                    bias=0.0, scale=1.0), "ss")
                cp_tok[bi] = t_c
                w("gpsimd", t_c)
                tok = o("gpsimd", lambda e, rc=rc, dst_d=dst_d, ncols=ncols, bi=bi: e.dma_start(
                    out=dst_d[rc, :, :], in_=wst_b[:, bi, 0:ncols]), "pcs%d" % bi, 16)
                st_tok[bi] = tok
                yield
        R["t_precast"] = tok

    def gen_wout():
        wqo_reuse = [None, None]
        t_sc_pr = None
        t_p2_prev = None
        for jc in range(32):
            bi = jc % 2
            w("sync", wqo_reuse[bi])
            t_l = o("sync", lambda e, jc=jc, bi=bi: e.dma_start(
                out=wqo_f[:, bi, :], in_=w_out[jc * 128:(jc + 1) * 128, :]), "wqo%d" % bi, 16)
            ws("vector", [t_l, t_sc_pr])
            t_am = o("vector", lambda e, jc=jc, bi=bi: e.tensor_reduce(
                so_t[:, jc:jc + 1], wqo_f[:, bi, :], AX.X, ALU.max,
                apply_absolute_value=True), "sv")
            w("vector", t_am)
            t_oc = o("vector", lambda e, jc=jc: e.tensor_scalar(
                so_t[:, jc:jc + 1], so_t[:, jc:jc + 1], 1e-8, None, ALU.max), "sv")
            w("vector", t_oc)
            t_or = o("vector", lambda e, jc=jc: e.tensor_scalar(
                sor_t[:, jc:jc + 1], so_t[:, jc:jc + 1], 1.0 / 127.0, None, ALU.mult), "sv")
            t_o127 = o("vector", lambda e, jc=jc: e.reciprocal(
                w127[:, jc:jc + 1], so_t[:, jc:jc + 1]), "sv")
            ws("vector", [t_or, t_o127])
            t_ncr = o("vector", lambda e, jc=jc: e.tensor_scalar(
                nCr_t[:, jc:jc + 1], sor_t[:, jc:jc + 1], -CMAGIC, None, ALU.mult), "sv")
            t_osd = o("vector", lambda e, jc=jc: e.tensor_scalar(
                so_t[:, jc:jc + 1], w127[:, jc:jc + 1], 127.0, None, ALU.mult), "sv")
            t_sc_pr = t_osd
            ws("scalar", [t_osd, t_ncr, t_consts[1]])
            t_p1 = o("scalar", lambda e, jc=jc, bi=bi: e.activation(
                wqo_f[:, bi, :], wqo_f[:, bi, :], AF.Identity,
                bias=cmag_b[:, :], scale=so_t[:, jc:jc + 1]), "ss")
            ws("scalar", [t_p1, t_p2_prev])
            t_p2 = o("scalar", lambda e, jc=jc, bi=bi: e.activation(
                wob16[:, :], wqo_f[:, bi, :], AF.Identity,
                bias=nCr_t[:, jc:jc + 1], scale=sor_t[:, jc:jc + 1]), "ss")
            wqo_reuse[bi] = t_p2
            w("gpsimd", t_p2)
            t_p2_prev = o("gpsimd", lambda e, jc=jc: e.dma_start(
                out=wout_d[jc, :, :], in_=wob16[:, :]), "wts", 16)
            yield

    def gen_win():
        wq_reuse = [None] * 8
        wi_prev = [None, None]
        t_sbc_read = [None]
        t_fold_done = [None]
        t_ac = None
        for half in range(2):
            c0 = half * 2048
            mprev = None
            for fc in range(8):
                w("sync", wq_reuse[fc])
                t_l = o("sync", lambda e, fc=fc, c0=c0: e.dma_start(
                    out=wq_h[:, fc, :], in_=w_in[fc * 128:(fc + 1) * 128, c0:c0 + 2048]),
                    "wql%d" % fc, 16)
                if fc == 0:
                    ws("scalar", [t_l, mprev, t_fold_done[0]])
                    mprev = o("scalar", lambda e, fc=fc: e.activation(
                        M_h[:, :], wq_h[:, fc, :], AF.Abs, bias=0.0, scale=1.0), "ss")
                else:
                    ws("scalar", [t_l, t_sbc_read[0]])
                    t_abs = o("scalar", lambda e, fc=fc: e.activation(
                        Sbc_h[:, :], wq_h[:, fc, :], AF.Abs, bias=0.0, scale=1.0), "ss")
                    ws("vector", [t_abs, mprev])
                    mprev = o("vector", lambda e: e.tensor_tensor(
                        M_h[:, :], M_h[:, :], Sbc_h[:, :], ALU.max), "sv")
                    t_sbc_read[0] = mprev
                yield
            slots = [ps3[:, 0:128], ps3[:, 128:256], ps4[:, 0:128], ps4[:, 128:256]]
            slot_tok = [psum_last.get("ps3"), psum_last.get("ps3"),
                        psum_last.get("ps4"), psum_last.get("ps4")]
            t_red = None
            for c in range(16):
                sl_ = c % 4
                ws("tensor", [mprev, t_idn, slot_tok[sl_]])
                t_tr = o("tensor", lambda e, c=c, sl_=sl_: e.transpose(
                    slots[sl_], M_h[:, c * 128:(c + 1) * 128], idn32_s[:, :]), "st")
                w("vector", t_tr)
                t_red = o("vector", lambda e, c=c, half=half, sl_=sl_: e.tensor_reduce(
                    A_t[:, half * 16 + c: half * 16 + c + 1], slots[sl_], AX.X, ALU.max), "sv")
                slot_tok[sl_] = t_red
                if c % 4 == 3:
                    yield
            t_fold_done[0] = t_red
            psum_last["ps3"] = t_red
            psum_last["ps4"] = t_red
            w("vector", t_red)
            t_ac = o("vector", lambda e, half=half: e.tensor_scalar(
                A_t[:, half * 16:(half + 1) * 16], A_t[:, half * 16:(half + 1) * 16],
                1e-8, None, ALU.max), "sv")
            ws("tensor", [t_ac, psum_last.get("ps3")])
            t_t2 = o("tensor", lambda e, half=half: e.transpose(
                ps3[0:16, 0:128], A_t[:, half * 16:(half + 1) * 16], idn32_s[:, :]), "st")
            w("vector", t_t2)
            t_cpt = o("vector", lambda e: e.tensor_copy(at_sb[:, :], ps3[0:16, 0:128]), "sv")
            psum_last["ps3"] = t_cpt
            w("gpsimd", t_cpt)
            t_fl1 = o("gpsimd", lambda e: e.dma_start(
                out=flip_d[0:1, 0:2048].rearrange("o (a b) -> (o a) b", b=128),
                in_=at_sb[:, :]), "flst", 16)
            w("sync", t_fl1)
            t_fl2 = o("sync", lambda e: e.dma_start(
                out=srow_h[:, :], in_=flip_d[0:1, 0:2048]), "flrd", 16)
            w("vector", t_fl2)
            t_sr1 = o("vector", lambda e: e.reciprocal(srow_h[:, :], srow_h[:, :]), "sv")
            w("vector", t_sr1)
            t_sr2 = o("vector", lambda e: e.tensor_scalar(
                srow_h[:, :], srow_h[:, :], 127.0, None, ALU.mult), "sv")
            yield
            t_sbc = None
            for q in range(4):
                ws("tensor", [t_sr2, t_ones, psum_last.get("ps4")])
                t_mmq = o("tensor", lambda e, q=q: e.matmul(
                    ps4[:, 0:512], ones_row[:, :], srow_h[:, q * 512:(q + 1) * 512],
                    start=True, stop=True), "st")
                w("vector", t_mmq)
                t_sbc = o("vector", lambda e, q=q: e.tensor_copy(
                    Sbc_h[:, q * 512:(q + 1) * 512], ps4[:, 0:512]), "sv")
                psum_last["ps4"] = t_sbc
            yield
            t_wi_last = None
            for fc in range(8):
                ws("vector", [t_sbc, wi_prev[fc % 2]])
                t_m1 = o("vector", lambda e, fc=fc: e.tensor_tensor(
                    wq_h[:, fc, :], wq_h[:, fc, :], Sbc_h[:, :], ALU.mult), "sv")
                t_sbc_read[0] = t_m1
                ws("vector", [t_m1, wi_prev[1 - fc % 2]])
                t_m2 = o("vector", lambda e, fc=fc: e.tensor_scalar(
                    wint_h[:, :], wq_h[:, fc, :], CMAGIC, CMAGIC, ALU.add, ALU.subtract), "sv")
                wq_reuse[fc] = t_m2
                w("gpsimd", t_m2)
                t_wi_last = o("gpsimd", lambda e, fc=fc, half=half: e.dma_start(
                    out=wint_d[half * 16:(half + 1) * 16, :, fc, :].rearrange("j p c -> p j c"),
                    in_=wint_h[:, :].rearrange("p (j c) -> p j c", c=128)), "wis", 16)
                wi_prev[fc % 2] = t_wi_last
                yield
        w("vector", t_ac)
        t_sjr = o("vector", lambda e: e.tensor_scalar(
            sjr_p[:, :], A_t[:, :], 1.0 / 127.0, None, ALU.mult), "sv")
        R["t_sjr_done"] = t_sjr

    def drive(gens):
        gens = list(gens)
        while gens:
            nxt = []
            for gobj in gens:
                try:
                    next(gobj)
                    nxt.append(gobj)
                except StopIteration:
                    pass
            gens = nxt

    drive([gen_ln1(), gen_precast(), gen_wout(), gen_win()])
    t_ln1_both = R["t_ln1_both"]
    t_precast = R["t_precast"]
    t_sjr_done = R["t_sjr_done"]

    kb.join()
    stA.close()

    stKV = contextlib.ExitStack()
    sbKV = mksb(stKV)
    K_fm = sbKV("K_fm", [128, NPAIR, NKEY], BF16)
    V_tm = sbKV("V_tm", [128, NKC, NH * 65], BF16)
    t_vones = o("vector", lambda e: e.memset(
        V_tm[:, :, :].rearrange("p k (h v) -> p k h v", v=65)[:, :, :, 64:65], 1.0), "sv")

    stB = contextlib.ExitStack()
    sbB = mksb(stB)
    lx_t = sbB("lx_t", [128, 2, 8, 512], BF16)
    cs_t = sbB("cs_t", [128, 2, 2, 512], BF16)
    rp_t0 = sbB("rp_t0", [128, 2, 512], BF16)
    rp_t1 = sbB("rp_t1", [128, 2, 512], BF16)
    rp_t2 = sbB("rp_t2", [128, 2, 512], BF16)
    kc_bf = sbB("kc_bf", [128, 2, 512], BF16)
    sq_t = sbB("sq_t", [128, 2, 512], BF16)
    stB1 = contextlib.ExitStack()
    sbB1 = mksb(stB1)
    ckv_t = sbB1("ckv_t", [128, 4, 512], BF16)
    wdkv_bf = sbB1("wdkv_bf", [128, 8, 512], BF16)
    wuv_bf = sbB1("wuv_bf", [128, 4, 1024], BF16)
    wuk_bf = sbB1("wuk_bf", [128, 4, 1024], BF16)
    wkrp = sbB1("wkrp", [128, 2, 8, 128], BF16)
    vB_bc = sbB1("vB_bc", [128, D_], BF16)
    row_b = sbB1("row_b", [1, 512])

    vb_toks = []
    t_rb = None
    for h in range(2):
        w("sync", t_rb)
        t_rb = o("sync", lambda e, h=h: e.dma_start(
            out=row_b[:, :], in_=bu_v_row[0:1, h * 512:(h + 1) * 512]), "rowb", 16)
        ws("tensor", [t_rb, t_ones, psum_last.get("ps4")])
        vb_toks.append(o("tensor", lambda e: e.matmul(
            ps4[:, 0:512], ones_row[:, :], row_b[:, :],
            start=True, stop=True), "st"))
        w("vector", vb_toks[-1])
        t_vB = o("vector", lambda e, h=h: e.tensor_copy(
            vB_bc[:, h * 512:(h + 1) * 512], ps4[:, 0:512]), "sv")
        psum_last["ps4"] = t_vB
        t_rb_tok = vb_toks[-1]
        t_rb = t_rb if False else vb_toks[-1]

    def loadw(dst, src_d, nrc, ncols, prev, sem="wlb1"):
        tok = prev
        for rc in range(nrc):
            tok = o("sync", lambda e, rc=rc: e.dma_start(
                out=dst[:, rc, :], in_=src_d[rc, :, 0:ncols]), sem, 16)
        return tok

    ws("sync", [t_precast])
    t_wb = loadw(wdkv_bf, wdkv_d, 8, 512, None)
    t_wb = loadw(wuv_bf, wuv_d, 4, 1024, t_wb)
    t_wb = loadw(wuk_bf, wuk_d, 4, 1024, t_wb)
    t_wB1 = t_wb

    lx_reuse = None
    cs_reuse = [None, None]
    wkr_reuse = [None, None]
    rope_gate = [None, None]
    ckv_reuse = None
    t_last_rope = None
    t_last_vev = None
    lx_read = [None, None]
    u4_tok = {}
    kn_i = [0]
    kn_sqd = []
    kn_mm_prev = [None]
    kn_rd_toks = []

    def pump_kn():
        if kn_sqd:
            p0, s80, t_sq0 = kn_sqd.pop(0)
            ws("tensor", [t_sq0, psum_last.get(("kr", 0)), t_ones2])
            t_mmn = o("tensor", lambda e: e.matmul(
                ps4[0:2, 0:512], ones2b[:, :], sq_t[:, 1, :], start=True, stop=True), "st")
            kn_mm_prev[0] = t_mmn
            psum_last[("kr", 0)] = t_mmn
            w("vector", t_mmn)
            t_rdn = o("vector", lambda e, p0=p0, s80=s80: e.tensor_reduce(
                knall[:, p0 * 8 + s80:p0 * 8 + s80 + 1], ps4[0:2, 0:512], AX.X, ALU.max), "sv")
            psum_last[("kr", 0)] = t_rdn
            kn_rd_toks.append(t_rdn)
        j = kn_i[0]
        if j < 64:
            s8j, pj = j // 8, j % 8
            if (pj, s8j) in u4_tok:
                ws("vector", [u4_tok[(pj, s8j)], kn_mm_prev[0]])
                t_sqn = o("vector", lambda e, pj=pj, s8j=s8j: e.scalar_tensor_tensor(
                    sq_t[:, 1, :], K_fm[:, pj, s8j * 512:(s8j + 1) * 512], 1.0,
                    K_fm[:, pj, s8j * 512:(s8j + 1) * 512], ALU.bypass, ALU.mult), "sv")
                kn_sqd.append((pj, s8j, t_sqn))
                kn_i[0] += 1

    for kt in range(NKT):
        lbi = kt % 2
        ws("sync", t_ln1_both + [lx_read[lbi]])
        ttr = None
        for fc in range(8):
            ttr = o("sync", lambda e, kt=kt, fc=fc, lbi=lbi: e.dma_start_transpose(
                lx_t[:, lbi, fc, :],
                ln1x_d[kt * 512:(kt + 1) * 512, fc * 128:(fc + 1) * 128]), "lxt%d" % lbi, 16)
        t_lx = ttr
        ckv_toks = []
        for cc in range(4):
            psl = [psA[:, 0:512], psA[:, 512:1024], psB[:, 0:512], psB[:, 512:1024]][cc]
            deps = [t_lx, t_wB1, psum_last.get(("pAB", cc)), ckv_reuse]
            if cc >= 2:
                deps += [psum_last.get(("kc", 1)), psum_last.get(("kr", 1)),
                         psum_last.get(("sw", 1))]
            ws("tensor", deps)
            tokm = None
            for fc in range(8):
                def mmf(e, cc=cc, fc=fc, psl=psl, lbi=lbi):
                    return e.matmul(psl, wdkv_bf[:, fc, cc * 128:(cc + 1) * 128],
                                    lx_t[:, lbi, fc, :], start=(fc == 0), stop=(fc == 7))
                tokm = o("tensor", mmf, "st") if fc == 7 else o("tensor", mmf)
            w("scalar", tokm)
            t_ev = o("scalar", lambda e, cc=cc, psl=psl: e.activation(
                ckv_t[:, cc, :], psl, AF.Identity, bias=bdkv_s[:, cc:cc + 1], scale=1.0), "ss")
            psum_last[("pAB", cc)] = t_ev
            ckv_toks.append(t_ev)
        for tc in range(4):
            kcix = kt * 4 + tc
            for h2 in range(2):
                psl = ps1[:, 0:512] if h2 == 0 else ps2[:, 0:512]
                ws("tensor", ckv_toks + [psum_last.get(("pV", h2)),
                                         psum_last.get(("sw", h2))])
                tokm = None
                for cc in range(4):
                    def mmv(e, cc=cc, tc=tc, h2=h2, psl=psl):
                        return e.matmul(psl, ckv_t[:, cc, tc * 128:(tc + 1) * 128],
                                        wuv_bf[:, cc, h2 * 512:(h2 + 1) * 512],
                                        start=(cc == 0), stop=(cc == 3))
                    tokm = o("tensor", mmv, "st") if cc == 3 else o("tensor", mmv)
                ws("vector", [tokm, t_vB, t_vones])
                vv = V_tm[:, kcix, :].rearrange("p (h v) -> p h v", v=65)
                t_vev = o("vector", lambda e, h2=h2, psl=psl, vv=vv: e.tensor_tensor(
                    vv[:, h2 * 8:(h2 + 1) * 8, 0:64],
                    psl.rearrange("p (h v) -> p h v", v=64),
                    vB_bc[:, h2 * 512:(h2 + 1) * 512].rearrange("p (h v) -> p h v", v=64),
                    ALU.add), "sv")
                psum_last[("pV", h2)] = t_vev
                t_last_vev = t_vev
        for p in range(NPAIR):
            cbi = p % 2
            if kt >= 1:
                pump_kn()
            KCP = ps3[:, 0:512] if cbi == 0 else psB[:, 0:512]
            KRP = ps4[:, 0:512] if cbi == 0 else psB[:, 512:1024]
            SWP = ps1[:, 0:512] if cbi == 0 else ps2[:, 0:512]
            w("sync", wkr_reuse[cbi])
            t_wk = o("sync", lambda e, p=p, cbi=cbi: e.dma_start(
                out=wkrp[:, cbi, :, :],
                in_=wkr_d[:, :, p * 128:(p + 1) * 128]), "wkl%d" % cbi, 16)
            w("sync", cs_reuse[cbi])
            t_c1 = o("sync", lambda e, p=p, kt=kt, cbi=cbi: e.dma_start(
                out=cs_t[:, cbi, 0, :],
                in_=cos_d[p * 128:(p + 1) * 128, kt * 512:(kt + 1) * 512]), "csl%d" % cbi, 16)
            t_c2 = o("sync", lambda e, p=p, kt=kt, cbi=cbi: e.dma_start(
                out=cs_t[:, cbi, 1, :],
                in_=sin_d[p * 128:(p + 1) * 128, kt * 512:(kt + 1) * 512]), "csl%d" % cbi, 16)
            ws("tensor", ckv_toks + [psum_last.get(("kc", cbi))])
            tok_kc = None
            for cc in range(4):
                def mmk(e, cc=cc, p=p, KCP=KCP):
                    return e.matmul(KCP, wuk_bf[:, cc, p * 128:(p + 1) * 128],
                                    ckv_t[:, cc, :], start=(cc == 0), stop=(cc == 3))
                tok_kc = o("tensor", mmk, "st") if cc == 3 else o("tensor", mmk)
            ws("scalar", [tok_kc, rope_gate[cbi]])
            t_kc = o("scalar", lambda e, p=p, cbi=cbi, KCP=KCP: e.activation(
                kc_bf[:, cbi, :], KCP, AF.Identity, bias=buk_s[:, p:p + 1], scale=1.0), "ss")
            psum_last[("kc", cbi)] = t_kc
            ws("tensor", [t_lx, t_wk, psum_last.get(("kr", cbi))])
            tok_kr = None
            for fc in range(8):
                def mmr(e, fc=fc, cbi=cbi, KRP=KRP, lbi=lbi):
                    return e.matmul(KRP, wkrp[:, cbi, fc, :],
                                    lx_t[:, lbi, fc, :], start=(fc == 0), stop=(fc == 7))
                tok_kr = o("tensor", mmr, "st") if fc == 7 else o("tensor", mmr)
            wkr_reuse[cbi] = tok_kr
            ws("scalar", [tok_kr, rope_gate[cbi]])
            t_kr = o("scalar", lambda e, p=p, cbi=cbi, KRP=KRP: e.activation(
                rp_t0[:, cbi, :], KRP, AF.Identity, bias=bkr_s[:, p:p + 1], scale=1.0), "ss")
            psum_last[("kr", cbi)] = t_kr
            ws("tensor", [t_kr, psum_last.get(("sw", cbi)),
                          psum_last.get(("pV", cbi)), t_small_loads])
            tok_sw = o("tensor", lambda e, cbi=cbi, SWP=SWP: e.matmul(
                SWP, pswap_s[:, :], rp_t0[:, cbi, :], start=True, stop=True), "st")
            ws("vector", [t_kr, tok_sw, t_c1, t_c2, rope_gate[cbi]])
            t_u1 = o("vector", lambda e, cbi=cbi: e.tensor_tensor(
                rp_t1[:, cbi, :], rp_t0[:, cbi, :], cs_t[:, cbi, 0, :], ALU.mult), "sv")
            t_u2 = o("vector", lambda e, cbi=cbi, SWP=SWP: e.tensor_tensor(
                rp_t2[:, cbi, :], SWP, cs_t[:, cbi, 1, :], ALU.mult), "sv")
            psum_last[("sw", cbi)] = t_u2
            ws("vector", [t_u1, t_u2])
            t_u3 = o("vector", lambda e, cbi=cbi: e.tensor_tensor(
                rp_t1[:, cbi, :], rp_t1[:, cbi, :], rp_t2[:, cbi, :], ALU.add), "sv")
            ws("vector", [t_u3, t_kc])
            t_u4 = o("vector", lambda e, p=p, kt=kt, cbi=cbi: e.tensor_tensor(
                K_fm[:, p, kt * 512:(kt + 1) * 512], rp_t1[:, cbi, :], kc_bf[:, cbi, :], ALU.add), "sv")
            cs_reuse[cbi] = t_u2
            rope_gate[cbi] = t_u4
            t_last_rope = t_u4
            u4_tok[(p, kt)] = t_u4
            if p == NPAIR - 1:
                lx_read[lbi] = tok_kr
        ckv_reuse = t_u4
    t_K_done = t_u4
    t_V_done = t_last_vev


    kb.join()
    stB1.close()
    ctq_t = sbB("ctq_t", [128, 4, 2, 512], BF16)
    wdq_bf = sbB("wdq_bf", [128, 8, 512], BF16)
    wuq_bf = sbB("wuq_bf", [128, 4, 1024], BF16)
    wqr_bf = sbB("wqr_bf", [128, 4, 1024], BF16)
    q_bf = sbB("q_bf", [128, 2, 512], BF16)

    t_wb2 = loadw(wdq_bf, wdq_d, 8, 512, t_wB1, "wlb2")
    t_wb2 = loadw(wuq_bf, wuq_d, 4, 1024, t_wb2, "wlb2")
    t_wb2 = loadw(wqr_bf, wqr_d, 4, 1024, t_wb2, "wlb2")

    t_qspill = None
    qsp_gate = [None, None]
    qsp_last = [None, None]
    pend_qn = [None]
    nsq_gate = [None, None]
    ct_read = [None, None]

    qn_mm_pend = [None]

    def qnorm_sq(p, qt, cbi, t_u4tok):
        ws("vector", [t_u4tok, t_qn_pr[0]])
        t_sq2 = o("vector", lambda e, cbi=cbi: e.scalar_tensor_tensor(
            sq_t[:, 0, :], q_bf[:, cbi, :], 1.0, q_bf[:, cbi, :], ALU.bypass, ALU.mult), "sv")
        nsq_gate[cbi] = t_sq2
        qn_mm_pend[0] = (p, qt, t_sq2)

    def qnorm_mm():
        p, qt, t_sq2 = qn_mm_pend[0]
        qn_mm_pend[0] = None
        ws("tensor", [t_sq2, psum_last.get(("kc", 0))])
        t_mm = o("tensor", lambda e: e.matmul(
            ps3[0:2, 0:512], ones2b[:, :], sq_t[:, 0, :], start=True, stop=True), "st")
        psum_last[("kc", 0)] = t_mm
        w("vector", t_mm)
        t_rd = o("vector", lambda e, p=p, qt=qt: e.tensor_reduce(
            qn2[:, p * 4 + qt:p * 4 + qt + 1], ps3[0:2, 0:512], AX.X, ALU.max), "sv")
        psum_last[("kc", 0)] = t_rd
        t_qn_pr[0] = t_rd

    t_qn_pr = [None]
    for qt in range(NQT):
        lbi = (NKT + qt) % 2
        ws("sync", t_ln1_both + [lx_read[lbi]])
        ttr = None
        for fc in range(8):
            ttr = o("sync", lambda e, qt=qt, fc=fc, lbi=lbi: e.dma_start_transpose(
                lx_t[:, lbi, fc, :],
                ln1x_d[qt * 512:(qt + 1) * 512, fc * 128:(fc + 1) * 128]), "lxt%d" % lbi, 16)
        t_lx = ttr
        ctq_toks = []
        for cc in range(4):
            psl = [psA[:, 0:512], psA[:, 512:1024], psB[:, 0:512], psB[:, 512:1024]][cc]
            deps = [t_lx, t_wb2, psum_last.get(("pAB", cc))]
            if cc >= 2:
                deps += [psum_last.get(("kc", 1)), psum_last.get(("kr", 1)),
                         psum_last.get(("sw", 1))]
            ws("tensor", deps)
            tokm = None
            for fc in range(8):
                def mmf(e, cc=cc, fc=fc, psl=psl, lbi=lbi):
                    return e.matmul(psl, wdq_bf[:, fc, cc * 128:(cc + 1) * 128],
                                    lx_t[:, lbi, fc, :], start=(fc == 0), stop=(fc == 7))
                tokm = o("tensor", mmf, "st") if fc == 7 else o("tensor", mmf)
            ws("scalar", [tokm, ct_read[qt % 2]])
            t_ev = o("scalar", lambda e, cc=cc, qt=qt, psl=psl: e.activation(
                ctq_t[:, cc, qt % 2, :], psl, AF.Identity,
                bias=bdq_s[:, cc:cc + 1], scale=1.0), "ss")
            psum_last[("pAB", cc)] = t_ev
            ctq_toks.append(t_ev)
        lq_reuse = t_ev
        for p in range(NPAIR):
            cbi = p % 2
            QCP = ps3[:, 0:512] if cbi == 0 else psB[:, 0:512]
            QRP = ps4[:, 0:512] if cbi == 0 else psB[:, 512:1024]
            SWP = ps1[:, 0:512] if cbi == 0 else ps2[:, 0:512]
            if qt < 2:
                pump_kn()
            if qn_mm_pend[0] is not None:
                qnorm_mm()
            if pend_qn[0] is not None:
                qnorm_sq(*pend_qn[0])
                pend_qn[0] = None
            w("sync", cs_reuse[cbi])
            t_c1 = o("sync", lambda e, p=p, qt=qt, cbi=cbi: e.dma_start(
                out=cs_t[:, cbi, 0, :],
                in_=cos_d[p * 128:(p + 1) * 128, qt * 512:(qt + 1) * 512]), "csl%d" % cbi, 16)
            t_c2 = o("sync", lambda e, p=p, qt=qt, cbi=cbi: e.dma_start(
                out=cs_t[:, cbi, 1, :],
                in_=sin_d[p * 128:(p + 1) * 128, qt * 512:(qt + 1) * 512]), "csl%d" % cbi, 16)
            ws("tensor", ctq_toks + [psum_last.get(("kc", cbi))])
            tok_qc = None
            for cc in range(4):
                def mmk(e, cc=cc, p=p, qt=qt, QCP=QCP):
                    return e.matmul(QCP, wuq_bf[:, cc, p * 128:(p + 1) * 128],
                                    ctq_t[:, cc, qt % 2, :],
                                    start=(cc == 0), stop=(cc == 3))
                tok_qc = o("tensor", mmk, "st") if cc == 3 else o("tensor", mmk)
            ws("scalar", [tok_qc, rope_gate[cbi]])
            t_qc = o("scalar", lambda e, p=p, cbi=cbi, QCP=QCP: e.activation(
                kc_bf[:, cbi, :], QCP, AF.Identity, bias=buq_s[:, p:p + 1], scale=1.0), "ss")
            psum_last[("kc", cbi)] = t_qc
            ws("tensor", ctq_toks + [psum_last.get(("kr", cbi))])
            tok_qr = None
            for cc in range(4):
                def mmr(e, cc=cc, p=p, qt=qt, QRP=QRP):
                    return e.matmul(QRP, wqr_bf[:, cc, p * 128:(p + 1) * 128],
                                    ctq_t[:, cc, qt % 2, :],
                                    start=(cc == 0), stop=(cc == 3))
                tok_qr = o("tensor", mmr, "st") if cc == 3 else o("tensor", mmr)
            ws("scalar", [tok_qr, rope_gate[cbi]])
            t_qr = o("scalar", lambda e, p=p, cbi=cbi, QRP=QRP: e.activation(
                rp_t0[:, cbi, :], QRP, AF.Identity, bias=bqr_s[:, p:p + 1], scale=1.0), "ss")
            psum_last[("kr", cbi)] = t_qr
            ws("tensor", [t_qr, t_qc, psum_last.get(("sw", cbi))])
            tok_sw = o("tensor", lambda e, cbi=cbi, SWP=SWP: e.matmul(
                SWP, pswap_s[:, :], rp_t0[:, cbi, :], start=True, stop=True), "st")
            ws("vector", [t_qr, tok_sw, t_c1, t_c2, rope_gate[cbi]])
            t_u1 = o("vector", lambda e, cbi=cbi: e.tensor_tensor(
                rp_t1[:, cbi, :], rp_t0[:, cbi, :], cs_t[:, cbi, 0, :], ALU.mult), "sv")
            t_u2 = o("vector", lambda e, cbi=cbi, SWP=SWP: e.tensor_tensor(
                rp_t2[:, cbi, :], SWP, cs_t[:, cbi, 1, :], ALU.mult), "sv")
            psum_last[("sw", cbi)] = t_u2
            ws("vector", [t_u1, t_u2])
            t_u3 = o("vector", lambda e, cbi=cbi: e.tensor_tensor(
                rp_t1[:, cbi, :], rp_t1[:, cbi, :], rp_t2[:, cbi, :], ALU.add), "sv")
            ws("vector", [t_u3, t_qc, qsp_gate[cbi], nsq_gate[cbi]])
            t_u4 = o("vector", lambda e, cbi=cbi: e.tensor_tensor(
                q_bf[:, cbi, :], rp_t1[:, cbi, :], kc_bf[:, cbi, :], ALU.add), "sv")
            cs_reuse[cbi] = t_u2
            rope_gate[cbi] = t_u4
            if p == NPAIR - 1:
                lx_read[lbi] = tok_qr
                ct_read[qt % 2] = tok_qr
            w("gpsimd", t_u4)
            t_qspill = o("gpsimd", lambda e, p=p, qt=qt, cbi=cbi: e.dma_start(
                out=q_d[p, :, qt * 512:(qt + 1) * 512], in_=q_bf[:, cbi, :]), "qsp%d" % cbi, 16)
            qsp_gate[cbi] = t_qspill
            qsp_last[cbi] = t_qspill
            pend_qn[0] = (p, qt, cbi, t_u4)
    if qn_mm_pend[0] is not None:
        qnorm_mm()
    qnorm_sq(*pend_qn[0])
    pend_qn[0] = None
    qnorm_mm()
    while kn_sqd or kn_i[0] < 64:
        pump_kn()
    t_kn = None
    ws("vector", kn_rd_toks)
    for p in range(NPAIR):
        t_kn = o("vector", lambda e, p=p: e.tensor_reduce(
            kn2[:, p:p + 1], knall[:, p * 8:(p + 1) * 8], AX.X, ALU.max), "sv")
    t_knorm = t_kn
    t_rd = t_qn_pr[0]
    t_Q_done = [qsp_last[0], qsp_last[1]]
    t_qnorm = t_rd

    ws("scalar", [t_qnorm, t_knorm])
    ws("scalar", t_consts)
    t_b1 = o("scalar", lambda e: e.activation(qn2[:, :], qn2[:, :], AF.Ln, bias=tiny_b[0:2, :], scale=1.0), "ss")
    w("scalar", t_b1)
    t_b2 = o("scalar", lambda e: e.activation(qn2[:, :], qn2[:, :], AF.Exp, bias=0.0, scale=0.5), "ss")
    w("scalar", t_b2)
    t_b3 = o("scalar", lambda e: e.activation(kn2[:, :], kn2[:, :], AF.Ln, bias=tiny_b[0:2, :], scale=1.0), "ss")
    w("scalar", t_b3)
    t_b4 = o("scalar", lambda e: e.activation(kn2[:, :], kn2[:, :], AF.Exp, bias=0.0, scale=0.5), "ss")
    kn4v = kn4[:, :].rearrange("p (q m) -> p q m", m=4)
    w("vector", t_b4)
    t_b5 = None
    for mq in range(4):
        w("vector", t_b5)
        t_b5 = o("vector", lambda e, mq=mq, kn4v=kn4v: e.tensor_copy(
            kn4v[:, :, mq], kn2[:, :]), "sv")
    ws("vector", [t_b5, t_b2])
    t_b6 = o("vector", lambda e: e.tensor_tensor(cc_sc[:, :], qn2[:, :], kn4[:, :], ALU.mult), "sv")
    ws("tensor", [t_b6, t_o21, psum_last.get("ps4")])
    t_b7 = o("tensor", lambda e: e.matmul(
        ps4[0:1, 0:32], ones21[:, :], cc_sc[:, :], start=True, stop=True), "st")
    w("vector", t_b7)
    t_b8 = o("vector", lambda e: e.tensor_scalar(
        negc[:, :], ps4[0:1, 0:32], -INV_SCALE * 1.0001, None, ALU.mult), "sv")
    psum_last["ps4"] = t_b8
    ws("tensor", [t_b8, psum_last.get("ps3")])
    t_b9 = o("tensor", lambda e: e.matmul(
        ps3[:, 0:32], ones_row[:, :], negc[:, :], start=True, stop=True), "st")
    w("vector", t_b9)
    t_bias_c = o("vector", lambda e: e.tensor_copy(bias_c[:, :], ps3[:, 0:32]), "sv")
    psum_last["ps3"] = t_bias_c

    kb.join()
    stB.close()

    stC = contextlib.ExitStack()
    sbC = mksb(stC)
    O_fm = sbC("O_fm", [128, 2, 8, QMAC], BF16)
    P_pr = sbC("P_pr", [128, 2, 1024], BF16)
    Qm = sbC("Qm", [128, NPAIR, QMAC], BF16)
    Oraw = sbC("Oraw", [128, 2, 512])
    wo_bf = sbC("wo_bf", [128, 8, D_], BF16)
    woB_bc = sbC("woB_bc", [128, D_])
    w_sb = sbC("w_sb", [128, D_])
    xr_t = sbC("xr_t", [128, D_])
    taff = sbC("taff", [128, D_])
    r_sb = sbC("r_sb", [128, 512])

    t_wol = loadw(wo_bf, wo_d, 8, 1024, None, "wlc")
    t_rb2 = o("sync", lambda e: e.dma_start(out=xr_t[0:1, 0:1024], in_=bo_row), "rowb", 16)
    t_woB = None
    for h in range(2):
        ws("tensor", [t_rb2, t_ones, psum_last.get("ps4")])
        t_mmb = o("tensor", lambda e, h=h: e.matmul(
            ps4[:, 0:512], ones_row[:, :], xr_t[0:1, h * 512:(h + 1) * 512],
            start=True, stop=True), "st")
        w("vector", t_mmb)
        t_woB = o("vector", lambda e, h=h: e.tensor_copy(
            woB_bc[:, h * 512:(h + 1) * 512], ps4[:, 0:512]), "sv")
        psum_last["ps4"] = t_woB

    if dbg:
        dbgw_d = dint("dbgw_d", [NTT, 128, D_])
        dbgo_d = dint("dbgo_d", [NMACRO, 128, 8, QMAC], BF16)
        dbgp_d = dint("dbgp_d", [128, 1024], BF16)
        dbgs_d = dint("dbgs_d", [NMACRO, NPAIR, 2, 512])
    s_toks = {0: None, 1: None}
    p_toks = {0: None, 1: None}
    av_read = {0: None, 1: None}
    oraw_gate = {0: None, 1: None}
    qm_pair = [None] * NPAIR
    CS = {"t_r_ps": None, "t_wo_ps": t_woB, "t_wo_ps3": None, "o_read": [None, None],
          "t_af": None, "xr_gate": [t_woB], "t_x1s": None, "t_am2": None}
    evac_toks = {m: [] for m in range(NMACRO)}

    def load_qm_pair(mm, p, gate):
        ws("sync", t_Q_done + [gate])
        qm_pair[p] = o("sync", lambda e, p=p, mm=mm: e.dma_start(
            out=Qm[:, p, :], in_=q_d[p, :, mm * 512:(mm + 1) * 512]), "qml%d" % p, 16)

    def norm_recip(pn, j):
        if dbg:
            w("vector", pn["cps"][j])
            t_dsum = o("vector", lambda e, j=j: e.tensor_copy(
                r_sb[0:1, 0:512], Oraw[64:65, j, :]), "sv")
            w("gpsimd", t_dsum)
            o("gpsimd", lambda e, m=pn["m"], p=pn["p"], j=j: e.dma_start(
                out=dbgs_d[m, p, j, :].rearrange("(o f) -> o f", o=1),
                in_=r_sb[0:1, 0:512]), "dbgs", 16)
            ws("vector", [t_dsum])
        ws("vector", [pn["cps"][j], CS["t_r_ps"], CS.get("t_oe_prev")])
        pn["rc%d" % j] = o("vector", lambda e, j=j: e.reciprocal(
            r_sb[0:1, :], Oraw[64:65, j, :]), "sv")

    def norm_bcast(pn, j):
        om = pn["m"] % 2
        ws("tensor", [pn["rc%d" % j], psum_last.get("ps3")])
        t_ro = o("tensor", lambda e: e.matmul(
            ps3[:, 0:512], ones_row[:, :], r_sb[0:1, :],
            start=True, stop=True), "st")
        w("vector", t_ro)
        t_rcp = o("vector", lambda e: e.tensor_copy(r_sb[:, :], ps3[:, 0:512]), "sv")
        psum_last["ps3"] = t_rcp
        CS["t_r_ps"] = t_rcp
        ws("vector", [t_rcp, CS["o_read"][om]])
        t_oe = o("vector", lambda e, p=pn["p"], j=j, om=om: e.tensor_tensor(
            O_fm[j * 64:(j + 1) * 64, om, p, :], Oraw[0:64, j, :],
            r_sb[0:64, 0:512], ALU.mult), "sv")
        oraw_gate[j] = t_oe
        CS["t_oe_prev"] = t_oe
        evac_toks[pn["m"]].append(t_oe)

    def emit_epi(m, tc4):
        om = m % 2
        tcg = m * 4 + tc4
        wo_half_toks = []
        tokm = None
        for h2 in range(2):
            WOP = ps4[:, 0:512] if h2 == 0 else ps3[:, 0:512]
            wkey = "ps4" if h2 == 0 else "ps3"
            prv = CS["t_wo_ps"] if h2 == 0 else CS["t_wo_ps3"]
            ws("tensor", evac_toks[m] + [t_wol, prv, psum_last.get(wkey)])
            for dc in range(8):
                def mmw(e, dc=dc, tc4=tc4, h2=h2, om=om, WOP=WOP):
                    return e.matmul(WOP, O_fm[:, om, dc, tc4 * 128:(tc4 + 1) * 128],
                                    wo_bf[:, dc, h2 * 512:(h2 + 1) * 512],
                                    start=(dc == 0), stop=(dc == 7))
                tokm = o("tensor", mmw, "st") if dc == 7 else o("tensor", mmw)
            ws("vector", [tokm, t_woB, CS["t_af"]])
            t_wv = o("vector", lambda e, h2=h2, WOP=WOP: e.tensor_tensor(
                w_sb[:, h2 * 512:(h2 + 1) * 512], WOP,
                woB_bc[:, h2 * 512:(h2 + 1) * 512], ALU.add), "sv")
            if h2 == 0:
                CS["t_wo_ps"] = t_wv
            else:
                CS["t_wo_ps3"] = t_wv
            psum_last[wkey] = t_wv
            wo_half_toks.append(t_wv)
        CS["o_read"][om] = tokm
        if dbg:
            ws("gpsimd", wo_half_toks)
            t_dw = o("gpsimd", lambda e, tcg=tcg: e.dma_start(
                out=dbgw_d[tcg], in_=w_sb[:, :]), "dbgs", 16)
            ws("vector", [t_dw])
            ws("scalar", [t_dw])
        ws("vector", wo_half_toks)
        t_ln2, rstd2_ap, nmr2_ap, g2 = ln_stats_from(w_sb[:, :], [CS["t_af"], CS["t_am2"]])
        ws("sync", CS["xr_gate"])
        t_xl = o("sync", lambda e, tcg=tcg: e.dma_start(
            out=xr_t[:, :], in_=x_in[tcg * 128:(tcg + 1) * 128, :]), "xrl", 16)
        ws("scalar", [t_ln2, CS["t_am2"]])
        t_af = o("scalar", lambda e, rstd2_ap=rstd2_ap, nmr2_ap=nmr2_ap: e.activation(
            taff[:, :], w_sb[:, :], AF.Identity, bias=nmr2_ap, scale=rstd2_ap), "ss")
        CS["t_af"] = t_af
        ws("vector", [t_af, t_xl, CS["t_x1s"]])
        t_x1 = o("vector", lambda e: e.tensor_tensor(
            xr_t[:, :], xr_t[:, :], taff[:, :], ALU.add), "sv")
        w("gpsimd", t_x1)
        CS["t_x1s"] = o("gpsimd", lambda e, tcg=tcg: e.dma_start(
            out=x1_d[tcg * 128:(tcg + 1) * 128, :], in_=xr_t[:, :]), "x1s", 16)
        w("vector", t_x1)
        ln_mark_readers(g2, [t_af, t_x1])
        t_ln3, rstd3_ap, nmr3_ap, g3 = ln_stats_from(xr_t[:, :], [t_af, t_x1])
        w("vector", t_ln3)
        t_sv3 = o("vector", lambda e, tcg=tcg, rstd3_ap=rstd3_ap: e.tensor_copy(
            rstd3a[:, tcg:tcg + 1], rstd3_ap), "sv")
        t_sn3 = o("vector", lambda e, tcg=tcg, nmr3_ap=nmr3_ap: e.tensor_copy(
            nmr3a[:, tcg:tcg + 1], nmr3_ap), "sv")
        ws("vector", [t_sv3, t_sn3, t_af, t_x1])
        t_x3 = o("vector", lambda e, rstd3_ap=rstd3_ap, nmr3_ap=nmr3_ap: e.tensor_scalar(
            taff[:, :], xr_t[:, :], rstd3_ap, nmr3_ap, ALU.mult, ALU.add), "sv")
        CS["xr_gate"] = [CS["t_x1s"], t_ln3, t_x3]
        w("vector", t_x3)
        t_am = o("vector", lambda e, g3=g3: e.tensor_reduce(
            s_t1[:, g3:g3 + 1], taff[:, :], AX.X, ALU.max,
            apply_absolute_value=True), "sv")
        ws("vector", [t_am, t_amz])
        CS["t_am2"] = o("vector", lambda e, g3=g3: e.tensor_tensor(
            amax_r[:, :], amax_r[:, :], s_t1[:, g3:g3 + 1], ALU.max), "sv")
        ln_mark_readers(g3, [t_sv3, t_sn3, t_x3, t_am])

    for p in range(NPAIR):
        load_qm_pair(0, p, None)
    pend_norm = None
    pend_epi = []
    for m in range(NMACRO):
        for p in range(NPAIR):
            hA, hB = 2 * p, 2 * p + 1
            t_avA = t_avB = None
            pend = [None]

            def emit_av(kc, t_ex):
                nonlocal t_avA, t_avB
                sbix = kc % 2
                av_deps = [t_ex]
                if kc == 0:
                    av_deps += [av_read[0], av_read[1]]
                ws("tensor", av_deps)
                t_avA = o("tensor", lambda e, kc=kc, hA=hA, sbix=sbix: e.matmul(
                    ps1[0:65, 0:512], V_tm[:, kc, hA * 65:hA * 65 + 65],
                    P_pr[:, sbix, 0:512],
                    start=(kc == 0), stop=(kc == NKC - 1)), "st")
                t_avB = o("tensor", lambda e, kc=kc, hB=hB, sbix=sbix: e.matmul(
                    ps2[0:65, 0:512], V_tm[:, kc, hB * 65:hB * 65 + 65],
                    P_pr[:, sbix, 512:1024],
                    start=(kc == 0), stop=(kc == NKC - 1)), "st")
                p_toks[sbix] = t_avB

            for kc in range(NKC):
                sbix = kc % 2
                S = psA if sbix == 0 else psB
                ws("tensor", [s_toks[sbix], qm_pair[p], t_K_done, t_bias_c])
                o("tensor", lambda e, p=p, kc=kc, S=S: e.matmul(
                    S[:, 0:512], K_fm[0:64, p, kc * 128:(kc + 1) * 128],
                    Qm[0:64, p, :], start=True, stop=True, tile_position=(0, 0)))
                t_qk = o("tensor", lambda e, p=p, kc=kc, S=S: e.matmul(
                    S[:, 512:1024], K_fm[64:128, p, kc * 128:(kc + 1) * 128],
                    Qm[64:128, p, :], start=True, stop=True, tile_position=(64, 0)), "st")
                if kc == NKC - 1 and m + 1 < NMACRO:
                    load_qm_pair(m + 1, p, t_qk)
                ws("scalar", [t_qk, p_toks[sbix]])
                t_ex = o("scalar", lambda e, p=p, m=m, S=S, sbix=sbix: e.activation(
                    P_pr[:, sbix, :], S[:, :], AF.Exp,
                    bias=bias_c[:, p * 4 + m:p * 4 + m + 1], scale=INV_SCALE), "ss")
                s_toks[sbix] = t_ex
                if dbg and m == 0 and p == 0 and kc == 0:
                    w("gpsimd", t_ex)
                    t_dp = o("gpsimd", lambda e, sbix=sbix: e.dma_start(
                        out=dbgp_d[:, :], in_=P_pr[:, sbix, :]), "dbgs", 16)
                    ws("scalar", [t_dp])
                if pend[0] is not None:
                    emit_av(*pend[0])
                pend[0] = (kc, t_ex)
                if kc == 2 and pend_norm is not None:
                    norm_recip(pend_norm, 0)
                if kc == 6 and pend_norm is not None:
                    norm_bcast(pend_norm, 0)
                    norm_recip(pend_norm, 1)
                if kc == 10 and pend_norm is not None:
                    norm_bcast(pend_norm, 1)
                    pend_norm = None
                if kc == 16 and pend_epi:
                    emit_epi(*pend_epi.pop(0))
            emit_av(*pend[0])
            t_cps = []
            for j, (avps, t_av) in enumerate([(ps1, t_avA), (ps2, t_avB)]):
                ws("vector", [t_av, oraw_gate[j]])
                t_cp = o("vector", lambda e, j=j, avps=avps: e.tensor_copy(
                    Oraw[0:65, j, :], avps[0:65, 0:512]), "sv")
                av_read[j] = t_cp
                t_cps.append(t_cp)
            pend_norm = {"m": m, "p": p, "cps": t_cps}
        pend_epi += [(m, tc4) for tc4 in range(4)]
    norm_recip(pend_norm, 0)
    norm_bcast(pend_norm, 0)
    norm_recip(pend_norm, 1)
    norm_bcast(pend_norm, 1)
    pend_norm = None
    for item in pend_epi:
        emit_epi(*item)
    pend_epi = []
    t_x1s = CS["t_x1s"]
    t_am2 = CS["t_am2"]
    t_x1_last = t_x1s
    t_amax_done = t_am2

    kb.join()
    stC.close()
    stKV.close()

    stD = contextlib.ExitStack()
    sbD = mksb(stD)
    xsimT = sbD("xsimT", [128, 8, NTOK], BF16)
    wjt = sbD("wjt", [128, 2, 8, 128], BF16)
    wos_s = sbD("wos_s", [128, 32, D_], BF16)
    hg_t = sbD("hg_t", [128, 2, 512], BF16)
    hq_f = sbD("hq_f", [128, 4, 512], BF16)
    hq_w = sbD("hq_w", [128, 2, 512])
    hq_bf = sbD("hq_bf", [128, 4, 512], BF16)
    xq_t = sbD("xq_t", [128, 2, D_])
    xq_bf = sbD("xq_bf", [128, 2, D_], BF16)
    y_sb = sbD("y_sb", [128, D_])
    boB_bc = sbD("boB_bc", [128, D_])
    w_sb2 = sbD("w_sb2", [128, D_])
    xr_t2 = sbD("xr_t2", [128, D_])
    taff2 = sbD("taff2", [128, D_])

    def amax_collective(src_col, cc_in_h, cc_out_h, dst_s, dst_inv, prev_toks):
        ws("gpsimd", prev_toks)
        t_f1 = o("gpsimd", lambda e: e.dma_start(
            out=flip_d[0:1, 0:128], in_=src_col[:, :]), "flst", 16)
        w("sync", t_f1)
        t_f2 = o("sync", lambda e: e.dma_start(out=grow[:, :], in_=flip_d[0:1, 0:128]), "flrd", 16)
        w("gpsimd", t_f2)
        t_c1 = o("gpsimd", lambda e: e.dma_start(out=cc_in_h.ap(), in_=grow[:, :]), "ccb", 16)
        w("gpsimd", t_c1)
        t_cc = o("gpsimd", lambda e: e.collective_compute(
            "AllReduce", ALU.max, replica_groups=[list(range(8))],
            ins=[cc_in_h.ap().opt()], outs=[cc_out_h.ap().opt()]), "sg")
        w("gpsimd", t_cc)
        t_c2 = o("gpsimd", lambda e: e.dma_start(out=grow[:, :], in_=cc_out_h.ap()), "ccb", 16)
        w("vector", t_c2)
        t_g1 = o("vector", lambda e: e.tensor_reduce(sc_a[:, :], grow[:, :], AX.X, ALU.max), "sv")
        w("vector", t_g1)
        t_g2 = o("vector", lambda e: e.tensor_scalar(sc_a[:, :], sc_a[:, :], 1e-8, None, ALU.max), "sv")
        w("vector", t_g2)
        t_g3 = o("vector", lambda e: e.reciprocal(sc_c[:, :], sc_a[:, :]), "sv")
        w("vector", t_g3)
        t_g4 = o("vector", lambda e: e.tensor_scalar(
            sc_b[:, :], sc_c[:, :], QMAX_A16, None, ALU.mult), "sv")
        w("vector", t_g4)
        t_g6 = o("vector", lambda e: e.tensor_scalar(
            sc_c[:, :], sc_a[:, :], 1.0 / QMAX_A16, None, ALU.mult), "sv")
        ws("tensor", [t_g6, psum_last.get("ps3")])
        o("tensor", lambda e: e.matmul(ps3[:, 0:1], ones_row[:, :], sc_b[:, :], start=True, stop=True))
        t_g8 = o("tensor", lambda e: e.matmul(ps3[:, 1:2], ones_row[:, :], sc_c[:, :], start=True, stop=True), "st")
        w("vector", t_g8)
        t_g9 = o("vector", lambda e: e.tensor_copy(dst_s[:, :], ps3[:, 0:1]), "sv")
        w("vector", t_g9)
        t_g10 = o("vector", lambda e: e.tensor_copy(dst_inv[:, :], ps3[:, 1:2]), "sv")
        psum_last["ps3"] = t_g10
        return t_g10

    t_sx = amax_collective(amax_r, cc1_in_h, cc1_out_h, sxb, invsxb, [t_amax_done])

    xq_g3 = [None, None]
    xq_g4 = [None, None]
    xq_st = [None, None]
    for tcg in range(NTT):
        bq = tcg % 2
        ws("sync", [xq_g4[bq]])
        t_l = o("sync", lambda e, tcg=tcg, bq=bq: e.dma_start(
            out=xq_t[:, bq, :], in_=x1_d[tcg * 128:(tcg + 1) * 128, :]), "xql%d" % bq, 16)
        ws("vector", [t_l, t_sx, xq_g3[bq]])
        t_q1 = o("vector", lambda e, tcg=tcg, bq=bq: e.tensor_tensor(
            q2sc[:, bq:bq + 1], rstd3a[:, tcg:tcg + 1], sxb[:, :], ALU.mult), "sv")
        t_q2 = o("vector", lambda e, tcg=tcg, bq=bq: e.tensor_tensor(
            q2bi[:, bq:bq + 1], nmr3a[:, tcg:tcg + 1], sxb[:, :], ALU.mult), "sv")
        w("vector", t_q2)
        t_q2b = o("vector", lambda e, bq=bq: e.tensor_scalar(
            q2bi[:, bq:bq + 1], q2bi[:, bq:bq + 1], CMAGIC, None, ALU.add), "sv")
        ws("scalar", [t_l, t_q1, t_q2b, xq_g4[bq]])
        t_q3 = o("scalar", lambda e, bq=bq: e.activation(
            xq_t[:, bq, :], xq_t[:, bq, :], AF.Identity,
            bias=q2bi[:, bq:bq + 1], scale=q2sc[:, bq:bq + 1]), "ss")
        xq_g3[bq] = t_q3
        ws("vector", [t_q3, xq_st[bq]])
        t_q4 = o("vector", lambda e, bq=bq: e.tensor_scalar(
            xq_bf[:, bq, :], xq_t[:, bq, :], CMAGIC, invsxb[:, :], ALU.subtract, ALU.mult), "sv")
        xq_g4[bq] = t_q4
        w("gpsimd", t_q4)
        t_xq_tok = o("gpsimd", lambda e, tcg=tcg, bq=bq: e.dma_start(
            out=xsim_d[tcg * 128:(tcg + 1) * 128, :], in_=xq_bf[:, bq, :]), "xss%d" % bq, 16)
        xq_st[bq] = t_xq_tok
    ws("scalar", [xq_st[0], xq_st[1]])
    t_xt = None
    for fc in range(8):
        t_xt = o("scalar", lambda e, fc=fc: e.dma_start_transpose(
            xsimT[:, fc, :], xsim_d[:, fc * 128:(fc + 1) * 128]), "dsc", 16)
    t_xsimT = t_xt

    t_wosl = None
    for jc in range(32):
        t_wosl = o("sync", lambda e, jc=jc: e.dma_start(
            out=wos_s[:, jc, :], in_=wout_d[jc, :, :]), "wosl", 16)
    t_wos = t_wosl

    hb_reuse = [None, None]
    hm_reuse = [None, None]
    wj_reuse = [None, None]
    mm1_ps = {}
    t_hm2 = None
    t_hs = None
    for jc in range(32):
        bj = jc % 2
        w("sync", wj_reuse[bj])
        t_wl = o("sync", lambda e, jc=jc, bj=bj: e.dma_start(
            out=wjt[:, bj, :, :], in_=wint_d[jc, :, :, :]), "wjl%d" % bj, 16)
        for t4 in range(4):
            psl = [psA[:, 0:512], psA[:, 512:1024], psB[:, 0:512], psB[:, 512:1024]][t4]
            ws("tensor", [t_wl, t_xsimT, mm1_ps.get(t4)])
            tokm = None
            for fc in range(8):
                def mm1(e, fc=fc, t4=t4, bj=bj, psl=psl):
                    return e.matmul(psl, wjt[:, bj, fc, :],
                                    xsimT[:, fc, t4 * 512:(t4 + 1) * 512],
                                    start=(fc == 0), stop=(fc == 7))
                tokm = o("tensor", mm1, "st") if fc == 7 else o("tensor", mm1)
            if t4 == 3:
                wj_reuse[bj] = tokm
            bh = t4 % 2
            ws("scalar", [tokm, hb_reuse[bh], t_sjr_done, hm_reuse[bh]])
            t_ge = o("scalar", lambda e, jc=jc, bh=bh, psl=psl: e.activation(
                hg_t[:, bh, :], psl, AF.Gelu, bias=bin_s[:, jc:jc + 1],
                scale=sjr_p[:, jc:jc + 1]), "ss")
            mm1_ps[t4] = t_ge
            ws("vector", [t_ge, t_hm2])
            t_hm = o("vector", lambda e, bh=bh: e.tensor_reduce(
                hmt_s[:, :], hg_t[:, bh, :], AX.X, ALU.max, apply_absolute_value=True), "sv")
            hm_reuse[bh] = t_hm
            ws("vector", [t_hm, t_hmz])
            t_hm2 = o("vector", lambda e: e.tensor_tensor(
                hmax_r[:, :], hmax_r[:, :], hmt_s[:, :], ALU.max), "sv")
            w("gpsimd", t_ge)
            t_hs = o("gpsimd", lambda e, jc=jc, t4=t4, bh=bh: e.dma_start(
                out=hid_d[jc, :, t4 * 512:(t4 + 1) * 512], in_=hg_t[:, bh, :]), "his%d" % bh, 16)
            hb_reuse[bh] = t_hs
    t_hid_done = t_hs
    t_hmax_done = t_hm2

    kb.join()
    t_sh = amax_collective(hmax_r, cc2_in_h, cc2_out_h, shb, invshb, [t_hmax_done])

    w("gpsimd", t_sh)
    t_bor = o("gpsimd", lambda e: e.dma_start(out=xq_t[0:1, 0, :], in_=bout_row), "rowb2", 16)
    bo_toks = []
    for h2 in range(2):
        ws("tensor", [t_bor, psum_last.get("ps3"), psum_last.get("ps4"), t_sh])
        bo_toks.append(o("tensor", lambda e, h2=h2: e.matmul(
            ps3[:, 0:512] if h2 == 0 else ps4[:, 0:512], ones_row[:, :],
            xq_t[0:1, 0, h2 * 512:(h2 + 1) * 512], start=True, stop=True), "st"))
    ws("vector", bo_toks)
    t_boB = o("vector", lambda e: e.tensor_copy(boB_bc[:, 0:512], ps3[:, 0:512]), "sv")
    t_boB2 = o("vector", lambda e: e.tensor_copy(boB_bc[:, 512:1024], ps4[:, 0:512]), "sv")
    psum_last["ps3"] = t_boB
    psum_last["ps4"] = t_boB2

    ps_pairs = [(psA[:, 0:512], psA[:, 512:1024]),
                (psB[:, 0:512], psB[:, 512:1024]),
                (ps1[:, 0:512], ps2[:, 0:512]),
                (ps3[:, 0:512], ps4[:, 0:512])]
    hqf_reuse = [None, None, None, None]
    hqw_gate = [None, None]
    mm2_last = {}
    y_store = []
    t_ln4_prev = None
    t_y_prev = None
    t_ys = None
    for tb in range(4):
        mm2_toks = {}
        for jc in range(32):
            bh = jc % 4
            w("sync", hqf_reuse[bh])
            t_hl = o("sync", lambda e, jc=jc, tb=tb, bh=bh: e.dma_start(
                out=hq_f[:, bh, :], in_=hid_d[jc, :, tb * 512:(tb + 1) * 512]), "hql%d" % bh, 16)
            bw = bh % 2
            ws("scalar", [t_hl, t_sh, hqw_gate[bw]])
            t_tq = o("scalar", lambda e, bh=bh, bw=bw: e.activation(
                hq_w[:, bw, :], hq_f[:, bh, :], AF.Identity, bias=cmag_b[:, :], scale=shb[:, :]), "ss")
            ws("vector", [t_tq, hqf_reuse[bh]])
            t_hq = o("vector", lambda e, bh=bh, bw=bw: e.tensor_scalar(
                hq_bf[:, bh, :], hq_w[:, bw, :], CMAGIC, invshb[:, :],
                ALU.subtract, ALU.mult), "sv")
            hqw_gate[bw] = t_hq
            tk = None
            for tc4 in range(4):
                pa, pb = ps_pairs[tc4]
                deps = [t_hq, t_wos]
                if jc == 0:
                    deps.append(mm2_last.get(tc4))
                ws("tensor", deps)
                def mma(e, jc=jc, tc4=tc4, bh=bh, pa=pa):
                    return e.matmul(pa, hq_bf[:, bh, tc4 * 128:(tc4 + 1) * 128],
                                    wos_s[:, jc, 0:512], start=(jc == 0), stop=(jc == 31))
                def mmb(e, jc=jc, tc4=tc4, bh=bh, pb=pb):
                    return e.matmul(pb, hq_bf[:, bh, tc4 * 128:(tc4 + 1) * 128],
                                    wos_s[:, jc, 512:1024], start=(jc == 0), stop=(jc == 31))
                o("tensor", mma)
                tk = o("tensor", mmb, "st")
                if jc == 31:
                    mm2_toks[tc4] = tk
            hqf_reuse[bh] = tk
        for tc4 in range(4):
            tcg = tb * 4 + tc4
            pa, pb = ps_pairs[tc4]
            ws("vector", [mm2_toks[tc4], t_boB, t_boB2, t_ys, t_ln4_prev])
            t_w1 = o("vector", lambda e, pa=pa: e.tensor_tensor(
                w_sb2[:, 0:512], pa, boB_bc[:, 0:512], ALU.add), "sv")
            w("vector", t_w1)
            t_w2 = o("vector", lambda e, pb=pb: e.tensor_tensor(
                w_sb2[:, 512:1024], pb, boB_bc[:, 512:1024], ALU.add), "sv")
            mm2_last[tc4] = t_w2
            w("vector", t_w2)
            t_ln4, rstd4_ap, nmr4_ap, g4 = ln_stats_from(w_sb2[:, :], [t_ln4_prev, t_y_prev])
            w("sync", t_y_prev)
            t_xl = o("sync", lambda e, tcg=tcg: e.dma_start(
                out=xr_t2[:, :], in_=x1_d[tcg * 128:(tcg + 1) * 128, :]), "xr2", 16)
            ws("scalar", [t_ln4, t_y_prev])
            t_af = o("scalar", lambda e, rstd4_ap=rstd4_ap, nmr4_ap=nmr4_ap: e.activation(
                taff2[:, :], w_sb2[:, :], AF.Identity, bias=nmr4_ap, scale=rstd4_ap), "ss")
            t_ln4_prev = t_af
            ws("vector", [t_af, t_xl])
            t_y = o("vector", lambda e: e.tensor_tensor(
                y_sb[:, :], xr_t2[:, :], taff2[:, :], ALU.add), "sv")
            t_y_prev = t_y
            ln_mark_readers(g4, [t_af, t_y])
            w("gpsimd", t_y)
            t_ys = o("gpsimd", lambda e, tcg=tcg: e.dma_start(
                out=y_out[tcg * 128:(tcg + 1) * 128, :], in_=y_sb[:, :]), "yst", 16)
            y_store.append(t_ys)

    if dbg:
        dbg_specs = [
            ("dbg_w", dbgw_d, [NTT, 128, D_], F32),
            ("dbg_o", dbgo_d, [NMACRO, 128, 8, QMAC], BF16),
            ("dbg_p00", dbgp_d, [128, 1024], BF16),
            ("dbg_sums", dbgs_d, [NMACRO, NPAIR, 2, 512], F32),
            ("dbg_ln1x", ln1x_d, [NKEY, D_], BF16),
            ("dbg_x1", x1_d, [NTOK, D_], F32),
            ("dbg_xsim", xsim_d, [NTOK, D_], BF16),
            ("dbg_q", q_d, [NPAIR, 128, NTOK], BF16),
            ("dbg_hid0", hid_d[0], [128, NTOK], BF16),
            ("dbg_hid7", hid_d[7], [128, NTOK], BF16),
            ("dbg_wint0", wint_d[0], [8, 128, 128], BF16),
            ("dbg_wout0", wout_d[0], [128, D_], BF16),
        ]
        ws("gpsimd", [y_store[-1], t_x1_last, t_hid_done, t_xq_tok])
        for nm, srcap, shp, dt in dbg_specs:
            tdg = o("gpsimd", lambda e, nm=nm, srcap=srcap, shp=shp, dt=dt: e.dma_start(
                out=nc.dram_tensor(nm, shp, dt, kind="ExternalOutput").ap(), in_=srcap), "dbgs", 16)
        for nm, srcap, shp in [("dbg_biasc", bias_c[:, :], [128, 32]),
                               ("dbg_qn2", qn2[:, :], [2, 32]),
                               ("dbg_kn2", kn2[:, :], [2, NPAIR]),
                               ("dbg_sjr", sjr_p[:, :], [128, 32]),
                               ("dbg_sxb", sxb[:, :], [128, 1]),
                               ("dbg_shb", shb[:, :], [128, 1]),
                               ("dbg_rstd3", rstd3a[:, :], [128, NTT]),
                               ("dbg_nmr3", nmr3a[:, :], [128, NTT])]:
            tdg = o("gpsimd", lambda e, nm=nm, srcap=srcap, shp=shp: e.dma_start(
                out=nc.dram_tensor(nm, shp, F32 if "wint" not in nm else BF16,
                                   kind="ExternalOutput").ap(), in_=srcap), "dbgs", 16)
        w("gpsimd", tdg)

    ws("gpsimd", [y_store[-1], t_x1_last, t_hid_done])

    with nc.Block() as block:
        block.sync(kb.runner("sync"))
        block.vector(kb.runner("vector"))
        block.scalar(kb.runner("scalar"))
        block.tensor(kb.runner("tensor"))
        block.gpsimd(kb.runner("gpsimd"))
    stD.close()
    kb.stack.close()
    return nc


def _headperm():
    pi = np.empty(D_, np.int64)
    for h in range(NH):
        for j in range(32):
            pi[64 * h + j] = 64 * h + 2 * j
            pi[64 * h + 32 + j] = 64 * h + 2 * j + 1
    return pi


def _rope_tables(positions):
    pos = np.asarray(positions, np.float32)
    ids = np.arange(512, dtype=np.float32)
    theta = np.float32(10000.0) ** (np.float32(-2.0) * ids) / np.float32(512.0)
    f = np.arange(D_)
    c = f // 128
    r = f % 128
    head = 2 * c + r // 64
    j = r % 64
    pg = 32 * head + (j % 32)
    ang = pos[None, :] * theta[pg][:, None]
    cos = np.cos(ang).astype(np.float32)
    sin = np.sin(ang).astype(np.float32)
    sin[j < 32, :] = -sin[j < 32, :]
    return (cos.astype(ml_dtypes.bfloat16), sin.astype(ml_dtypes.bfloat16))


def _pswap():
    m = np.zeros((128, 128), np.float32)
    for r in range(128):
        jj = r % 64
        partner = r + 32 if jj < 32 else r - 32
        m[partner, r] = 1.0
    return m.astype(ml_dtypes.bfloat16)


def _bias_pm(b, chunks):
    return np.ascontiguousarray(np.asarray(b, np.float32).reshape(chunks, 128).T)


_NC_CACHE = [None]


def kernel(L0_attn_x, L0_attn_Wd_q_w, L0_attn_Wd_q_b, L0_attn_Wu_q_w, L0_attn_Wu_q_b,
           L0_attn_Wq_r_w, L0_attn_Wq_r_b, L0_attn_Wk_r_w, L0_attn_Wk_r_b,
           L0_attn_Wd_kv_w, L0_attn_Wd_kv_b, L0_attn_Wu_k_w, L0_attn_Wu_k_b,
           L0_attn_Wu_v_w, L0_attn_Wu_v_b, L0_attn_Wo_w, L0_attn_Wo_b,
           L0_ffn_W_in, L0_ffn_b_in, L0_ffn_W_out, L0_ffn_b_out):
    f32 = lambda a: np.ascontiguousarray(np.asarray(a), dtype=np.float32)
    x = f32(L0_attn_x)
    pi = _headperm()
    shared = {
        "wd_q": f32(L0_attn_Wd_q_w), "wd_kv": f32(L0_attn_Wd_kv_w),
        "wu_q": np.ascontiguousarray(f32(L0_attn_Wu_q_w)[:, pi]),
        "wq_r": np.ascontiguousarray(f32(L0_attn_Wq_r_w)[:, pi]),
        "wk_r": np.ascontiguousarray(f32(L0_attn_Wk_r_w)[:, pi]),
        "wu_k": np.ascontiguousarray(f32(L0_attn_Wu_k_w)[:, pi]),
        "wu_v": f32(L0_attn_Wu_v_w), "wo": f32(L0_attn_Wo_w),
        "w_in": f32(L0_ffn_W_in), "w_out": f32(L0_ffn_W_out),
        "bd_q": _bias_pm(L0_attn_Wd_q_b, 4), "bd_kv": _bias_pm(L0_attn_Wd_kv_b, 4),
        "bu_q": _bias_pm(np.asarray(L0_attn_Wu_q_b, np.float32)[pi], 8),
        "bq_r": _bias_pm(np.asarray(L0_attn_Wq_r_b, np.float32)[pi], 8),
        "bk_r": _bias_pm(np.asarray(L0_attn_Wk_r_b, np.float32)[pi], 8),
        "bu_k": _bias_pm(np.asarray(L0_attn_Wu_k_b, np.float32)[pi], 8),
        "b_in": _bias_pm(L0_ffn_b_in, 32),
        "bu_v_row": f32(L0_attn_Wu_v_b).reshape(1, D_),
        "bo_row": f32(L0_attn_Wo_b).reshape(1, D_),
        "bout_row": f32(L0_ffn_b_out).reshape(1, D_),
        "pswap": _pswap(),
        "idn": np.eye(128, dtype=np.float32).astype(ml_dtypes.bfloat16),
        "idn32": np.eye(128, dtype=np.float32),
    }
    in_maps = []
    for core in range(8):
        bc, hc = core // 2, core % 2
        order = np.concatenate([np.arange(hc * NTOK, (hc + 1) * NTOK),
                                np.arange((1 - hc) * NTOK, (2 - hc) * NTOK)])
        cosb, sinb = _rope_tables(order)
        m = dict(shared)
        m["x"] = np.ascontiguousarray(x[bc][order])
        m["cos_d"] = np.ascontiguousarray(cosb)
        m["sin_d"] = np.ascontiguousarray(sinb)
        in_maps.append(m)

    if _NC_CACHE[0] is None:
        _NC_CACHE[0] = build_core_program()
    nc = _NC_CACHE[0]
    res = run_bass_kernel_spmd(nc, in_maps, core_ids=list(range(8)))
    out = np.empty((B_, S_, D_), np.float32)
    for core in range(8):
        bc, hc = core // 2, core % 2
        out[bc, hc * NTOK:(hc + 1) * NTOK, :] = res.results[core]["y"]
    return out
